# version 02
# baseline (speedup 1.0000x reference)
import numpy as np
import ml_dtypes
from contextlib import ExitStack

import concourse.bass as bass
import concourse.mybir as mybir
from concourse.bass_utils import run_bass_kernel_spmd

F32 = mybir.dt.float32
BF16 = mybir.dt.bfloat16
AF = mybir.ActivationFunctionType
ALU = mybir.AluOpType
AX = mybir.AxisListType
BF = ml_dtypes.bfloat16

NCORES = 8
T = 1024
TH = T + 3
CS = 128
NCH = T // CS
H, P, G, N = 128, 64, 8, 128
DIM = 4096
INTER = 8192
CONV = 10240
FT_GATE, FT_HBC = 64, 80
FT = FT_GATE + FT_HBC + 1
KT = DIM // 128
EPS = 1e-5
NEG = -1e9
CLAMP = 1e30


class Sem:
    def __init__(self, nc, es, name):
        self.h = es.enter_context(nc.semaphore(name))
        self.n = 0

    def inc(self, inst, k=1):
        inst.then_inc(self.h, k)
        self.n += k
        return self.n

    def wait(self, eng, n=None):
        eng.wait_ge(self.h, self.n if n is None else n)


class Stream:
    def __init__(self, mk, name, nbuf):
        self.s = [mk(f"{name}{i}") for i in range(nbuf)]
        self.nb = nbuf

    def load(self, inst, i, k=16):
        b = i % self.nb
        return self.s[b].inc(inst, k)

    def wait(self, eng, i):
        if i < 0:
            return
        b = i % self.nb
        eng.wait_ge(self.s[b].h, 16 * (i // self.nb + 1))


def build(debug=False, upto=99, b2n=NCH):
    nc = bass.Bass("TRN2", target_bir_lowering=False)
    ap = lambda t: t.ap()

    xh = ap(nc.dram_tensor("xh", [TH, DIM], F32, kind="ExternalInput"))
    w1 = ap(nc.dram_tensor("w1", [FT, 128, KT * 128], BF16, kind="ExternalInput"))
    convd = ap(nc.dram_tensor("convd", [FT_HBC, 128, 4 * 128], BF16, kind="ExternalInput"))
    convb = ap(nc.dram_tensor("convb", [128, FT_HBC], F32, kind="ExternalInput"))
    normw = ap(nc.dram_tensor("normw", [128, FT_GATE], F32, kind="ExternalInput"))
    dtbias = ap(nc.dram_tensor("dtbias", [128, 1], F32, kind="ExternalInput"))
    alog = ap(nc.dram_tensor("alog", [128, 1], F32, kind="ExternalInput"))
    dexp = ap(nc.dram_tensor("dexp", [INTER], BF16, kind="ExternalInput"))
    opw = ap(nc.dram_tensor("opw", [8, 4, 128, 16 * 512], BF16, kind="ExternalInput"))
    bw = ap(nc.dram_tensor("bw", [8, 2, 128, 16 * 512], BF16, kind="ExternalInput"))
    arw = ap(nc.dram_tensor("arw", [128, KT * 80], BF16, kind="ExternalInput"))
    betw = ap(nc.dram_tensor("betw", [8, 64, 16 * 512], BF16, kind="ExternalInput"))
    bbias = ap(nc.dram_tensor("bbias", [1, DIM], BF16, kind="ExternalInput"))
    identb = ap(nc.dram_tensor("identb", [128, 128], BF16, kind="ExternalInput"))
    identf = ap(nc.dram_tensor("identf", [128, 128], F32, kind="ExternalInput"))
    triu01 = ap(nc.dram_tensor("triu01", [128, 128], F32, kind="ExternalInput"))
    onesb = ap(nc.dram_tensor("onesb", [1, 128], BF16, kind="ExternalInput"))
    rmask = ap(nc.dram_tensor("rmask", [1, 3 * 128], F32, kind="ExternalInput"))
    rmaskc = ap(nc.dram_tensor("rmaskc", [1, 3 * 128], F32, kind="ExternalInput"))
    epsb = ap(nc.dram_tensor("epsb", [128, 1], F32, kind="ExternalInput"))

    out = ap(nc.dram_tensor("out", [T, DIM], F32, kind="ExternalOutput"))

    d_xln = nc.dram_tensor("d_xln", [1152, DIM], BF16)
    d_hbc = nc.dram_tensor("d_hbc", [CONV, T], BF16)
    d_sg = nc.dram_tensor("d_sg", [INTER, T], BF16)
    d_cum = nc.dram_tensor("d_cum", [NCH, 128 * 128], F32)
    d_lastr = nc.dram_tensor("d_lastr", [NCH, 128], F32)
    d_dpref = nc.dram_tensor("d_dpref", [NCH + 1, 128], F32)
    d_spref = nc.dram_tensor("d_spref", [NCH, 128, INTER], BF16)
    d_yrms = nc.dram_tensor("d_yrms", [T, INTER], BF16)
    d_attn = nc.dram_tensor("d_attn", [T, DIM], BF16)
    d_xnew = nc.dram_tensor("d_xnew", [T, DIM], F32)
    d_xln2 = nc.dram_tensor("d_xln2", [T, DIM], BF16)
    d_ffn = nc.dram_tensor("d_ffn", [T, DIM], BF16)
    ag_s_in = nc.dram_tensor("ag_s_in", [128, INTER], BF16)
    ag_s_out = nc.dram_tensor("ag_s_out", [4 * 128, INTER], BF16)
    ag_g_in = nc.dram_tensor("ag_g_in", [1, 128], F32)
    ag_g_out = nc.dram_tensor("ag_g_out", [4, 128], F32)
    d_m = nc.dram_tensor("d_m", [3, 128], F32)

    dbg = {}
    if debug:
        for nm, shp, dt in [
            ("dbg_xln", [1027, DIM], BF16), ("dbg_hbc", [CONV, T], BF16),
            ("dbg_g0", [128, T], F32), ("dbg_s0", [128, T], F32),
            ("dbg_sg", [INTER, T], BF16), ("dbg_dt", [128, T], F32),
            ("dbg_cum", [NCH, 128 * 128], F32), ("dbg_spref", [NCH, 128, INTER], BF16),
            ("dbg_spub", [128, INTER], BF16), ("dbg_sin", [128, INTER], BF16),
            ("dbg_yrms", [T, INTER], BF16), ("dbg_attn", [T, DIM], BF16),
            ("dbg_bt", [128, 1024], BF16), ("dbg_xdd", [128, INTER], BF16),
            ("dbg_ddt", [128, T], mybir.dt.float32), ("dbg_xtb", [128, INTER], BF16),
            ("dbg_xnew", [T, DIM], F32), ("dbg_xln2", [T, DIM], BF16),
        ]:
            dbg[nm] = ap(nc.dram_tensor(nm, shp, dt, kind="ExternalOutput"))

    es = ExitStack()
    Sm = lambda name: Sem(nc, es, name)
    sb = lambda *a, **k: es.enter_context(nc.sbuf_tensor(*a, **k))

    PH = Sm("PH")
    DBGS = Sm("DBGS")
    nphase = [0]

    def barrier(dma_sems=()):
        nphase[0] += 1
        tgt = 5 * nphase[0]
        PH.n = tgt
        with nc.Block() as block:
            @block.sync
            def _(sync):
                for s in dma_sems:
                    s.wait(sync)
                sync.sem_inc(PH.h, 1)
                sync.wait_ge(PH.h, tgt)

            @block.tensor
            def _(tensor):
                tensor.sem_inc(PH.h, 1)
                tensor.wait_ge(PH.h, tgt)

            @block.vector
            def _(vector):
                vector.drain()
                vector.sem_inc(PH.h, 1)
                vector.wait_ge(PH.h, tgt)

            @block.scalar
            def _(scalar):
                scalar.drain()
                scalar.sem_inc(PH.h, 1)
                scalar.wait_ge(PH.h, tgt)

            @block.gpsimd
            def _(gpsimd):
                gpsimd.drain()
                gpsimd.sem_inc(PH.h, 1)
                gpsimd.wait_ge(PH.h, tgt)

    def dbg_dump(pairs):
        if not debug or not pairs:
            return
        with nc.Block() as block:
            @block.gpsimd
            def _(gpsimd):
                for dst, src in pairs:
                    DBGS.inc(gpsimd.dma_start(dst, src), 16)
                gpsimd.wait_ge(DBGS.h, DBGS.n)

    c_identb = sb("c_identb", [128, 128], BF16)
    c_identf = sb("c_identf", [128, 128], F32)
    c_triu = sb("c_triu", [128, 128], F32)
    c_onesb = sb("c_onesb", [1, 128], BF16)
    c_dtb = sb("c_dtb", [128, 1], F32)
    c_alog = sb("c_alog", [128, 1], F32)
    c_eps = sb("c_eps", [128, 1], F32)
    stat = sb("stat", [128, 16], F32)
    bsM = ExitStack()
    mb_ = lambda *a, **k: bsM.enter_context(nc.sbuf_tensor(*a, **k))
    biasT_all = mb_("biasT_all", [128, NCH * 128], F32)
    expcA_all = mb_("expcA_all", [128, NCH * 128], F32)
    DT = mb_("DT", [128, T], F32)
    A_dt = mb_("A_dt", [128, T], F32)

    CL = Sm("CL"); SQ = Sm("SQ"); VST = Sm("VST"); RST = Sm("RST")
    XW = Sm("XW"); SSELF = Sm("SSELF"); VSELF = Sm("VSELF")
    LD = Stream(Sm, "LD", 2); XS = Stream(Sm, "XS", 2)
    NT0 = 9
    with ExitStack() as ps:
        psb = lambda *a, **k: ps.enter_context(nc.sbuf_tensor(*a, **k))
        xbuf = [psb(f"a0x{i}", [128, DIM], F32) for i in range(2)]
        xlnb = [psb(f"a0l{i}", [128, DIM], BF16) for i in range(2)]
        dump = psb("a0d", [128, DIM], F32)
        st = [psb(f"a0s{i}", [128, 8], F32) for i in range(2)]

        with nc.Block() as block:
            @block.sync
            def _(sync):
                for dst, src in [(c_identb[:], identb), (c_identf[:], identf),
                                 (c_triu[:], triu01), (c_onesb[:], onesb),
                                 (c_dtb[:], dtbias), (c_alog[:], alog),
                                 (c_eps[:], epsb)]:
                    CL.inc(sync.dma_start(dst, src), 16)
                for i in range(NT0):
                    r0 = i * 128
                    nr = min(128, TH - r0)
                    if i >= 2:
                        sync.wait_ge(XW.h, i - 1)
                        sync.wait_ge(SQ.h, i - 1)
                    LD.load(sync.dma_start(xbuf[i % 2][0:nr, :], xh[r0:r0 + nr, :]), i)

            @block.scalar
            def _(scalar):
                scalar.wait_ge(CL.h, CL.n)
                for i in range(NT0):
                    nr = min(128, TH - i * 128)
                    b = i % 2
                    LD.wait(scalar, i)
                    SQ.inc(scalar.activation(dump[0:nr, :], xbuf[b][0:nr, :], AF.Square,
                                             accum_out=st[b][0:nr, 4:5]))
                    scalar.wait_ge(VST.h, i + 1)
                    sl = SSELF.inc(scalar.activation(st[b][0:nr, 5:6], st[b][0:nr, 3:4],
                                                     AF.Ln, bias=c_eps[0:nr, :]))
                    scalar.wait_ge(SSELF.h, sl)
                    RST.inc(scalar.activation(st[b][0:nr, 6:7], st[b][0:nr, 5:6],
                                              AF.Exp, scale=-0.5))

            @block.vector
            def _(vector):
                for i in range(NT0):
                    nr = min(128, TH - i * 128)
                    b = i % 2
                    LD.wait(vector, i)
                    if i >= 2:
                        vector.wait_ge(RST.h, i - 1)
                    v0 = VSELF.inc(vector.tensor_reduce(st[b][0:nr, 0:1], xbuf[b][0:nr, :],
                                                        AX.X, ALU.add))
                    vector.wait_ge(VSELF.h, v0)
                    v1 = VSELF.inc(vector.tensor_scalar(st[b][0:nr, 1:2], st[b][0:nr, 0:1],
                                                        1.0 / DIM, None, ALU.mult))
                    vector.wait_ge(VSELF.h, v1)
                    v2 = VSELF.inc(vector.tensor_mul(st[b][0:nr, 2:3], st[b][0:nr, 1:2],
                                                     st[b][0:nr, 1:2]))
                    vector.wait_ge(SQ.h, i + 1)
                    vector.wait_ge(VSELF.h, v2)
                    VST.inc(vector.tensor_scalar(st[b][0:nr, 3:4], st[b][0:nr, 4:5],
                                                 1.0 / DIM, st[b][0:nr, 2:3],
                                                 ALU.mult, ALU.subtract))
                    vector.wait_ge(RST.h, i + 1)
                    XS.wait(vector, i - 2)
                    XW.inc(vector.tensor_scalar(xlnb[b][0:nr, :], xbuf[b][0:nr, :],
                                                st[b][0:nr, 1:2], st[b][0:nr, 6:7],
                                                ALU.subtract, ALU.mult))

            @block.gpsimd
            def _(gpsimd):
                for i in range(NT0):
                    nr = min(128, TH - i * 128)
                    gpsimd.wait_ge(XW.h, i + 1)
                    XS.load(gpsimd.dma_start(d_xln.ap()[i * 128:i * 128 + nr, :],
                                             xlnb[i % 2][0:nr, :]), i)
                for b in range(2):
                    gpsimd.wait_ge(XS.s[b].h, XS.s[b].n)
                m = XW.inc(gpsimd.memset(dump[0:125, :], 0.0))
                gpsimd.wait_ge(XW.h, m)
                XS.s[0].inc(gpsimd.dma_start(d_xln.ap()[1027:1152, :], dump[0:125, :]), 16)
                gpsimd.wait_ge(XS.s[0].h, XS.s[0].n)

        barrier(dma_sems=[CL])
        dbg_dump([(dbg["dbg_xln"], d_xln.ap()[0:1027, :])] if debug else [])
    if upto <= 0:
        bsM.close(); es.close()
        return nc, dbg

    XF = Sm("XF"); PJD = Sm("PJD")
    PJE = [Sm("PJE0"), Sm("PJE1")]
    RW = Sm("RW"); CVD = Sm("CVD"); CE = Sm("CE")
    SGA = Sm("SGA"); SGB = Sm("SGB")
    WLs = [Sm("WL0"), Sm("WL1")]
    SGS = Stream(Sm, "SGS", 2); HS = Stream(Sm, "HS", 2)
    A1V = Sm("A1V"); A1S = Sm("A1S")
    with ExitStack() as ps:
        psb = lambda *a, **k: ps.enter_context(nc.sbuf_tensor(*a, **k))
        xfm = psb("xfm", [128, KT * 1152], BF16)
        wbuf = [psb(f"wb{i}", [128, KT * 128], BF16) for i in range(2)]
        cvbuf = [psb(f"cv{i}", [128, 4 * 128], BF16) for i in range(2)]
        rawb = [psb(f"raw{i}", [128, TH], BF16) for i in range(2)]
        hsil = [psb(f"hs{i}", [128, T], BF16) for i in range(2)]
        sgt = [psb(f"sgt{i}", [128, T], F32) for i in range(2)]
        sgo = [psb(f"sgo{i}", [128, T], BF16) for i in range(2)]
        c_convb = psb("c_convb", [128, FT_HBC], F32)
        c_normw = psb("c_normw", [128, FT_GATE], F32)
        zb = psb("zb", [128, T], F32)
        az = psb("az", [128, T], F32)
        statA = psb("statA", [128, 2], F32)
        dbgg = psb("dbgg", [128, 2 * T], F32) if debug else None
        pjp = [ps.enter_context(nc.psum_tensor(f"pj{i}", [128, 1536], F32)) for i in range(2)]
        cvp = ps.enter_context(nc.psum_tensor("cvp", [128, 1024], F32))

        wl_cum = []

        with nc.Block() as block:
            @block.sync
            def _(sync):
                CL.inc(sync.dma_start(c_convb[:], convb), 16)
                CL.inc(sync.dma_start(c_normw[:], normw), 16)
                for k in range(KT):
                    XF.inc(sync.dma_start(xfm[:, k * 1152:(k + 1) * 1152],
                                          d_xln.ap()[0:1152, k * 128:(k + 1) * 128],
                                          transpose=True), 16)
                for ft in range(FT):
                    if ft >= 2:
                        sync.wait_ge(PJD.h, ft - 1)
                        prev_hbc = (ft - 2) - FT_GATE
                        if 0 <= prev_hbc:
                            sync.wait_ge(CVD.h, min(prev_hbc, FT_HBC - 1) + 1)
                    WLs[ft % 2].inc(sync.dma_start(wbuf[ft % 2][:], w1[ft]), 16)
                    if FT_GATE <= ft < FT_GATE + FT_HBC:
                        WLs[ft % 2].inc(sync.dma_start(cvbuf[ft % 2][:], convd[ft - FT_GATE]), 16)

            accp = [0, 0]
            for ft in range(FT):
                accp[ft % 2] += 16
                if FT_GATE <= ft < FT_GATE + FT_HBC:
                    accp[ft % 2] += 16
                wl_cum.append(accp[ft % 2])

            @block.tensor
            def _(tensor):
                tensor.wait_ge(XF.h, XF.n)
                tensor.wait_ge(CL.h, CL.n)
                pje_use = [0, 0]
                for ft in range(FT):
                    b = ft % 2
                    tensor.wait_ge(WLs[b].h, wl_cum[ft])
                    if pje_use[b] > 0:
                        tensor.wait_ge(PJE[b].h, pje_use[b])
                    pje_use[b] += 1
                    is_hbc = FT_GATE <= ft < FT_GATE + FT_HBC
                    cols = ([(0, 512), (512, 1024), (1024, 1027)] if is_hbc
                            else [(3, 515), (515, 1027)])
                    for k in range(KT):
                        lhs = wbuf[b][:, k * 128:(k + 1) * 128]
                        for ci, (c0, c1) in enumerate(cols):
                            o0 = c0 if is_hbc else c0 - 3
                            inst = tensor.matmul(pjp[b][:, o0:o0 + (c1 - c0)], lhs,
                                                 xfm[:, k * 1152 + c0:k * 1152 + c1],
                                                 start=(k == 0), stop=(k == KT - 1))
                    PJD.inc(inst)
                    if is_hbc:
                        hi = ft - FT_GATE
                        tensor.wait_ge(RW.h, hi + 1)
                        if hi > 0:
                            tensor.wait_ge(CE.h, hi)
                        rb = rawb[hi % 2]
                        for k in range(4):
                            dg = cvbuf[b][:, k * 128:(k + 1) * 128]
                            tensor.matmul(cvp[:, 0:512], dg, rb[:, k:k + 512],
                                          start=(k == 0), stop=(k == 3))
                        for k in range(4):
                            dg = cvbuf[b][:, k * 128:(k + 1) * 128]
                            inst = tensor.matmul(cvp[:, 512:1024], dg, rb[:, 512 + k:1024 + k],
                                                 start=(k == 0), stop=(k == 3))
                        CVD.inc(inst)

            @block.vector
            def _(vector):
                gi = 0
                for ft in range(FT):
                    b = ft % 2
                    vector.wait_ge(PJD.h, ft + 1)
                    if ft < FT_GATE:
                        vector.wait_ge(SGA.h, ft + 1)
                        SGS.wait(vector, ft - 2)
                        SGB.inc(vector.tensor_scalar(sgo[b][:], sgt[b][:],
                                                     c_normw[:, ft:ft + 1], None, ALU.mult))
                    elif ft < FT_GATE + FT_HBC:
                        hi = ft - FT_GATE
                        if hi >= 2:
                            vector.wait_ge(CVD.h, hi - 1)
                        n = RW.inc(vector.tensor_copy(rawb[hi % 2][:], pjp[b][:, 0:TH]))
                        vector.wait_ge(RW.h, n)
                        PJE[b].inc(vector.engine_nop())
                    else:
                        v = A1V.inc(vector.tensor_scalar(zb[:], pjp[b][:, 0:T],
                                                         c_dtb[:], None, ALU.add))
                        vector.wait_ge(A1V.h, v)
                        PJE[b].inc(vector.engine_nop())
                        v = A1V.inc(vector.tensor_scalar_max(DT[:], zb[:], 0.0))
                        vector.wait_ge(A1V.h, v)
                        vector.wait_ge(A1S.h, 3)
                        v = A1V.inc(vector.tensor_add(DT[:], DT[:], az[:]))
                        vector.wait_ge(A1V.h, v)
                        vector.wait_ge(A1S.h, 4)
                        A1V.inc(vector.tensor_scalar(A_dt[:], DT[:], statA[:, 0:1], -1.0,
                                                     ALU.mult, ALU.mult))
                        if debug:
                            A1V.inc(vector.tensor_copy(dbgg[:, 0:T], zb[:]))
                            A1V.inc(vector.tensor_copy(dbgg[:, T:2 * T], az[:]))

            @block.scalar
            def _(scalar):
                for ft in range(FT_GATE):
                    b = ft % 2
                    scalar.wait_ge(PJD.h, ft + 1)
                    if ft >= 2:
                        scalar.wait_ge(SGB.h, ft - 1)
                    n = SGA.inc(scalar.activation(sgt[b][:], pjp[b][:, 0:T], AF.Silu))
                    scalar.wait_ge(SGA.h, n)
                    PJE[b].inc(scalar.nop())
                for hi in range(FT_HBC):
                    scalar.wait_ge(CVD.h, hi + 1)
                    HS.wait(scalar, hi - 2)
                    CE.inc(scalar.activation(hsil[hi % 2][:], cvp[:], AF.Silu,
                                             bias=c_convb[:, hi:hi + 1]))
                scalar.wait_ge(A1V.h, 1)
                s1 = A1S.inc(scalar.activation(az[:], zb[:], AF.Abs))
                scalar.wait_ge(A1S.h, s1)
                s1 = A1S.inc(scalar.activation(az[:], az[:], AF.Exp, scale=-1.0))
                scalar.wait_ge(A1S.h, s1)
                A1S.inc(scalar.activation(az[:], az[:], AF.Ln, bias=1.0))
                A1S.inc(scalar.activation(statA[:, 0:1], c_alog[:], AF.Exp))

            @block.gpsimd
            def _(gpsimd):
                for ft in range(FT_GATE):
                    gpsimd.wait_ge(SGB.h, ft + 1)
                    SGS.load(gpsimd.dma_start(d_sg.ap()[ft * 128:(ft + 1) * 128, :],
                                              sgo[ft % 2][:]), ft)
                for hi in range(FT_HBC):
                    gpsimd.wait_ge(CE.h, hi + 1)
                    HS.load(gpsimd.dma_start(d_hbc.ap()[hi * 128:(hi + 1) * 128, :],
                                             hsil[hi % 2][:]), hi)
                for b in range(2):
                    gpsimd.wait_ge(SGS.s[b].h, SGS.s[b].n)
                    gpsimd.wait_ge(HS.s[b].h, HS.s[b].n)

        barrier(dma_sems=[XF, CL])
        dbg_dump([(dbg["dbg_hbc"], d_hbc.ap()[:, :]), (dbg["dbg_sg"], d_sg.ap()[:, :]),
                  (dbg["dbg_dt"], DT[:]),
                  (dbg["dbg_g0"], dbgg[:, 0:T]), (dbg["dbg_s0"], dbgg[:, T:2 * T])]
                 if debug else [])

    if upto <= 1:
        bsM.close(); es.close()
        return nc, dbg

    bsA = ExitStack()
    ab = lambda *a, **k: bsA.enter_context(nc.sbuf_tensor(*a, **k))
    Sb = ab("Sb", [128, INTER], BF16)
    ysb = ab("ysb", [128, INTER], F32)
    xTb = ab("xTb", [128, INTER], BF16)
    bcm = ab("bcm", [128, G * 128], BF16)

    B0V = Sm("B0V"); B0S = Sm("B0S"); B0T = Sm("B0T"); B0D = Sm("B0D"); B0E = Sm("B0E")
    LB = Stream(Sm, "LB", 2)
    bs2 = ExitStack()
    pb2 = lambda *a, **k: bs2.enter_context(nc.sbuf_tensor(*a, **k))
    S = pb2("S", [128, INTER], F32)
    xdd = pb2("xdd", [128, INTER], BF16)
    bT = pb2("bT", [128, G * 128], BF16)
    cumA_all = pb2("cumA_all", [128, T], F32)
    ddtT_all = pb2("ddtT_all", [128, T], F32)
    expdA_all = pb2("expdA_all", [128, T], F32)
    dpref = pb2("dpref", [1, (NCH + 1) * 128], F32)
    catb = [pb2(f"catb{i}", [128, 128], F32) for i in range(2)]
    dttb = [pb2(f"dttb{i}", [128, 128], F32) for i in range(2)]
    lastb = [pb2(f"lastb{i}", [128, 128], F32) for i in range(2)]
    lnd = [pb2(f"lnd{i}", [128, 128], F32) for i in range(2)]
    tmb = [pb2(f"tmb{i}", [128, 128], F32) for i in range(2)]
    e2b = [pb2(f"e2b{i}", [128, 128], F32) for i in range(2)]
    pp2 = bs2.enter_context(nc.psum_tensor("pp2", [128, 256], F32))

    scan_i = {}; evA_i = {}; evB_i = {}; biasT_i = {}; tmb_i = {}; ddt_i = {}
    lnd_i = {}; eca_i = {}; eda_i = {}; e2_i = {}; tr_i = {}; lb_i = {}
    dpf_i = {}
    with nc.Block() as block:
        @block.vector
        def _(vector):
            for c in range(NCH):
                scan_i[c] = B0V.inc(vector.tensor_tensor_scan(
                    cumA_all[:, c * 128:(c + 1) * 128], A_dt[:, c * 128:(c + 1) * 128],
                    A_dt[:, c * 128:(c + 1) * 128], 0.0, ALU.add, ALU.bypass))
            for c in range(NCH):
                cb = c % 2
                vector.wait_ge(B0T.h, c + 1)
                if c >= 2:
                    vector.wait_ge(B0S.h, 4 * (c - 2) + 2)
                v = B0E.inc(vector.tensor_copy(catb[cb][:], pp2[:, 0:128]))
                evA_i[c] = v
                vector.wait_ge(B0E.h, v)
                evB_i[c] = B0E.inc(vector.tensor_copy(dttb[cb][:], pp2[:, 128:256]))
                vector.wait_ge(B0E.h, evB_i[c])
                vector.wait_ge(B0S.h, 4 * c + 1)
                biasT_i[c] = B0V.inc(vector.tensor_sub(
                    biasT_all[:, c * 128:(c + 1) * 128], lnd[cb][:], catb[cb][:]))
                LB.wait(vector, c)
                if c >= 2:
                    vector.wait_ge(B0S.h, 4 * (c - 2) + 4)
                tmb_i[c] = B0V.inc(vector.tensor_sub(tmb[cb][:], lastb[cb][:], catb[cb][:]))
                vector.wait_ge(B0V.h, tmb_i[c])
                vector.wait_ge(B0S.h, 4 * c + 4)
                ddt_i[c] = B0V.inc(vector.tensor_mul(
                    ddtT_all[:, c * 128:(c + 1) * 128], e2b[cb][:], dttb[cb][:]))
            v = B0V.inc(vector.memset(dpref[:, 0:128], 1.0))
            for c in range(NCH):
                vector.wait_ge(B0V.h, v)
                vector.wait_ge(B0S.h, 4 * c + 3)
                v = B0V.inc(vector.tensor_mul(dpref[:, (c + 1) * 128:(c + 2) * 128],
                                              dpref[:, c * 128:(c + 1) * 128],
                                              expdA_all[0:1, c * 128:(c + 1) * 128]))
                dpf_i[c] = v

        @block.sync
        def _(sync):
            sync.wait_ge(B0V.h, scan_i[NCH - 1])
            for c in range(NCH):
                B0D.inc(sync.dma_start(d_cum.ap()[c], cumA_all[:, c * 128:(c + 1) * 128]), 16)
            for c in range(NCH):
                B0D.inc(sync.dma_start(d_lastr.ap()[c].unsqueeze(0),
                                       cumA_all[:, c * 128 + 127:c * 128 + 128],
                                       allow_slow_non_contiguous=True), 16)
            for c in range(NCH):
                if c >= 2:
                    sync.wait_ge(B0S.h, 4 * (c - 1) + 3)
                sync.wait_ge(B0D.h, 16 * 2 * NCH)
                lb_i[c] = LB.load(sync.dma_start(
                    lastb[c % 2][:], d_lastr.ap()[c].unsqueeze(0).partition_broadcast(128)), c)
            sync.wait_ge(B0V.h, dpf_i[NCH - 1])
            B0D.inc(sync.dma_start(d_dpref.ap().rearrange("c h -> (c h)").unsqueeze(0),
                                   dpref[:]), 16)
            B0D.inc(sync.dma_start(ag_g_in.ap(), dpref[:, NCH * 128:(NCH + 1) * 128]), 16)

        @block.tensor
        def _(tensor):
            for c in range(NCH):
                tensor.wait_ge(B0V.h, c + 1)
                if c > 0:
                    tensor.wait_ge(B0E.h, evB_i[c - 1])
                tensor.transpose(pp2[:, 0:128], cumA_all[:, c * 128:(c + 1) * 128], c_identf[:])
                tr_i[c] = B0T.inc(tensor.transpose(pp2[:, 128:256],
                                                   DT[:, c * 128:(c + 1) * 128], c_identf[:]))

        @block.scalar
        def _(scalar):
            for c in range(NCH):
                cb = c % 2
                scalar.wait_ge(B0E.h, evB_i[c])
                if c >= 2:
                    scalar.wait_ge(B0V.h, ddt_i[c - 2])
                lnd_i[c] = B0S.inc(scalar.activation(lnd[cb][:], dttb[cb][:], AF.Ln))
                eca_i[c] = B0S.inc(scalar.activation(expcA_all[:, c * 128:(c + 1) * 128],
                                                     catb[cb][:], AF.Exp))
                LB.wait(scalar, c)
                eda_i[c] = B0S.inc(scalar.activation(expdA_all[:, c * 128:(c + 1) * 128],
                                                     lastb[cb][:], AF.Exp))
                scalar.wait_ge(B0V.h, tmb_i[c])
                e2_i[c] = B0S.inc(scalar.activation(e2b[cb][:], tmb[cb][:], AF.Exp))
                assert e2_i[c] == 4 * c + 4 and lnd_i[c] == 4 * c + 1

    barrier(dma_sems=[B0D, LB.s[0], LB.s[1]])
    dbg_dump([(dbg["dbg_cum"], d_cum.ap()[:, :])] if debug else [])
    if upto <= 2:
        bs2.close(); bsA.close(); bsM.close(); es.close()
        return nc, dbg

    B1T = Sm("B1T"); B1V = Sm("B1V"); B1D = Sm("B1D"); EVB = Sm("EVB")
    CCS = Sm("CCS"); SIV = Sm("SIV"); SID = Sm("SID")
    XTS = Sm("XTS"); BCS = Sm("BCS"); SGL = Stream(Sm, "SGL", 2)
    bs3p = ExitStack()
    p3 = lambda *a, **k: bs3p.enter_context(nc.sbuf_tensor(*a, **k))
    sgb = [p3(f"sgb{i}", [128, INTER], BF16) for i in range(2)]
    mbx = [p3(f"mbx{i}", [128, 128], F32) for i in range(3)]
    grow = p3("grow", [1, 512], F32)
    rmr = p3("rmr", [1, 384], F32)
    rmcr = p3("rmcr", [1, 384], F32)
    gmod = p3("gmod", [1, 384], F32)
    mrows = p3("mrows", [1, 384], F32)
    pst = bs3p.enter_context(nc.psum_tensor("pst", [128, 2048], F32))
    ptr = bs3p.enter_context(nc.psum_tensor("ptr", [128, 1024], BF16))
    DBV = Sm("DBV")

    bv_sb = lambda c: 10 * c + 2
    bv_xdd = lambda c: 10 * c + 3
    bv_tt2 = lambda c, qq: 10 * c + 5 + 2 * qq
    bv_pub = 10 * NCH + 2
    bt_trb = lambda c: 5 * c + 1
    bt_stq = lambda c, qq: 5 * c + 2 + qq
    bx_x = lambda c: 32 * c + 16
    bx_b = lambda c: 32 * c + 32
    sb_i = {}; xdd_i = {}; tt2_i = {}; trb_i = {}; stq_i = {}; evb_i = {}
    xload_i = {}; bload_i = {}; sin_j = {}
    sid_mark = {}
    with nc.Block() as block:
        @block.sync
        def _(sync):
            for c in range(NCH):
                if c > 0:
                    sync.wait_ge(B1V.h, bv_xdd(c - 1))
                    sync.wait_ge(B1T.h, bt_trb(c - 1))
                xload_i[c] = XTS.inc(sync.dma_start(
                    xTb[:], d_hbc.ap()[0:INTER, c * 128:(c + 1) * 128], transpose=True), 16)
                bload_i[c] = BCS.inc(sync.dma_start(
                    bcm[:].rearrange("p (g t) -> p g t", t=128),
                    d_hbc.ap()[INTER:INTER + 1024, c * 128:(c + 1) * 128]
                    .rearrange("(g n) t -> n g t", n=128)), 16)
            sync.wait_ge(CCS.h, 2)
            SID.inc(sync.dma_start(grow[:], ag_g_out.ap().rearrange("a b -> (a b)").unsqueeze(0)), 16)
            SID.inc(sync.dma_start(rmr[:], rmask), 16)
            sid_mark["gr"] = SID.inc(sync.dma_start(rmcr[:], rmaskc), 16)
            sync.wait_ge(SIV.h, 8)
            n = SID.inc(sync.dma_start(d_m.ap().rearrange("j h -> (j h)").unsqueeze(0),
                                       mrows[:]), 16)
            sync.wait_ge(SID.h, n)
            for j in range(3):
                sid_mark[f"mb{j}"] = SID.inc(sync.dma_start(
                    mbx[j][:], d_m.ap()[j].unsqueeze(0).partition_broadcast(128)), 16)
            for j in range(3):
                if j == 2:
                    sync.wait_ge(SIV.h, 9)
                SGL.load(sync.dma_start(
                    sgb[j % 2][:], ag_s_out.ap()[j * 128:(j + 1) * 128, :]), j)

        @block.vector
        def _(vector):
            v0 = B1V.inc(vector.memset(S[:], 0.0))
            for c in range(NCH):
                vector.wait_ge(B1V.h, tt2_i[c - 1][3] if c > 0 else v0)
                if c > 0:
                    vector.wait_ge(B1D.h, 16 * c)
                sb_i[c] = B1V.inc(vector.tensor_copy(Sb[:], S[:]))
                assert sb_i[c] == bv_sb(c), (c, sb_i[c])
                vector.wait_ge(XTS.h, 16 * (c + 1))
                xdv = ddtT_all[:, c * 128:(c + 1) * 128].unsqueeze(2).broadcast_to([128, 128, 64])
                if debug and False and c == 1:
                    vector.wait_ge(DBGS.h, DBGS.n + 64)
                xdd_i[c] = B1V.inc(vector.tensor_tensor(
                    xdd[:].rearrange("p (h q) -> p h q", q=64),
                    xTb[:].rearrange("p (h q) -> p h q", q=64), xdv, ALU.mult))
                tt2_i[c] = {}
                for qq in range(4):
                    vector.wait_ge(B1T.h, bt_stq(c, qq))
                    dav = expdA_all[:, c * 128 + 32 * qq:c * 128 + 32 * (qq + 1)] \
                        .unsqueeze(2).broadcast_to([128, 32, 64])
                    sl3 = S[:, 2048 * qq:2048 * (qq + 1)].rearrange("p (h q) -> p h q", q=64)
                    vector.wait_ge(B1V.h, sb_i[c])
                    w1_ = B1V.inc(vector.tensor_tensor(sl3, sl3, dav, ALU.mult))
                    vector.wait_ge(B1V.h, w1_)
                    tt2_i[c][qq] = B1V.inc(vector.tensor_add(
                        S[:, 2048 * qq:2048 * (qq + 1)], S[:, 2048 * qq:2048 * (qq + 1)], pst[:]))
                    assert tt2_i[c][qq] == bv_tt2(c, qq), (c, qq, tt2_i[c][qq])
            vector.wait_ge(B1V.h, tt2_i[NCH - 1][3])
            vector.wait_ge(B1D.h, 16 * NCH)
            pub_i = B1V.inc(vector.tensor_copy(Sb[:], S[:]))
            assert pub_i == bv_pub
            vector.wait_ge(SID.h, sid_mark["gr"])
            g2 = SIV.inc(vector.tensor_mul(gmod[:, 256:384], grow[:, 256:384], rmr[:, 256:384]))
            vector.wait_ge(SIV.h, g2)
            g2 = SIV.inc(vector.tensor_add(gmod[:, 256:384], gmod[:, 256:384], rmcr[:, 256:384]))
            g1 = SIV.inc(vector.tensor_mul(gmod[:, 128:256], grow[:, 128:256], rmr[:, 128:256]))
            vector.wait_ge(SIV.h, g1)
            g1 = SIV.inc(vector.tensor_add(gmod[:, 128:256], gmod[:, 128:256], rmcr[:, 128:256]))
            vector.wait_ge(SIV.h, g1)
            SIV.inc(vector.tensor_copy(mrows[:, 256:384], rmr[:, 256:384]))
            SIV.inc(vector.tensor_mul(mrows[:, 128:256], gmod[:, 256:384], rmr[:, 128:256]))
            m0_ = SIV.inc(vector.tensor_mul(mrows[:, 0:128], gmod[:, 256:384], gmod[:, 128:256]))
            vector.wait_ge(SIV.h, m0_)
            m0_ = SIV.inc(vector.tensor_mul(mrows[:, 0:128], mrows[:, 0:128], rmr[:, 0:128]))
            assert m0_ == 8
            vector.wait_ge(SID.h, sid_mark["mb2"])
            for j in range(3):
                SGL.wait(vector, j)
                mv = mbx[j][:].unsqueeze(2).broadcast_to([128, 128, 64])
                sgv = sgb[j % 2][:].rearrange("p (h q) -> p h q", q=64)
                if j == 0:
                    sin_j[j] = SIV.inc(vector.tensor_tensor(
                        S[:].rearrange("p (h q) -> p h q", q=64), sgv, mv, ALU.mult))
                    assert sin_j[0] == 9
                else:
                    t_ = SIV.inc(vector.tensor_tensor(
                        ysb[:].rearrange("p (h q) -> p h q", q=64), sgv, mv, ALU.mult))
                    vector.wait_ge(SIV.h, t_)
                    sin_j[j] = SIV.inc(vector.tensor_add(S[:], S[:], ysb[:]))
                vector.wait_ge(SIV.h, sin_j[j])
            vector.wait_ge(B1D.h, 16 * (NCH + 1))
            SIV.inc(vector.tensor_copy(Sb[:], S[:]))

        @block.tensor
        def _(tensor):
            for c in range(NCH):
                tensor.wait_ge(BCS.h, 16 * (c + 1))
                if c > 0:
                    tensor.wait_ge(EVB.h, c)
                for g in range(G):
                    inst = tensor.transpose(ptr[:, g * 128:(g + 1) * 128],
                                            bcm[:, g * 128:(g + 1) * 128], c_identb[:])
                trb_i[c] = B1T.inc(inst)
                assert trb_i[c] == bt_trb(c)
                tensor.wait_ge(EVB.h, c + 1)
                for qq in range(4):
                    tensor.wait_ge(B1V.h, bv_xdd(c))
                    prev = bv_tt2(c - 1, 3) if (qq == 0 and c > 0) else \
                        (bv_tt2(c, qq - 1) if qq > 0 else 0)
                    if prev:
                        tensor.wait_ge(B1V.h, prev)
                    for gg in range(2):
                        g = 2 * qq + gg
                        for jj in range(2):
                            inst = tensor.matmul(
                                pst[:, gg * 1024 + jj * 512:gg * 1024 + (jj + 1) * 512],
                                bT[:, g * 128:(g + 1) * 128],
                                xdd[:, g * 1024 + jj * 512:g * 1024 + (jj + 1) * 512],
                                start=True, stop=True)
                    stq_i[(c, qq)] = B1T.inc(inst)
                    assert stq_i[(c, qq)] == bt_stq(c, qq)

        @block.scalar
        def _(scalar):
            for c in range(NCH):
                scalar.wait_ge(B1T.h, trb_i[c])
                if c > 0:
                    scalar.wait_ge(B1T.h, stq_i[(c - 1, 3)])
                evb_i[c] = EVB.inc(scalar.copy(bT[:], ptr[:]))

        @block.gpsimd
        def _(gpsimd):
            pass
            for c in range(NCH):
                gpsimd.wait_ge(B1V.h, bv_sb(c))
                B1D.inc(gpsimd.dma_start(d_spref.ap()[c], Sb[:]), 16)
            gpsimd.wait_ge(B1V.h, bv_pub)
            B1D.inc(gpsimd.dma_start(ag_s_in.ap(), Sb[:]), 16)
            gpsimd.wait_ge(B1D.h, B1D.n)
            gpsimd.wait_ge(B0D.h, B0D.n)
            CCS.inc(gpsimd.collective_compute(
                "AllGather", ALU.bypass, replica_groups=[[0, 1, 2, 3], [4, 5, 6, 7]],
                ins=[ag_s_in.ap().opt()], outs=[ag_s_out.ap().opt()]))
            CCS.inc(gpsimd.collective_compute(
                "AllGather", ALU.bypass, replica_groups=[[0, 1, 2, 3], [4, 5, 6, 7]],
                ins=[ag_g_in.ap().opt()], outs=[ag_g_out.ap().opt()]))
            gpsimd.wait_ge(CCS.h, 2)

    barrier(dma_sems=[XTS, BCS, B1D, SID])
    dbg_dump([(dbg["dbg_spref"], d_spref.ap()[:, :, :]), (dbg["dbg_spub"], ag_s_in.ap()),
              (dbg["dbg_sin"], Sb[:])] if debug else [])
    bs3p.close()
    if upto <= 3:
        bs2.close(); bsA.close(); bsM.close(); es.close()
        return nc, dbg

    bs2.close()

    B2T = Sm("B2T"); B2V = Sm("B2V"); B2S = Sm("B2S"); B2D = Sm("B2D")
    X4 = Sm("X4"); BC4 = Sm("BC4"); SP4 = Sm("SP4"); DB4 = Sm("DB4")
    DP4 = Stream(Sm, "DP4", 1); RR4 = Stream(Sm, "RR4", 2)
    bs4 = ExitStack()
    p4 = lambda *a, **k: bs4.enter_context(nc.sbuf_tensor(*a, **k))
    ccm = p4("ccm", [128, G * 128], BF16)
    rr = [p4(f"rr{i}", [128, 2048], F32) for i in range(2)]
    spc = p4("spc", [128, INTER], BF16)
    suse = p4("suse", [128, INTER], BF16)
    xdc = p4("xdc", [128, INTER], BF16)
    yrb = p4("yrb", [128, INTER], BF16)
    dbc = p4("dbc", [128, INTER], BF16)
    gmt = p4("gmt", [128, G * 128], F32)
    dcb = [p4(f"dcb{i}", [128, 512], F32) for i in range(4)]
    mbb = [p4(f"mbb{i}", [128, 512], BF16) for i in range(4)]
    ucb = [p4(f"ucb{i}", [128, 512], F32) for i in range(2)]
    dpb = [p4("dpb0", [128, 128], F32)] * 2
    pgm = bs4.enter_context(nc.psum_tensor("pgm", [128, 1024], F32))
    pyd = [bs4.enter_context(nc.psum_tensor(f"pyd{i}", [128, 512], F32)) for i in range(2)]
    pyo = [bs4.enter_context(nc.psum_tensor(f"pyo{i}", [128, 512], F32)) for i in range(2)]

    v_tu = lambda c: 69 * c + 1
    v_su = lambda c: 69 * c + 2
    v_xdc = lambda c: 69 * c + 3
    v_gmt = lambda c: 69 * c + 4
    v_mp = lambda c, a: 69 * c + (5 + a if a < 2 else 4 * (a // 2) + 3 + (a % 2))
    v_tt1 = lambda c, b: 69 * c + (4 * b + 9 if b < 15 else 67)
    v_tt2 = lambda c, b: 69 * c + (4 * b + 10 if b < 15 else 68)
    v_yr = lambda c: 69 * c + 69
    t_gm = lambda c: 17 * c + 1
    t_bank = lambda c, b: 17 * c + 2 + b
    s_pack = lambda c, a: 35 * c + a + 1
    s_sq = lambda c: 35 * c + 33
    s_rstd = lambda c: 35 * c + 35
    xm = {}

    with nc.Block() as block:
        @block.sync
        def _(sync):
            DB4.inc(sync.dma_start(dbc[:], dexp.unsqueeze(0).partition_broadcast(128)), 16)
            for c in range(b2n):
                if c > 0:
                    sync.wait_ge(B2T.h, t_bank(c - 1, 15))
                    sync.wait_ge(B2V.h, v_xdc(c - 1))
                xm[(c, "xT")] = X4.inc(sync.dma_start(
                    xTb[:], d_hbc.ap()[0:INTER, c * 128:(c + 1) * 128], transpose=True), 16)
                if c > 0:
                    sync.wait_ge(B2T.h, t_gm(c - 1))
                xm[(c, "bc")] = BC4.inc(sync.dma_start(
                    bcm[:].rearrange("p (g t) -> p g t", t=128),
                    d_hbc.ap()[INTER:INTER + 1024, c * 128:(c + 1) * 128]
                    .rearrange("(g n) t -> n g t", n=128)), 16)
                xm[(c, "cc")] = BC4.inc(sync.dma_start(
                    ccm[:].rearrange("p (g t) -> p g t", t=128),
                    d_hbc.ap()[INTER + 1024:INTER + 2048, c * 128:(c + 1) * 128]
                    .rearrange("(g n) t -> n g t", n=128)), 16)
                if c > 0:
                    sync.wait_ge(B2V.h, v_su(c - 1))
                xm[(c, "sp")] = SP4.inc(sync.dma_start(spc[:], d_spref.ap()[c]), 16)
                if c > 0:
                    sync.wait_ge(B2V.h, v_tu(c - 1))
                DP4.load(sync.dma_start(
                    dpb[c % 2][:], d_dpref.ap()[c].unsqueeze(0).partition_broadcast(128)), c)
                for q in range(8):
                    if c > 0 or q >= 2:
                        pc, pq = (c, q - 2) if q >= 2 else (c - 1, q + 6)
                        sync.wait_ge(B2S.h, s_pack(pc, 4 * pq + 3))
                    RR4.load(sync.dma_start(
                        rr[q % 2][:],
                        d_cum.ap()[c][16 * 128 * q:16 * 128 * (q + 1)]
                        .unsqueeze(0).partition_broadcast(128)), 8 * c + q)

        @block.scalar
        def _(scalar):
            def rms(c):
                scalar.wait_ge(B2V.h, v_tt2(c, 15))
                scalar.wait_ge(B2T.h, t_bank(c, 15))
                n = B2S.inc(scalar.activation(xdc[:], ysb[:], AF.Square,
                                              accum_out=stat[:, 8:9]))
                assert n == s_sq(c), (c, n)
                scalar.wait_ge(B2S.h, n)
                n = B2S.inc(scalar.activation(stat[:, 9:10], stat[:, 8:9], AF.Ln,
                                              bias=c_eps[:], scale=1.0 / INTER))
                scalar.wait_ge(B2S.h, n)
                n = B2S.inc(scalar.activation(stat[:, 10:11], stat[:, 9:10], AF.Exp, scale=-0.5))
                assert n == s_rstd(c), (c, n)

            for c in range(b2n):
                if c >= 1:
                    rms(c - 1)
                for a in range(32):
                    q = a // 4
                    RR4.wait(scalar, 8 * c + q)
                    if c > 0 or a >= 4:
                        pc, pa = (c, a - 4) if a >= 4 else (c - 1, a + 28)
                        scalar.wait_ge(B2V.h, v_mp(pc, pa))
                    for j in range(4):
                        h = 4 * a + j
                        inst = scalar.activation(
                            dcb[a % 4][:, j * 128:(j + 1) * 128],
                            rr[q % 2][:, (h % 16) * 128:(h % 16 + 1) * 128],
                            AF.Exp, bias=biasT_all[:, c * 128 + h:c * 128 + h + 1])
                    n = B2S.inc(inst)
                    assert n == s_pack(c, a)
                pass

            if b2n > 0:
                rms(b2n - 1)

        @block.vector
        def _(vector):
            for c in range(b2n):
                DP4.wait(vector, c)
                if c > 0:
                    vector.wait_ge(B2V.h, v_yr(c - 1))
                    vector.wait_ge(B2D.h, 16 * (c))
                dpv = dpb[c % 2][:].unsqueeze(2).broadcast_to([128, 128, 64])
                n = B2V.inc(vector.tensor_tensor(
                    ysb[:].rearrange("p (h q) -> p h q", q=64),
                    Sb[:].rearrange("p (h q) -> p h q", q=64), dpv, ALU.mult))
                assert n == v_tu(c)
                vector.wait_ge(B2V.h, n)
                vector.wait_ge(SP4.h, 16 * (c + 1))
                if c > 0:
                    vector.wait_ge(B2T.h, t_bank(c - 1, 15))
                n = B2V.inc(vector.tensor_add(suse[:], ysb[:], spc[:]))
                assert n == v_su(c)
                vector.wait_ge(X4.h, 16 * (c + 1))
                vector.wait_ge(DB4.h, 16)
                if c > 0:
                    vector.wait_ge(B2S.h, s_sq(c - 1))
                n = B2V.inc(vector.tensor_mul(xdc[:], xTb[:], dbc[:]))
                assert n == v_xdc(c)
                vector.wait_ge(B2T.h, t_gm(c))
                tv = c_triu[:].unsqueeze(1).broadcast_to([128, 8, 128])
                n = B2V.inc(vector.tensor_tensor(
                    gmt[:].rearrange("p (g l) -> p g l", l=128),
                    pgm[:].rearrange("p (g l) -> p g l", l=128), tv, ALU.mult))
                assert n == v_gmt(c)
                vector.wait_ge(B2V.h, n)
                def emit_pack(a):
                    g = a // 4
                    vector.wait_ge(B2S.h, s_pack(c, a))
                    if c > 0 or a >= 4:
                        pc, pa = (c, a - 4) if a >= 4 else (c - 1, a + 28)
                        vector.wait_ge(B2T.h, t_bank(pc, pa // 2))
                    gv = gmt[:, g * 128:(g + 1) * 128].unsqueeze(1).broadcast_to([128, 4, 128])
                    n = B2V.inc(vector.scalar_tensor_tensor(
                        mbb[a % 4][:].rearrange("p (j l) -> p j l", l=128),
                        dcb[a % 4][:].rearrange("p (j l) -> p j l", l=128),
                        CLAMP, gv, ALU.min, ALU.mult))
                    assert n == v_mp(c, a), (c, a, n)

                def emit_evac(b):
                    vector.wait_ge(B2T.h, t_bank(c, b))
                    if c > 0 or b >= 2:
                        pc, pb = (c, b - 2) if b >= 2 else (c - 1, b + 14)
                        vector.wait_ge(B2V.h, v_tt2(pc, pb))
                    sdv = expcA_all[:, c * 128 + 8 * b:c * 128 + 8 * b + 8] \
                        .unsqueeze(2).broadcast_to([128, 8, 64])
                    n = B2V.inc(vector.tensor_tensor(
                        ucb[b % 2][:].rearrange("p (j q) -> p j q", q=64),
                        pyo[b % 2][:].rearrange("p (j q) -> p j q", q=64), sdv, ALU.mult))
                    assert n == v_tt1(c, b), (c, b, n)
                    vector.wait_ge(B2V.h, n)
                    n = B2V.inc(vector.tensor_add(ysb[:, 512 * b:512 * (b + 1)],
                                                  ucb[b % 2][:], pyd[b % 2][:]))
                    assert n == v_tt2(c, b), (c, b, n)

                for b in range(16):
                    emit_pack(2 * b)
                    emit_pack(2 * b + 1)
                    if b >= 1:
                        emit_evac(b - 1)
                emit_evac(15)
                vector.wait_ge(B2S.h, s_rstd(c))
                if c > 0:
                    vector.wait_ge(B2D.h, 16 * c)
                n = B2V.inc(vector.tensor_scalar(yrb[:], ysb[:], stat[:, 10:11], None, ALU.mult))
                assert n == v_yr(c)

        @block.tensor
        def _(tensor):
            for c in range(b2n):
                tensor.wait_ge(BC4.h, 32 * (c + 1))
                if c > 0:
                    tensor.wait_ge(B2V.h, v_gmt(c - 1))
                for g in range(G):
                    inst = tensor.matmul(pgm[:, g * 128:(g + 1) * 128],
                                         bcm[:, g * 128:(g + 1) * 128],
                                         ccm[:, g * 128:(g + 1) * 128],
                                         start=True, stop=True)
                n = B2T.inc(inst)
                assert n == t_gm(c)
                for b in range(16):
                    g = b // 2
                    tensor.wait_ge(B2V.h, v_mp(c, 2 * b + 1))
                    if c > 0 or b >= 2:
                        pc, pb = (c, b - 2) if b >= 2 else (c - 1, b + 14)
                        tensor.wait_ge(B2V.h, v_tt2(pc, pb))
                    for hh in range(8):
                        h = 8 * b + hh
                        a = h // 4
                        tensor.matmul(pyd[b % 2][:, hh * 64:(hh + 1) * 64],
                                      mbb[a % 4][:, (h % 4) * 128:(h % 4 + 1) * 128],
                                      xTb[:, h * 64:(h + 1) * 64],
                                      start=(hh == 0), stop=False)
                    tensor.matmul(pyd[b % 2][:, 0:512], c_identb[:],
                                  xdc[:, 512 * b:512 * (b + 1)], start=False, stop=True)
                    inst = tensor.matmul(pyo[b % 2][:, 0:512], ccm[:, g * 128:(g + 1) * 128],
                                         suse[:, 512 * b:512 * (b + 1)], start=True, stop=True)
                    n = B2T.inc(inst)
                    assert n == t_bank(c, b)

        @block.gpsimd
        def _(gpsimd):
            for c in range(b2n):
                gpsimd.wait_ge(B2V.h, v_yr(c))
                B2D.inc(gpsimd.dma_start(d_yrms.ap()[c * 128:(c + 1) * 128, :], yrb[:]), 16)
            gpsimd.wait_ge(B2D.h, B2D.n)

    barrier(dma_sems=[X4, BC4, SP4, DB4, B2D])
    dbg_dump([(dbg["dbg_yrms"], d_yrms.ap()[:, :])] if debug else [])
    bs4.close()
    if upto <= 4:
        bsA.close(); bsM.close(); es.close()
        return nc, dbg

    bsA.close(); bsM.close()

    YGV = Sm("YGV"); OPK = Sm("OPK"); EOP = Sm("EOP")
    YTR = Stream(Sm, "YTR", 2); SGD = Stream(Sm, "SGD", 2)
    OWS = Stream(Sm, "OWS", 2); AOS = Stream(Sm, "AOS", 2)
    bs5 = ExitStack()
    p5 = lambda *a, **k: bs5.enter_context(nc.sbuf_tensor(*a, **k))
    YG = p5("YG", [128, 64 * 1024], BF16)
    ytr = [p5(f"ytr{i}", [128, 1024], BF16) for i in range(2)]
    sgl = [p5(f"sgl{i}", [128, 1024], BF16) for i in range(2)]
    ow = [p5(f"ow{i}", [128, 16 * 512], BF16) for i in range(2)]
    ao = [p5(f"ao{i}", [128, 512], BF16) for i in range(2)]
    pso = bs5.enter_context(nc.psum_tensor("pso", [128, 512 * 8], F32))

    with nc.Block() as block:
        @block.sync
        def _(sync):
            for j in range(64):
                if j >= 2:
                    sync.wait_ge(YGV.h, j - 1)
                YTR.load(sync.dma_start(ytr[j % 2][:],
                                        d_yrms.ap()[0:1024, j * 128:(j + 1) * 128],
                                        transpose=True), j)
                SGD.load(sync.dma_start(sgl[j % 2][:],
                                        d_sg.ap()[j * 128:(j + 1) * 128, :]), j)
            for dj in range(8):
                for kg in range(4):
                    i = 4 * dj + kg
                    if i >= 2:
                        sync.wait_ge(OPK.h, i - 1)
                    OWS.load(sync.dma_start(ow[i % 2][:], opw[dj, kg]), i)

        @block.vector
        def _(vector):
            for j in range(64):
                YTR.wait(vector, j)
                SGD.wait(vector, j)
                YGV.inc(vector.tensor_mul(YG[:, j * 1024:(j + 1) * 1024],
                                          ytr[j % 2][:], sgl[j % 2][:]))

        @block.tensor
        def _(tensor):
            for dj in range(8):
                if dj > 0:
                    tensor.wait_ge(EOP.h, 8 * dj)
                for kg in range(4):
                    i = 4 * dj + kg
                    OWS.wait(tensor, i)
                    for kt in range(16):
                        k = 16 * kg + kt
                        tensor.wait_ge(YGV.h, k + 1)
                        for t in range(8):
                            inst = tensor.matmul(
                                pso[:, t * 512:(t + 1) * 512],
                                YG[:, k * 1024 + t * 128:k * 1024 + (t + 1) * 128],
                                ow[i % 2][:, kt * 512:(kt + 1) * 512],
                                start=(k == 0), stop=(k == 63))
                    OPK.inc(inst)

        @block.scalar
        def _(scalar):
            for dj in range(8):
                scalar.wait_ge(OPK.h, 4 * dj + 4)
                for t in range(8):
                    i = 8 * dj + t
                    AOS.wait(scalar, i - 2)
                    EOP.inc(scalar.activation(ao[i % 2][:], pso[:, t * 512:(t + 1) * 512],
                                              AF.Copy))

        @block.gpsimd
        def _(gpsimd):
            for dj in range(8):
                for t in range(8):
                    i = 8 * dj + t
                    gpsimd.wait_ge(EOP.h, i + 1)
                    AOS.load(gpsimd.dma_start(
                        d_attn.ap()[t * 128:(t + 1) * 128, dj * 512:(dj + 1) * 512],
                        ao[i % 2][:]), i)
            for b in range(2):
                gpsimd.wait_ge(AOS.s[b].h, AOS.s[b].n)

    barrier(dma_sems=[])
    dbg_dump([(dbg["dbg_attn"], d_attn.ap()[:, :])] if debug else [])
    bs5.close()
    if upto <= 5:
        es.close()
        return nc, dbg

    DS = Sm("DS"); DV = Sm("DV")
    ATL = Stream(Sm, "ATL", 2); XBL = Stream(Sm, "XBL", 2)
    XNS = Stream(Sm, "XNS", 2); XL2S = Stream(Sm, "XL2S", 2)
    bs6 = ExitStack()
    p6 = lambda *a, **k: bs6.enter_context(nc.sbuf_tensor(*a, **k))
    atb = [p6(f"atb{i}", [128, DIM], BF16) for i in range(2)]
    xbb = [p6(f"xbb{i}", [128, DIM], F32) for i in range(2)]
    xnb = [p6(f"xnb{i}", [128, DIM], F32) for i in range(2)]
    xl2b = [p6(f"xl2b{i}", [128, DIM], BF16) for i in range(2)]
    tb6 = p6("tb6", [128, DIM], F32)
    dump6 = p6("dump6", [128, DIM], F32)

    with nc.Block() as block:
        @block.sync
        def _(sync):
            for i in range(8):
                if i >= 2:
                    sync.wait_ge(DV.h, 11 * (i - 1))
                    sync.wait_ge(DS.h, 6 * (i - 1))
                ATL.load(sync.dma_start(atb[i % 2][:], d_attn.ap()[i * 128:(i + 1) * 128, :]), i)
                XBL.load(sync.dma_start(xbb[i % 2][:], xh[3 + i * 128:3 + (i + 1) * 128, :]), i)

        @block.scalar
        def _(scalar):
            for i in range(8):
                b = i % 2
                ATL.wait(scalar, i)
                n = DS.inc(scalar.activation(dump6[:], atb[b][:], AF.Square,
                                             accum_out=stat[:, 0:1]))
                scalar.wait_ge(DV.h, 11 * i + 4)
                n = DS.inc(scalar.activation(stat[:, 11:12], stat[:, 3:4], AF.Ln, bias=c_eps[:]))
                scalar.wait_ge(DS.h, n)
                DS.inc(scalar.activation(stat[:, 4:5], stat[:, 11:12], AF.Exp, scale=-0.5))
                scalar.wait_ge(DV.h, 11 * i + 6)
                n = DS.inc(scalar.activation(dump6[:], xnb[b][:], AF.Square,
                                             accum_out=stat[:, 5:6]))
                scalar.wait_ge(DV.h, 11 * i + 10)
                n = DS.inc(scalar.activation(stat[:, 12:13], stat[:, 7:8], AF.Ln, bias=c_eps[:]))
                scalar.wait_ge(DS.h, n)
                DS.inc(scalar.activation(stat[:, 8:9], stat[:, 12:13], AF.Exp, scale=-0.5))

        @block.vector
        def _(vector):
            for i in range(8):
                b = i % 2
                ATL.wait(vector, i)
                if i > 0:
                    vector.wait_ge(DS.h, 6 * (i - 1) + 2)
                n = DV.inc(vector.tensor_reduce(stat[:, 1:2], atb[b][:], AX.X, ALU.add))
                vector.wait_ge(DV.h, n)
                n = DV.inc(vector.tensor_scalar(stat[:, 2:3], stat[:, 1:2], 1.0 / DIM, None, ALU.mult))
                vector.wait_ge(DV.h, n)
                n = DV.inc(vector.tensor_mul(stat[:, 13:14], stat[:, 2:3], stat[:, 2:3]))
                vector.wait_ge(DV.h, n)
                vector.wait_ge(DS.h, 6 * i + 1)
                n = DV.inc(vector.tensor_scalar(stat[:, 3:4], stat[:, 0:1], 1.0 / DIM,
                                                stat[:, 13:14], ALU.mult, ALU.subtract))
                assert n == 11 * i + 4
                vector.wait_ge(DS.h, 6 * i + 3)
                n = DV.inc(vector.tensor_scalar(tb6[:], atb[b][:], stat[:, 2:3],
                                                stat[:, 4:5], ALU.subtract, ALU.mult))
                XBL.wait(vector, i)
                vector.wait_ge(DV.h, n)
                XNS.wait(vector, i - 2)
                n = DV.inc(vector.tensor_add(xnb[b][:], tb6[:], xbb[b][:]))
                assert n == 11 * i + 6
                vector.wait_ge(DV.h, n)
                n = DV.inc(vector.tensor_reduce(stat[:, 6:7], xnb[b][:], AX.X, ALU.add))
                vector.wait_ge(DV.h, n)
                n = DV.inc(vector.tensor_scalar(stat[:, 14:15], stat[:, 6:7], 1.0 / DIM, None, ALU.mult))
                vector.wait_ge(DV.h, n)
                n = DV.inc(vector.tensor_mul(stat[:, 15:16], stat[:, 14:15], stat[:, 14:15]))
                vector.wait_ge(DV.h, n)
                vector.wait_ge(DS.h, 6 * i + 4)
                n = DV.inc(vector.tensor_scalar(stat[:, 7:8], stat[:, 5:6], 1.0 / DIM,
                                                stat[:, 15:16], ALU.mult, ALU.subtract))
                assert n == 11 * i + 10
                vector.wait_ge(DS.h, 6 * i + 6)
                XL2S.wait(vector, i - 2)
                vector.wait_ge(DV.h, n)
                n = DV.inc(vector.tensor_scalar(xl2b[b][:], xnb[b][:], stat[:, 14:15],
                                                stat[:, 8:9], ALU.subtract, ALU.mult))
                assert n == 11 * i + 11

        @block.gpsimd
        def _(gpsimd):
            for i in range(8):
                gpsimd.wait_ge(DV.h, 11 * i + 6)
                XNS.load(gpsimd.dma_start(d_xnew.ap()[i * 128:(i + 1) * 128, :],
                                          xnb[i % 2][:]), i)
                gpsimd.wait_ge(DV.h, 11 * i + 11)
                XL2S.load(gpsimd.dma_start(d_xln2.ap()[i * 128:(i + 1) * 128, :],
                                           xl2b[i % 2][:]), i)
            for st_ in (XNS, XL2S):
                for b in range(2):
                    gpsimd.wait_ge(st_.s[b].h, st_.s[b].n)

    barrier(dma_sems=[])
    dbg_dump([(dbg["dbg_xnew"], d_xnew.ap()[:, :]), (dbg["dbg_xln2"], d_xln2.ap()[:, :])]
             if debug else [])
    bs6.close()
    if upto <= 6:
        es.close()
        return nc, dbg

    EXF = Sm("EXF"); EC = EXF; ERT = Sm("ERT"); ERV = Sm("ERV"); ERS = Sm("ERS")
    RTT = Sm("RTT"); RTE = Sm("RTE"); EPK = Sm("EPK"); EFE = Sm("EFE")
    ES = Sm("ES"); EV = Sm("EV"); OUTS = Sm("OUTS")
    BWS = Stream(Sm, "BWS", 2); BTS = Stream(Sm, "BTS", 2); XNL = Stream(Sm, "XNL", 1)
    FFS = Stream(Sm, "FFS", 2); FFL = Stream(Sm, "FFL", 2)
    bs7 = ExitStack()
    p7 = lambda *a, **k: bs7.enter_context(nc.sbuf_tensor(*a, **k))
    x2fm = p7("x2fm", [128, KT * 1024], BF16)
    bbr = p7("bbr", [1, DIM], BF16)
    RH = p7("RH", [128, 8 * 64], F32)
    PT = p7("PT", [128, 8 * 16], F32)
    CMB = p7("CMB", [128, 8 * 16], F32)
    RSC = p7("RSC", [128, 64], BF16)
    RT = p7("RT", [64, 128 * 128], BF16)

    bs7p = ExitStack()
    arr = bs7p.enter_context(nc.sbuf_tensor("arr", [128, KT * 80], BF16))
    dump7a = bs7p.enter_context(nc.sbuf_tensor("dump7a", [128, 64], F32))
    psr = bs7p.enter_context(nc.psum_tensor("psr", [128, 4 * 512], F32))
    ptb = bs7p.enter_context(nc.psum_tensor("ptb", [64, 2048], BF16))
    with nc.Block() as block:
        @block.sync
        def _(sync):
            for k in range(KT):
                EXF.inc(sync.dma_start(x2fm[:, k * 1024:(k + 1) * 1024],
                                       d_xln2.ap()[0:1024, k * 128:(k + 1) * 128],
                                       transpose=True), 16)
            EC.inc(sync.dma_start(arr[:], arw), 16)
            EC.inc(sync.dma_start(bbr[:], bbias), 16)

        @block.tensor
        def _(tensor):
            tensor.wait_ge(EXF.h, EXF.n)
            for t in range(8):
                if t >= 4:
                    tensor.wait_ge(ERS.h, 2 * (t - 4) + 2)
                for k in range(KT):
                    inst = tensor.matmul(psr[:, (t % 4) * 512:(t % 4) * 512 + 80],
                                         x2fm[:, k * 1024 + t * 128:k * 1024 + (t + 1) * 128],
                                         arr[:, k * 80:(k + 1) * 80],
                                         start=(k == 0), stop=(k == KT - 1))
                ERT.inc(inst)
            for t in range(8):
                for e in range(16):
                    i = 16 * t + e
                    tensor.wait_ge(ERV.h, 104 + i + 1)
                    if i >= 16:
                        tensor.wait_ge(RTE.h, i // 8 - 1)
                    inst = tensor.transpose(ptb[:, (i % 16) * 128:(i % 16 + 1) * 128],
                                            RSC[:], c_identb[:])
                    RTT.inc(inst)

        @block.vector
        def _(vector):
            def part1(t):
                vector.wait_ge(ERT.h, t + 1)
                if t > 0:
                    vector.wait_ge(ERS.h, 2 * (t - 1) + 2)
                n = ERV.inc(vector.tensor_copy(RH[:, t * 64:(t + 1) * 64],
                                               psr[:, (t % 4) * 512:(t % 4) * 512 + 64]))
                vector.wait_ge(ERV.h, n)
                n = ERV.inc(vector.reduce_max(stat[:, 0:1],
                                              psr[:, (t % 4) * 512 + 64:(t % 4) * 512 + 80], AX.X))
                assert n == 13 * t + 2, (t, n)

            def top2(t):
                vector.wait_ge(ERS.h, 2 * t + 2)
                pt = PT[:, t * 16:(t + 1) * 16]
                n = ERV.inc(vector.reduce_max(stat[:, 2:3], pt, AX.X))
                vector.wait_ge(ERV.h, n)
                n = ERV.inc(vector.tensor_scalar(dump7a[:, 0:16], pt, stat[:, 2:3], None, ALU.is_lt))
                vector.wait_ge(ERV.h, n)
                n = ERV.inc(vector.tensor_mul(dump7a[:, 16:32], pt, dump7a[:, 0:16]))
                vector.wait_ge(ERV.h, n)
                n = ERV.inc(vector.reduce_max(stat[:, 3:4], dump7a[:, 16:32], AX.X))
                vector.wait_ge(ERV.h, n)
                n = ERV.inc(vector.tensor_add(stat[:, 4:5], stat[:, 2:3], stat[:, 3:4]))
                vector.wait_ge(ERV.h, n)
                n = ERV.inc(vector.tensor_scalar(stat[:, 5:6], stat[:, 1:2], 1e-6,
                                                 stat[:, 4:5], ALU.mult, ALU.add))
                assert n == 13 * t + 8, (t, n)
                vector.wait_ge(ERV.h, n)
                n = ERV.inc(vector.tensor_scalar(stat[:, 5:6], stat[:, 5:6], 2.0, None, ALU.mult))
                vector.wait_ge(ERV.h, n)
                n = ERV.inc(vector.reciprocal(stat[:, 6:7], stat[:, 5:6]))
                vector.wait_ge(ERV.h, n)
                n = ERV.inc(vector.tensor_scalar(dump7a[:, 32:48], pt, stat[:, 3:4], None, ALU.is_ge))
                vector.wait_ge(ERV.h, n)
                n = ERV.inc(vector.tensor_mul(dump7a[:, 48:64], pt, dump7a[:, 32:48]))
                vector.wait_ge(ERV.h, n)
                n = ERV.inc(vector.tensor_scalar(CMB[:, t * 16:(t + 1) * 16], dump7a[:, 48:64],
                                                 stat[:, 6:7], None, ALU.mult))
                assert n == 13 * t + 13, (t, n)

            part1(0)
            for t in range(1, 8):
                top2(t - 1)
                part1(t)
            top2(7)
            def rt_evac(p):
                vector.wait_ge(RTT.h, 8 * (p + 1))
                RTE.inc(vector.tensor_copy(
                    RT[:, p * 1024:(p + 1) * 1024],
                    ptb[:, (p % 2) * 1024:(p % 2 + 1) * 1024]))

            for t in range(8):
                for e in range(16):
                    i = 16 * t + e
                    if i >= 8 and i % 8 == 0:
                        rt_evac(i // 8 - 1)
                    if i > 0:
                        vector.wait_ge(RTT.h, i)
                    n = ERV.inc(vector.tensor_scalar(RSC[:], RH[:, t * 64:(t + 1) * 64],
                                                     CMB[:, t * 16 + e:t * 16 + e + 1],
                                                     None, ALU.mult))
                    assert n == 104 + i + 1, (i, n)
            rt_evac(15)

        @block.scalar
        def _(scalar):
            for t in range(8):
                scalar.wait_ge(ERV.h, 13 * t + 2)
                if t > 0:
                    scalar.wait_ge(ERV.h, 13 * (t - 1) + 8)
                n = ERS.inc(scalar.activation(stat[:, 7:8], stat[:, 0:1], AF.Copy, scale=-1.0))
                scalar.wait_ge(ERS.h, n)
                n = ERS.inc(scalar.activation(PT[:, t * 16:(t + 1) * 16],
                                              psr[:, (t % 4) * 512 + 64:(t % 4) * 512 + 80],
                                              AF.Exp, bias=stat[:, 7:8],
                                              accum_out=stat[:, 1:2]))
                assert n == 2 * t + 2

        @block.gpsimd
        def _(gpsimd):
            pass

    bs7p.close()

    bs7q = ExitStack()
    q7 = lambda *a, **k: bs7q.enter_context(nc.sbuf_tensor(*a, **k))
    bwb = [q7(f"bwb{i}", [128, 8 * 512], BF16) for i in range(2)]
    btb = [q7(f"btb{i}", [64, 8 * 512], BF16) for i in range(2)]
    xnl = [q7("xnl0", [128, DIM], F32)] * 2
    ob = [q7("ob0", [128, DIM], F32)] * 2
    fs = [q7(f"fs{i}", [128, 512], BF16) for i in range(2)]
    fol = [q7(f"fol{i}", [128, DIM], BF16) for i in range(2)]
    dump7 = q7("dump7", [128, DIM], F32)
    pso2 = bs7q.enter_context(nc.psum_tensor("pso2", [128, 8 * 512], F32))
    with nc.Block() as block:
        @block.sync
        def _(sync):
            for oj in range(8):
                for q in range(4):
                    i = 4 * oj + q
                    if i >= 2:
                        prev = i - 2
                        sync.wait_ge(EPK.h, 6 * (prev // 4) + (prev % 4) + 1)
                    BWS.load(sync.dma_start(
                        bwb[i % 2][:], bw[oj, q // 2][:, (q % 2) * 4096:(q % 2 + 1) * 4096]), i)
                for hh in range(2):
                    j = 2 * oj + hh
                    if j >= 2:
                        prev = j - 2
                        sync.wait_ge(EPK.h, 6 * (prev // 2) + 5 + (prev % 2))
                    BTS.load(sync.dma_start(
                        btb[j % 2][:], betw[oj][:, hh * 4096:(hh + 1) * 4096]), j)
            for b_ in range(2):
                sync.wait_ge(FFS.s[b_].h, 16 * 32)
            for i in range(8):
                if i >= 1:
                    sync.wait_ge(EV.h, 6 * i)
                FFL.load(sync.dma_start(fol[i % 2][:], d_ffn.ap()[i * 128:(i + 1) * 128, :]), i)
                XNL.load(sync.dma_start(xnl[i % 2][:], d_xnew.ap()[i * 128:(i + 1) * 128, :]), i)

        @block.tensor
        def _(tensor):
            tensor.wait_ge(RTE.h, 16)
            for oj in range(8):
                if oj > 0:
                    tensor.wait_ge(EFE.h, 8 * oj)
                for q in range(4):
                    i = 4 * oj + q
                    BWS.wait(tensor, i)
                    for kt in range(8):
                        k = 8 * q + kt
                        for t in range(8):
                            inst = tensor.matmul(
                                pso2[:, t * 512:(t + 1) * 512],
                                x2fm[:, k * 1024 + t * 128:k * 1024 + (t + 1) * 128],
                                bwb[i % 2][:, kt * 512:(kt + 1) * 512],
                                start=(k == 0), stop=False)
                    n = EPK.inc(inst)
                    assert n == 6 * oj + q + 1
                for t in range(8):
                    tensor.matmul(pso2[:, t * 512:(t + 1) * 512], c_onesb[:],
                                  bbr[:, oj * 512:(oj + 1) * 512], start=False, stop=False)
                for hh in range(2):
                    j = 2 * oj + hh
                    BTS.wait(tensor, j)
                    for e in range(8):
                        eg = 8 * hh + e
                        for t in range(8):
                            inst = tensor.matmul(
                                pso2[:, t * 512:(t + 1) * 512],
                                RT[:, (t * 16 + eg) * 128:(t * 16 + eg + 1) * 128],
                                btb[j % 2][:, e * 512:(e + 1) * 512],
                                start=False, stop=(eg == 15))
                    n = EPK.inc(inst)
                    assert n == 6 * oj + 5 + hh

        @block.scalar
        def _(scalar):
            for oj in range(8):
                scalar.wait_ge(EPK.h, 6 * oj + 6)
                for t in range(8):
                    i = 8 * oj + t
                    FFS.wait(scalar, i - 2)
                    EFE.inc(scalar.activation(fs[i % 2][:],
                                              pso2[:, t * 512:(t + 1) * 512], AF.Copy))
            for i in range(8):
                FFL.wait(scalar, i)
                if i > 0:
                    scalar.wait_ge(EV.h, 6 * (i - 1) + 6)
                n = ES.inc(scalar.activation(dump7[:], fol[i % 2][:], AF.Square,
                                             accum_out=stat[:, 0:1]))
                assert n == 3 * i + 1
                scalar.wait_ge(EV.h, 6 * i + 4)
                n = ES.inc(scalar.activation(stat[:, 11:12], stat[:, 3:4], AF.Ln, bias=c_eps[:]))
                scalar.wait_ge(ES.h, n)
                n = ES.inc(scalar.activation(stat[:, 4:5], stat[:, 11:12], AF.Exp, scale=-0.5))
                assert n == 3 * i + 3

        @block.vector
        def _(vector):
            for i in range(8):
                b = i % 2
                vector.wait_ge(ES.h, 3 * i + 1)
                if i > 0:
                    vector.wait_ge(ES.h, 3 * (i - 1) + 2)
                n = EV.inc(vector.tensor_reduce(stat[:, 1:2], fol[i % 2][:],
                                                AX.X, ALU.add))
                vector.wait_ge(EV.h, n)
                n = EV.inc(vector.tensor_scalar(stat[:, 2:3], stat[:, 1:2], 1.0 / DIM, None, ALU.mult))
                vector.wait_ge(EV.h, n)
                n = EV.inc(vector.tensor_mul(stat[:, 13:14], stat[:, 2:3], stat[:, 2:3]))
                vector.wait_ge(EV.h, n)
                n = EV.inc(vector.tensor_scalar(stat[:, 3:4], stat[:, 0:1], 1.0 / DIM,
                                                stat[:, 13:14], ALU.mult, ALU.subtract))
                assert n == 6 * i + 4
                vector.wait_ge(ES.h, 3 * i + 3)
                n = EV.inc(vector.tensor_scalar(dump7[:], fol[i % 2][:],
                                                stat[:, 2:3], stat[:, 4:5], ALU.subtract, ALU.mult))
                XNL.wait(vector, i)
                vector.wait_ge(EV.h, n)
                if i >= 1:
                    vector.wait_ge(OUTS.h, 16 * i)
                n = EV.inc(vector.tensor_add(ob[b][:], dump7[:], xnl[b][:]))
                assert n == 6 * i + 6

        @block.gpsimd
        def _(gpsimd):
            for oj in range(8):
                for t in range(8):
                    i = 8 * oj + t
                    gpsimd.wait_ge(EFE.h, i + 1)
                    FFS.load(gpsimd.dma_start(
                        d_ffn.ap()[t * 128:(t + 1) * 128, oj * 512:(oj + 1) * 512],
                        fs[i % 2][:]), i)
            for i in range(8):
                gpsimd.wait_ge(EV.h, 6 * i + 6)
                OUTS.inc(gpsimd.dma_start(out[i * 128:(i + 1) * 128, :], ob[i % 2][:]), 16)
            gpsimd.wait_ge(OUTS.h, 16 * 8)

    bs7q.close()
    bs7.close()
    es.close()
    return nc, dbg

def _prep_shared(i):
    w = {}
    ipw = np.asarray(i["L0_attn_in_proj_w"])
    w["w1"] = np.ascontiguousarray(
        ipw.reshape(FT, 128, KT, 128).transpose(0, 3, 2, 1).reshape(FT, 128, KT * 128)
    ).astype(BF)
    cw = np.asarray(i["L0_attn_conv_w"])
    cd = np.zeros((FT_HBC, 4, 128, 128), np.float32)
    idx = np.arange(128)
    cd[:, :, idx, idx] = cw.reshape(FT_HBC, 128, 4).transpose(0, 2, 1)
    w["convd"] = np.ascontiguousarray(
        cd.transpose(0, 2, 1, 3).reshape(FT_HBC, 128, 4 * 128)).astype(BF)
    w["convb"] = np.ascontiguousarray(np.asarray(i["L0_attn_conv_b"]).reshape(FT_HBC, 128).T).astype(np.float32)
    w["normw"] = np.ascontiguousarray(np.asarray(i["L0_attn_norm_w"]).reshape(FT_GATE, 128).T).astype(np.float32)
    w["dtbias"] = np.asarray(i["L0_attn_dt_bias"]).reshape(128, 1).astype(np.float32)
    w["alog"] = np.asarray(i["L0_attn_A_log"]).reshape(128, 1).astype(np.float32)
    w["dexp"] = np.repeat(np.asarray(i["L0_attn_D"]).astype(np.float32), P).astype(BF)
    opw = np.asarray(i["L0_attn_out_proj_w"])
    w["opw"] = np.ascontiguousarray(
        opw.T.reshape(4, 16, 128, 8, 512).transpose(3, 0, 2, 1, 4).reshape(8, 4, 128, 16 * 512)
    ).astype(BF)
    bwm = np.asarray(i["L0_ffn_base_w"])
    w["bw"] = np.ascontiguousarray(
        bwm.T.reshape(2, 16, 128, 8, 512).transpose(3, 0, 2, 1, 4).reshape(8, 2, 128, 16 * 512)
    ).astype(BF)
    A = np.asarray(i["L0_ffn_A"])
    rw = np.asarray(i["L0_ffn_router_w"])
    ar = np.concatenate([A.T, rw.T], axis=1)
    w["arw"] = np.ascontiguousarray(ar.reshape(KT, 128, 80).transpose(1, 0, 2).reshape(128, KT * 80)).astype(BF)
    B = np.asarray(i["L0_ffn_B"])
    w["betw"] = np.ascontiguousarray(
        B.transpose(2, 0, 1).reshape(64, 16, 8, 512).transpose(2, 0, 1, 3).reshape(8, 64, 16 * 512)
    ).astype(BF)
    w["bbias"] = np.asarray(i["L0_ffn_base_b"]).reshape(1, DIM).astype(BF)
    w["identb"] = np.eye(128, dtype=np.float32).astype(BF)
    w["identf"] = np.eye(128, dtype=np.float32)
    w["triu01"] = np.triu(np.ones((128, 128), np.float32))
    w["onesb"] = np.ones((1, 128), np.float32).astype(BF)
    w["epsb"] = np.full((128, 1), EPS, np.float32)
    return w


def make_in_maps(inputs):
    shared = _prep_shared(inputs)
    x = np.asarray(inputs["L0_attn_hidden_states"])
    maps = []
    for c in range(NCORES):
        b, q = c // 4, c % 4
        xhc = np.zeros((TH, DIM), np.float32)
        if q > 0:
            xhc[:] = x[b, q * T - 3:(q + 1) * T]
        else:
            xhc[3:] = x[b, 0:T]
        rank = q
        rm = np.zeros((1, 3 * 128), np.float32)
        for j in range(3):
            if j < rank:
                rm[0, j * 128:(j + 1) * 128] = 1.0
        m = dict(shared)
        m["xh"] = xhc
        m["rmask"] = rm
        m["rmaskc"] = 1.0 - rm
        maps.append(m)
    return maps


def kernel(**inputs):
    nc, _ = build(debug=False, upto=99)
    maps = make_in_maps(inputs)
    res = run_bass_kernel_spmd(nc, maps, core_ids=list(range(NCORES)))
    out = np.zeros((2, 4096, DIM), np.float32)
    for c in range(NCORES):
        b, q = c // 4, c % 4
        out[b, q * T:(q + 1) * T] = res.results[c]["out"]
    return out


# version 03
# speedup vs baseline: 1.0128x; 1.0128x over previous
import numpy as np
import ml_dtypes
from contextlib import ExitStack

import concourse.bass as bass
import concourse.mybir as mybir
from concourse.bass_utils import run_bass_kernel_spmd

F32 = mybir.dt.float32
BF16 = mybir.dt.bfloat16
AF = mybir.ActivationFunctionType
ALU = mybir.AluOpType
AX = mybir.AxisListType
BF = ml_dtypes.bfloat16

NCORES = 8
T = 1024
TH = T + 3
CS = 128
NCH = T // CS
H, P, G, N = 128, 64, 8, 128
DIM = 4096
INTER = 8192
CONV = 10240
FT_GATE, FT_HBC = 64, 80
FT = FT_GATE + FT_HBC + 1
KT = DIM // 128
EPS = 1e-5
NEG = -1e9
CLAMP = 1e30


class Sem:
    def __init__(self, nc, es, name):
        self.h = es.enter_context(nc.semaphore(name))
        self.n = 0

    def inc(self, inst, k=1):
        inst.then_inc(self.h, k)
        self.n += k
        return self.n

    def wait(self, eng, n=None):
        eng.wait_ge(self.h, self.n if n is None else n)


class Stream:
    def __init__(self, mk, name, nbuf):
        self.s = [mk(f"{name}{i}") for i in range(nbuf)]
        self.nb = nbuf

    def load(self, inst, i, k=16):
        b = i % self.nb
        return self.s[b].inc(inst, k)

    def wait(self, eng, i):
        if i < 0:
            return
        b = i % self.nb
        eng.wait_ge(self.s[b].h, 16 * (i // self.nb + 1))


def build(debug=False, upto=99, b2n=NCH):
    nc = bass.Bass("TRN2", target_bir_lowering=False)
    ap = lambda t: t.ap()

    xh = ap(nc.dram_tensor("xh", [TH, DIM], F32, kind="ExternalInput"))
    w1 = ap(nc.dram_tensor("w1", [FT, 128, KT * 128], BF16, kind="ExternalInput"))
    convd = ap(nc.dram_tensor("convd", [FT_HBC, 128, 4 * 128], BF16, kind="ExternalInput"))
    convb = ap(nc.dram_tensor("convb", [128, FT_HBC], F32, kind="ExternalInput"))
    normw = ap(nc.dram_tensor("normw", [128, FT_GATE], F32, kind="ExternalInput"))
    dtbias = ap(nc.dram_tensor("dtbias", [128, 1], F32, kind="ExternalInput"))
    alog = ap(nc.dram_tensor("alog", [128, 1], F32, kind="ExternalInput"))
    dexp = ap(nc.dram_tensor("dexp", [INTER], BF16, kind="ExternalInput"))
    opw = ap(nc.dram_tensor("opw", [8, 4, 128, 16 * 512], BF16, kind="ExternalInput"))
    bw = ap(nc.dram_tensor("bw", [8, 2, 128, 16 * 512], BF16, kind="ExternalInput"))
    arw = ap(nc.dram_tensor("arw", [128, KT * 80], BF16, kind="ExternalInput"))
    betw = ap(nc.dram_tensor("betw", [8, 64, 16 * 512], BF16, kind="ExternalInput"))
    bbias = ap(nc.dram_tensor("bbias", [1, DIM], BF16, kind="ExternalInput"))
    identb = ap(nc.dram_tensor("identb", [128, 128], BF16, kind="ExternalInput"))
    identf = ap(nc.dram_tensor("identf", [128, 128], F32, kind="ExternalInput"))
    triu01 = ap(nc.dram_tensor("triu01", [128, 128], F32, kind="ExternalInput"))
    onesb = ap(nc.dram_tensor("onesb", [1, 128], BF16, kind="ExternalInput"))
    rmask = ap(nc.dram_tensor("rmask", [1, 3 * 128], F32, kind="ExternalInput"))
    rmaskc = ap(nc.dram_tensor("rmaskc", [1, 3 * 128], F32, kind="ExternalInput"))
    epsb = ap(nc.dram_tensor("epsb", [128, 1], F32, kind="ExternalInput"))

    out = ap(nc.dram_tensor("out", [T, DIM], F32, kind="ExternalOutput"))

    d_xln = nc.dram_tensor("d_xln", [1152, DIM], BF16)
    d_hbc = nc.dram_tensor("d_hbc", [CONV, T], BF16)
    d_sg = nc.dram_tensor("d_sg", [INTER, T], BF16)
    d_cum = nc.dram_tensor("d_cum", [NCH, 128 * 128], F32)
    d_lastr = nc.dram_tensor("d_lastr", [NCH, 128], F32)
    d_dpref = nc.dram_tensor("d_dpref", [NCH + 1, 128], F32)
    d_spref = nc.dram_tensor("d_spref", [NCH, 128, INTER], BF16)
    d_yrms = nc.dram_tensor("d_yrms", [T, INTER], BF16)
    d_attn = nc.dram_tensor("d_attn", [T, DIM], BF16)
    d_xnew = nc.dram_tensor("d_xnew", [T, DIM], F32)
    d_xln2 = nc.dram_tensor("d_xln2", [T, DIM], BF16)
    d_ffn = nc.dram_tensor("d_ffn", [T, DIM], BF16)
    ag_s_in = nc.dram_tensor("ag_s_in", [128, INTER], BF16)
    ag_s_out = nc.dram_tensor("ag_s_out", [4 * 128, INTER], BF16)
    ag_g_in = nc.dram_tensor("ag_g_in", [1, 128], F32)
    ag_g_out = nc.dram_tensor("ag_g_out", [4, 128], F32)
    d_m = nc.dram_tensor("d_m", [3, 128], F32)

    dbg = {}
    if debug:
        for nm, shp, dt in [
            ("dbg_xln", [1027, DIM], BF16), ("dbg_hbc", [CONV, T], BF16),
            ("dbg_g0", [128, T], F32), ("dbg_s0", [128, T], F32),
            ("dbg_sg", [INTER, T], BF16), ("dbg_dt", [128, T], F32),
            ("dbg_cum", [NCH, 128 * 128], F32), ("dbg_spref", [NCH, 128, INTER], BF16),
            ("dbg_spub", [128, INTER], BF16), ("dbg_sin", [128, INTER], BF16),
            ("dbg_yrms", [T, INTER], BF16), ("dbg_attn", [T, DIM], BF16),
            ("dbg_bt", [128, 1024], BF16), ("dbg_xdd", [128, INTER], BF16),
            ("dbg_ddt", [128, T], mybir.dt.float32), ("dbg_xtb", [128, INTER], BF16),
            ("dbg_xnew", [T, DIM], F32), ("dbg_xln2", [T, DIM], BF16),
        ]:
            dbg[nm] = ap(nc.dram_tensor(nm, shp, dt, kind="ExternalOutput"))

    es = ExitStack()
    Sm = lambda name: Sem(nc, es, name)
    sb = lambda *a, **k: es.enter_context(nc.sbuf_tensor(*a, **k))

    PH = Sm("PH")
    DBGS = Sm("DBGS")
    nphase = [0]

    def barrier(dma_sems=()):
        nphase[0] += 1
        tgt = 5 * nphase[0]
        PH.n = tgt
        with nc.Block() as block:
            @block.sync
            def _(sync):
                for s in dma_sems:
                    s.wait(sync)
                sync.sem_inc(PH.h, 1)
                sync.wait_ge(PH.h, tgt)

            @block.tensor
            def _(tensor):
                tensor.sem_inc(PH.h, 1)
                tensor.wait_ge(PH.h, tgt)

            @block.vector
            def _(vector):
                vector.drain()
                vector.sem_inc(PH.h, 1)
                vector.wait_ge(PH.h, tgt)

            @block.scalar
            def _(scalar):
                scalar.drain()
                scalar.sem_inc(PH.h, 1)
                scalar.wait_ge(PH.h, tgt)

            @block.gpsimd
            def _(gpsimd):
                gpsimd.drain()
                gpsimd.sem_inc(PH.h, 1)
                gpsimd.wait_ge(PH.h, tgt)

    def dbg_dump(pairs):
        if not debug or not pairs:
            return
        with nc.Block() as block:
            @block.gpsimd
            def _(gpsimd):
                for dst, src in pairs:
                    DBGS.inc(gpsimd.dma_start(dst, src), 16)
                gpsimd.wait_ge(DBGS.h, DBGS.n)

    c_identb = sb("c_identb", [128, 128], BF16)
    c_identf = sb("c_identf", [128, 128], F32)
    c_triu = sb("c_triu", [128, 128], F32)
    c_onesb = sb("c_onesb", [1, 128], BF16)
    c_dtb = sb("c_dtb", [128, 1], F32)
    c_alog = sb("c_alog", [128, 1], F32)
    c_eps = sb("c_eps", [128, 1], F32)
    stat = sb("stat", [128, 32], F32)
    bsM = ExitStack()
    mb_ = lambda *a, **k: bsM.enter_context(nc.sbuf_tensor(*a, **k))
    biasT_all = mb_("biasT_all", [128, NCH * 128], F32)
    expcA_all = mb_("expcA_all", [128, NCH * 128], F32)
    DT = mb_("DT", [128, T], F32)
    A_dt = mb_("A_dt", [128, T], F32)

    CL = Sm("CL"); SQ = Sm("SQ"); VST = Sm("VST"); RST = Sm("RST")
    XW = Sm("XW"); SSELF = Sm("SSELF"); VSELF = Sm("VSELF")
    LD = Stream(Sm, "LD", 2); XS = Stream(Sm, "XS", 2)
    NT0 = 9
    with ExitStack() as ps:
        psb = lambda *a, **k: ps.enter_context(nc.sbuf_tensor(*a, **k))
        xbuf = [psb(f"a0x{i}", [128, DIM], F32) for i in range(2)]
        xlnb = [psb(f"a0l{i}", [128, DIM], BF16) for i in range(2)]
        dump = psb("a0d", [128, DIM], F32)
        st = [psb(f"a0s{i}", [128, 8], F32) for i in range(2)]

        with nc.Block() as block:
            @block.sync
            def _(sync):
                for dst, src in [(c_identb[:], identb), (c_identf[:], identf),
                                 (c_triu[:], triu01), (c_onesb[:], onesb),
                                 (c_dtb[:], dtbias), (c_alog[:], alog),
                                 (c_eps[:], epsb)]:
                    CL.inc(sync.dma_start(dst, src), 16)
                for i in range(NT0):
                    r0 = i * 128
                    nr = min(128, TH - r0)
                    if i >= 2:
                        sync.wait_ge(XW.h, i - 1)
                        sync.wait_ge(SQ.h, i - 1)
                    LD.load(sync.dma_start(xbuf[i % 2][0:nr, :], xh[r0:r0 + nr, :]), i)

            @block.scalar
            def _(scalar):
                scalar.wait_ge(CL.h, CL.n)
                for i in range(NT0):
                    nr = min(128, TH - i * 128)
                    b = i % 2
                    LD.wait(scalar, i)
                    SQ.inc(scalar.activation(dump[0:nr, :], xbuf[b][0:nr, :], AF.Square,
                                             accum_out=st[b][0:nr, 4:5]))
                    scalar.wait_ge(VST.h, i + 1)
                    sl = SSELF.inc(scalar.activation(st[b][0:nr, 5:6], st[b][0:nr, 3:4],
                                                     AF.Ln, bias=c_eps[0:nr, :]))
                    scalar.wait_ge(SSELF.h, sl)
                    RST.inc(scalar.activation(st[b][0:nr, 6:7], st[b][0:nr, 5:6],
                                              AF.Exp, scale=-0.5))

            @block.vector
            def _(vector):
                for i in range(NT0):
                    nr = min(128, TH - i * 128)
                    b = i % 2
                    LD.wait(vector, i)
                    if i >= 2:
                        vector.wait_ge(RST.h, i - 1)
                    v0 = VSELF.inc(vector.tensor_reduce(st[b][0:nr, 0:1], xbuf[b][0:nr, :],
                                                        AX.X, ALU.add))
                    vector.wait_ge(VSELF.h, v0)
                    v1 = VSELF.inc(vector.tensor_scalar(st[b][0:nr, 1:2], st[b][0:nr, 0:1],
                                                        1.0 / DIM, None, ALU.mult))
                    vector.wait_ge(VSELF.h, v1)
                    v2 = VSELF.inc(vector.tensor_mul(st[b][0:nr, 2:3], st[b][0:nr, 1:2],
                                                     st[b][0:nr, 1:2]))
                    vector.wait_ge(SQ.h, i + 1)
                    vector.wait_ge(VSELF.h, v2)
                    VST.inc(vector.tensor_scalar(st[b][0:nr, 3:4], st[b][0:nr, 4:5],
                                                 1.0 / DIM, st[b][0:nr, 2:3],
                                                 ALU.mult, ALU.subtract))
                    vector.wait_ge(RST.h, i + 1)
                    XS.wait(vector, i - 2)
                    XW.inc(vector.tensor_scalar(xlnb[b][0:nr, :], xbuf[b][0:nr, :],
                                                st[b][0:nr, 1:2], st[b][0:nr, 6:7],
                                                ALU.subtract, ALU.mult))

            @block.gpsimd
            def _(gpsimd):
                for i in range(NT0):
                    nr = min(128, TH - i * 128)
                    gpsimd.wait_ge(XW.h, i + 1)
                    XS.load(gpsimd.dma_start(d_xln.ap()[i * 128:i * 128 + nr, :],
                                             xlnb[i % 2][0:nr, :]), i)
                for b in range(2):
                    gpsimd.wait_ge(XS.s[b].h, XS.s[b].n)
                m = XW.inc(gpsimd.memset(dump[0:125, :], 0.0))
                gpsimd.wait_ge(XW.h, m)
                XS.s[0].inc(gpsimd.dma_start(d_xln.ap()[1027:1152, :], dump[0:125, :]), 16)
                gpsimd.wait_ge(XS.s[0].h, XS.s[0].n)

        barrier(dma_sems=[CL])
        dbg_dump([(dbg["dbg_xln"], d_xln.ap()[0:1027, :])] if debug else [])
    if upto <= 0:
        bsM.close(); es.close()
        return nc, dbg

    XF = Sm("XF"); PJD = Sm("PJD")
    PJE = [Sm("PJE0"), Sm("PJE1")]
    RW = Sm("RW"); CVD = Sm("CVD"); CE = Sm("CE")
    SGA = Sm("SGA"); SGB = Sm("SGB")
    WLs = [Sm("WL0"), Sm("WL1")]
    SGS = Stream(Sm, "SGS", 2); HS = Stream(Sm, "HS", 2)
    A1V = Sm("A1V"); A1S = Sm("A1S")
    with ExitStack() as ps:
        psb = lambda *a, **k: ps.enter_context(nc.sbuf_tensor(*a, **k))
        xfm = psb("xfm", [128, KT * 1152], BF16)
        wbuf = [psb(f"wb{i}", [128, KT * 128], BF16) for i in range(2)]
        cvbuf = [psb(f"cv{i}", [128, 4 * 128], BF16) for i in range(2)]
        rawb = [psb(f"raw{i}", [128, TH], BF16) for i in range(2)]
        hsil = [psb(f"hs{i}", [128, T], BF16) for i in range(2)]
        sgt = [psb(f"sgt{i}", [128, T], F32) for i in range(2)]
        sgo = [psb(f"sgo{i}", [128, T], BF16) for i in range(2)]
        c_convb = psb("c_convb", [128, FT_HBC], F32)
        c_normw = psb("c_normw", [128, FT_GATE], F32)
        zb = psb("zb", [128, T], F32)
        az = psb("az", [128, T], F32)
        statA = psb("statA", [128, 2], F32)
        dbgg = psb("dbgg", [128, 2 * T], F32) if debug else None
        pjp = [ps.enter_context(nc.psum_tensor(f"pj{i}", [128, 1536], F32)) for i in range(2)]
        cvp = ps.enter_context(nc.psum_tensor("cvp", [128, 1024], F32))

        wl_cum = []

        with nc.Block() as block:
            @block.sync
            def _(sync):
                CL.inc(sync.dma_start(c_convb[:], convb), 16)
                CL.inc(sync.dma_start(c_normw[:], normw), 16)
                for k in range(KT):
                    XF.inc(sync.dma_start(xfm[:, k * 1152:(k + 1) * 1152],
                                          d_xln.ap()[0:1152, k * 128:(k + 1) * 128],
                                          transpose=True), 16)
                for ft in range(FT):
                    if ft >= 2:
                        sync.wait_ge(PJD.h, ft - 1)
                        prev_hbc = (ft - 2) - FT_GATE
                        if 0 <= prev_hbc:
                            sync.wait_ge(CVD.h, min(prev_hbc, FT_HBC - 1) + 1)
                    WLs[ft % 2].inc(sync.dma_start(wbuf[ft % 2][:], w1[ft]), 16)
                    if FT_GATE <= ft < FT_GATE + FT_HBC:
                        WLs[ft % 2].inc(sync.dma_start(cvbuf[ft % 2][:], convd[ft - FT_GATE]), 16)

            accp = [0, 0]
            for ft in range(FT):
                accp[ft % 2] += 16
                if FT_GATE <= ft < FT_GATE + FT_HBC:
                    accp[ft % 2] += 16
                wl_cum.append(accp[ft % 2])

            @block.tensor
            def _(tensor):
                tensor.wait_ge(XF.h, XF.n)
                tensor.wait_ge(CL.h, CL.n)
                pje_use = [0, 0]
                for ft in range(FT):
                    b = ft % 2
                    tensor.wait_ge(WLs[b].h, wl_cum[ft])
                    if pje_use[b] > 0:
                        tensor.wait_ge(PJE[b].h, pje_use[b])
                    pje_use[b] += 1
                    is_hbc = FT_GATE <= ft < FT_GATE + FT_HBC
                    cols = ([(0, 512), (512, 1024), (1024, 1027)] if is_hbc
                            else [(3, 515), (515, 1027)])
                    for k in range(KT):
                        lhs = wbuf[b][:, k * 128:(k + 1) * 128]
                        for ci, (c0, c1) in enumerate(cols):
                            o0 = c0 if is_hbc else c0 - 3
                            inst = tensor.matmul(pjp[b][:, o0:o0 + (c1 - c0)], lhs,
                                                 xfm[:, k * 1152 + c0:k * 1152 + c1],
                                                 start=(k == 0), stop=(k == KT - 1))
                    PJD.inc(inst)
                    if is_hbc:
                        hi = ft - FT_GATE
                        tensor.wait_ge(RW.h, hi + 1)
                        if hi > 0:
                            tensor.wait_ge(CE.h, hi)
                        rb = rawb[hi % 2]
                        for k in range(4):
                            dg = cvbuf[b][:, k * 128:(k + 1) * 128]
                            tensor.matmul(cvp[:, 0:512], dg, rb[:, k:k + 512],
                                          start=(k == 0), stop=(k == 3))
                        for k in range(4):
                            dg = cvbuf[b][:, k * 128:(k + 1) * 128]
                            inst = tensor.matmul(cvp[:, 512:1024], dg, rb[:, 512 + k:1024 + k],
                                                 start=(k == 0), stop=(k == 3))
                        CVD.inc(inst)

            @block.vector
            def _(vector):
                gi = 0
                for ft in range(FT):
                    b = ft % 2
                    vector.wait_ge(PJD.h, ft + 1)
                    if ft < FT_GATE:
                        vector.wait_ge(SGA.h, ft + 1)
                        SGS.wait(vector, ft - 2)
                        SGB.inc(vector.tensor_scalar(sgo[b][:], sgt[b][:],
                                                     c_normw[:, ft:ft + 1], None, ALU.mult))
                    elif ft < FT_GATE + FT_HBC:
                        hi = ft - FT_GATE
                        if hi >= 2:
                            vector.wait_ge(CVD.h, hi - 1)
                        n = RW.inc(vector.tensor_copy(rawb[hi % 2][:], pjp[b][:, 0:TH]))
                        vector.wait_ge(RW.h, n)
                        PJE[b].inc(vector.engine_nop())
                    else:
                        v = A1V.inc(vector.tensor_scalar(zb[:], pjp[b][:, 0:T],
                                                         c_dtb[:], None, ALU.add))
                        vector.wait_ge(A1V.h, v)
                        PJE[b].inc(vector.engine_nop())
                        v = A1V.inc(vector.tensor_scalar_max(DT[:], zb[:], 0.0))
                        vector.wait_ge(A1V.h, v)
                        vector.wait_ge(A1S.h, 3)
                        v = A1V.inc(vector.tensor_add(DT[:], DT[:], az[:]))
                        vector.wait_ge(A1V.h, v)
                        vector.wait_ge(A1S.h, 4)
                        A1V.inc(vector.tensor_scalar(A_dt[:], DT[:], statA[:, 0:1], -1.0,
                                                     ALU.mult, ALU.mult))
                        if debug:
                            A1V.inc(vector.tensor_copy(dbgg[:, 0:T], zb[:]))
                            A1V.inc(vector.tensor_copy(dbgg[:, T:2 * T], az[:]))

            @block.scalar
            def _(scalar):
                for ft in range(FT_GATE):
                    b = ft % 2
                    scalar.wait_ge(PJD.h, ft + 1)
                    if ft >= 2:
                        scalar.wait_ge(SGB.h, ft - 1)
                    n = SGA.inc(scalar.activation(sgt[b][:], pjp[b][:, 0:T], AF.Silu))
                    scalar.wait_ge(SGA.h, n)
                    PJE[b].inc(scalar.nop())
                for hi in range(FT_HBC):
                    scalar.wait_ge(CVD.h, hi + 1)
                    HS.wait(scalar, hi - 2)
                    CE.inc(scalar.activation(hsil[hi % 2][:], cvp[:], AF.Silu,
                                             bias=c_convb[:, hi:hi + 1]))
                scalar.wait_ge(A1V.h, 1)
                s1 = A1S.inc(scalar.activation(az[:], zb[:], AF.Abs))
                scalar.wait_ge(A1S.h, s1)
                s1 = A1S.inc(scalar.activation(az[:], az[:], AF.Exp, scale=-1.0))
                scalar.wait_ge(A1S.h, s1)
                A1S.inc(scalar.activation(az[:], az[:], AF.Ln, bias=1.0))
                A1S.inc(scalar.activation(statA[:, 0:1], c_alog[:], AF.Exp))

            @block.gpsimd
            def _(gpsimd):
                for ft in range(FT_GATE):
                    gpsimd.wait_ge(SGB.h, ft + 1)
                    SGS.load(gpsimd.dma_start(d_sg.ap()[ft * 128:(ft + 1) * 128, :],
                                              sgo[ft % 2][:]), ft)
                for hi in range(FT_HBC):
                    gpsimd.wait_ge(CE.h, hi + 1)
                    HS.load(gpsimd.dma_start(d_hbc.ap()[hi * 128:(hi + 1) * 128, :],
                                             hsil[hi % 2][:]), hi)
                for b in range(2):
                    gpsimd.wait_ge(SGS.s[b].h, SGS.s[b].n)
                    gpsimd.wait_ge(HS.s[b].h, HS.s[b].n)

        barrier(dma_sems=[XF, CL])
        dbg_dump([(dbg["dbg_hbc"], d_hbc.ap()[:, :]), (dbg["dbg_sg"], d_sg.ap()[:, :]),
                  (dbg["dbg_dt"], DT[:]),
                  (dbg["dbg_g0"], dbgg[:, 0:T]), (dbg["dbg_s0"], dbgg[:, T:2 * T])]
                 if debug else [])

    if upto <= 1:
        bsM.close(); es.close()
        return nc, dbg

    bsA = ExitStack()
    ab = lambda *a, **k: bsA.enter_context(nc.sbuf_tensor(*a, **k))
    Sb = ab("Sb", [128, INTER], BF16)
    ysb = ab("ysb", [128, INTER], F32)
    xTb = ab("xTb", [128, INTER], BF16)
    bcm = ab("bcm", [128, G * 128], BF16)

    B0V = Sm("B0V"); B0S = Sm("B0S"); B0T = Sm("B0T"); B0D = Sm("B0D"); B0E = Sm("B0E")
    LB = Stream(Sm, "LB", 2)
    bs2 = ExitStack()
    pb2 = lambda *a, **k: bs2.enter_context(nc.sbuf_tensor(*a, **k))
    S = pb2("S", [128, INTER], F32)
    xT2 = pb2("xT2", [128, INTER], BF16)
    xdd = pb2("xdd", [128, INTER], BF16)
    bT = pb2("bT", [128, G * 128], BF16)
    cumA_all = pb2("cumA_all", [128, T], F32)
    ddtT_all = pb2("ddtT_all", [128, T], F32)
    expdA_all = pb2("expdA_all", [128, T], F32)
    dpref = pb2("dpref", [1, (NCH + 1) * 128], F32)
    catb = [pb2(f"catb{i}", [128, 128], F32) for i in range(2)]
    dttb = [pb2(f"dttb{i}", [128, 128], F32) for i in range(2)]
    lastb = [pb2(f"lastb{i}", [128, 128], F32) for i in range(2)]
    lnd = [pb2(f"lnd{i}", [128, 128], F32) for i in range(2)]
    tmb = [pb2(f"tmb{i}", [128, 128], F32) for i in range(2)]
    e2b = [pb2(f"e2b{i}", [128, 128], F32) for i in range(2)]
    pp2 = bs2.enter_context(nc.psum_tensor("pp2", [128, 256], F32))

    scan_i = {}; evA_i = {}; evB_i = {}; biasT_i = {}; tmb_i = {}; ddt_i = {}
    lnd_i = {}; eca_i = {}; eda_i = {}; e2_i = {}; tr_i = {}; lb_i = {}
    dpf_i = {}
    with nc.Block() as block:
        @block.vector
        def _(vector):
            for c in range(NCH):
                scan_i[c] = B0V.inc(vector.tensor_tensor_scan(
                    cumA_all[:, c * 128:(c + 1) * 128], A_dt[:, c * 128:(c + 1) * 128],
                    A_dt[:, c * 128:(c + 1) * 128], 0.0, ALU.add, ALU.bypass))
            for c in range(NCH):
                cb = c % 2
                vector.wait_ge(B0T.h, c + 1)
                if c >= 2:
                    vector.wait_ge(B0S.h, 4 * (c - 2) + 2)
                v = B0E.inc(vector.tensor_copy(catb[cb][:], pp2[:, 0:128]))
                evA_i[c] = v
                vector.wait_ge(B0E.h, v)
                evB_i[c] = B0E.inc(vector.tensor_copy(dttb[cb][:], pp2[:, 128:256]))
                vector.wait_ge(B0E.h, evB_i[c])
                vector.wait_ge(B0S.h, 4 * c + 1)
                biasT_i[c] = B0V.inc(vector.tensor_sub(
                    biasT_all[:, c * 128:(c + 1) * 128], lnd[cb][:], catb[cb][:]))
                LB.wait(vector, c)
                if c >= 2:
                    vector.wait_ge(B0S.h, 4 * (c - 2) + 4)
                tmb_i[c] = B0V.inc(vector.tensor_sub(tmb[cb][:], lastb[cb][:], catb[cb][:]))
                vector.wait_ge(B0V.h, tmb_i[c])
                vector.wait_ge(B0S.h, 4 * c + 4)
                ddt_i[c] = B0V.inc(vector.tensor_mul(
                    ddtT_all[:, c * 128:(c + 1) * 128], e2b[cb][:], dttb[cb][:]))
            v = B0V.inc(vector.memset(dpref[:, 0:128], 1.0))
            for c in range(NCH):
                vector.wait_ge(B0V.h, v)
                vector.wait_ge(B0S.h, 4 * c + 3)
                v = B0V.inc(vector.tensor_mul(dpref[:, (c + 1) * 128:(c + 2) * 128],
                                              dpref[:, c * 128:(c + 1) * 128],
                                              expdA_all[0:1, c * 128:(c + 1) * 128]))
                dpf_i[c] = v

        @block.sync
        def _(sync):
            sync.wait_ge(B0V.h, scan_i[NCH - 1])
            for c in range(NCH):
                B0D.inc(sync.dma_start(d_cum.ap()[c], cumA_all[:, c * 128:(c + 1) * 128]), 16)
            for c in range(NCH):
                B0D.inc(sync.dma_start(d_lastr.ap()[c].unsqueeze(0),
                                       cumA_all[:, c * 128 + 127:c * 128 + 128],
                                       allow_slow_non_contiguous=True), 16)
            for c in range(NCH):
                if c >= 2:
                    sync.wait_ge(B0S.h, 4 * (c - 1) + 3)
                sync.wait_ge(B0D.h, 16 * 2 * NCH)
                lb_i[c] = LB.load(sync.dma_start(
                    lastb[c % 2][:], d_lastr.ap()[c].unsqueeze(0).partition_broadcast(128)), c)
            sync.wait_ge(B0V.h, dpf_i[NCH - 1])
            B0D.inc(sync.dma_start(d_dpref.ap().rearrange("c h -> (c h)").unsqueeze(0),
                                   dpref[:]), 16)
            B0D.inc(sync.dma_start(ag_g_in.ap(), dpref[:, NCH * 128:(NCH + 1) * 128]), 16)

        @block.tensor
        def _(tensor):
            for c in range(NCH):
                tensor.wait_ge(B0V.h, c + 1)
                if c > 0:
                    tensor.wait_ge(B0E.h, evB_i[c - 1])
                tensor.transpose(pp2[:, 0:128], cumA_all[:, c * 128:(c + 1) * 128], c_identf[:])
                tr_i[c] = B0T.inc(tensor.transpose(pp2[:, 128:256],
                                                   DT[:, c * 128:(c + 1) * 128], c_identf[:]))

        @block.scalar
        def _(scalar):
            for c in range(NCH):
                cb = c % 2
                scalar.wait_ge(B0E.h, evB_i[c])
                if c >= 2:
                    scalar.wait_ge(B0V.h, ddt_i[c - 2])
                lnd_i[c] = B0S.inc(scalar.activation(lnd[cb][:], dttb[cb][:], AF.Ln))
                eca_i[c] = B0S.inc(scalar.activation(expcA_all[:, c * 128:(c + 1) * 128],
                                                     catb[cb][:], AF.Exp))
                LB.wait(scalar, c)
                eda_i[c] = B0S.inc(scalar.activation(expdA_all[:, c * 128:(c + 1) * 128],
                                                     lastb[cb][:], AF.Exp))
                scalar.wait_ge(B0V.h, tmb_i[c])
                e2_i[c] = B0S.inc(scalar.activation(e2b[cb][:], tmb[cb][:], AF.Exp))
                assert e2_i[c] == 4 * c + 4 and lnd_i[c] == 4 * c + 1

    barrier(dma_sems=[B0D, LB.s[0], LB.s[1]])
    dbg_dump([(dbg["dbg_cum"], d_cum.ap()[:, :])] if debug else [])
    if upto <= 2:
        bs2.close(); bsA.close(); bsM.close(); es.close()
        return nc, dbg

    B1T = Sm("B1T"); B1V = Sm("B1V"); B1D = Sm("B1D"); EVB = Sm("EVB")
    CCS = Sm("CCS"); SIV = Sm("SIV"); SID = Sm("SID")
    XTS = Stream(Sm, "XTS", 2); BCS = Sm("BCS"); SGL = Stream(Sm, "SGL", 1)
    bs3p = ExitStack()
    p3 = lambda *a, **k: bs3p.enter_context(nc.sbuf_tensor(*a, **k))
    sgb = [p3("sgb0", [128, INTER], BF16)] * 2
    mbx = [p3(f"mbx{i}", [128, 128], F32) for i in range(3)]
    grow = p3("grow", [1, 512], F32)
    rmr = p3("rmr", [1, 384], F32)
    rmcr = p3("rmcr", [1, 384], F32)
    gmod = p3("gmod", [1, 384], F32)
    mrows = p3("mrows", [1, 384], F32)
    pst = bs3p.enter_context(nc.psum_tensor("pst", [128, 2048], F32))
    ptr = bs3p.enter_context(nc.psum_tensor("ptr", [128, 1024], BF16))
    DBV = Sm("DBV")

    bv_sb = lambda c: 10 * c + 2
    bv_xdd = lambda c: 10 * c + 3
    bv_tt2 = lambda c, qq: 10 * c + 8 + qq
    bv_pub = 10 * NCH + 2
    bt_trb = lambda c: 5 * c + 1
    bt_stq = lambda c, qq: 5 * c + 2 + qq
    bx_x = lambda c: 32 * c + 16
    bx_b = lambda c: 32 * c + 32
    sb_i = {}; xdd_i = {}; tt2_i = {}; trb_i = {}; stq_i = {}; evb_i = {}
    xload_i = {}; bload_i = {}; sin_j = {}
    sid_mark = {}
    with nc.Block() as block:
        @block.sync
        def _(sync):
            for c in range(NCH):
                if c > 1:
                    sync.wait_ge(B1V.h, bv_xdd(c - 2))
                if c > 0:
                    sync.wait_ge(B1T.h, bt_trb(c - 1))
                xload_i[c] = XTS.load(sync.dma_start(
                    (xTb if c % 2 == 0 else xT2)[:],
                    d_hbc.ap()[0:INTER, c * 128:(c + 1) * 128], transpose=True), c)
                bload_i[c] = BCS.inc(sync.dma_start(
                    bcm[:].rearrange("p (g t) -> p g t", t=128),
                    d_hbc.ap()[INTER:INTER + 1024, c * 128:(c + 1) * 128]
                    .rearrange("(g n) t -> n g t", n=128)), 16)
            sync.wait_ge(CCS.h, 2)
            SID.inc(sync.dma_start(grow[:], ag_g_out.ap().rearrange("a b -> (a b)").unsqueeze(0)), 16)
            SID.inc(sync.dma_start(rmr[:], rmask), 16)
            sid_mark["gr"] = SID.inc(sync.dma_start(rmcr[:], rmaskc), 16)
            sync.wait_ge(SIV.h, 8)
            n = SID.inc(sync.dma_start(d_m.ap().rearrange("j h -> (j h)").unsqueeze(0),
                                       mrows[:]), 16)
            sync.wait_ge(SID.h, n)
            for j in range(3):
                sid_mark[f"mb{j}"] = SID.inc(sync.dma_start(
                    mbx[j][:], d_m.ap()[j].unsqueeze(0).partition_broadcast(128)), 16)
            for j in range(3):
                if j == 1:
                    sync.wait_ge(SIV.h, 9)
                if j == 2:
                    sync.wait_ge(SIV.h, 11)
                SGL.load(sync.dma_start(
                    sgb[0][:], ag_s_out.ap()[j * 128:(j + 1) * 128, :]), j)

        @block.vector
        def _(vector):
            v0 = B1V.inc(vector.memset(S[:], 0.0))
            for c in range(NCH):
                vector.wait_ge(B1V.h, tt2_i[c - 1][3] if c > 0 else v0)
                if c > 0:
                    vector.wait_ge(B1D.h, 16 * c)
                sb_i[c] = B1V.inc(vector.tensor_copy(Sb[:], S[:]))
                assert sb_i[c] == bv_sb(c), (c, sb_i[c])
                XTS.wait(vector, c)
                xdv = ddtT_all[:, c * 128:(c + 1) * 128].unsqueeze(2).broadcast_to([128, 128, 64])
                if debug and False and c == 1:
                    vector.wait_ge(DBGS.h, DBGS.n + 64)
                xdd_i[c] = B1V.inc(vector.tensor_tensor(
                    xdd[:].rearrange("p (h q) -> p h q", q=64),
                    (xTb if c % 2 == 0 else xT2)[:].rearrange("p (h q) -> p h q", q=64),
                    xdv, ALU.mult))
                tt2_i[c] = {}
                vector.wait_ge(B1V.h, sb_i[c])
                for qq in range(4):
                    dav = expdA_all[:, c * 128 + 32 * qq:c * 128 + 32 * (qq + 1)] \
                        .unsqueeze(2).broadcast_to([128, 32, 64])
                    sl3 = S[:, 2048 * qq:2048 * (qq + 1)].rearrange("p (h q) -> p h q", q=64)
                    B1V.inc(vector.tensor_tensor(sl3, sl3, dav, ALU.mult))
                for qq in range(4):
                    vector.wait_ge(B1T.h, bt_stq(c, qq))
                    vector.wait_ge(B1V.h, 10 * c + 7)
                    tt2_i[c][qq] = B1V.inc(vector.tensor_add(
                        S[:, 2048 * qq:2048 * (qq + 1)], S[:, 2048 * qq:2048 * (qq + 1)], pst[:]))
                    assert tt2_i[c][qq] == bv_tt2(c, qq), (c, qq, tt2_i[c][qq])
            vector.wait_ge(B1V.h, tt2_i[NCH - 1][3])
            vector.wait_ge(B1D.h, 16 * NCH)
            pub_i = B1V.inc(vector.tensor_copy(Sb[:], S[:]))
            assert pub_i == bv_pub
            vector.wait_ge(SID.h, sid_mark["gr"])
            g2 = SIV.inc(vector.tensor_mul(gmod[:, 256:384], grow[:, 256:384], rmr[:, 256:384]))
            vector.wait_ge(SIV.h, g2)
            g2 = SIV.inc(vector.tensor_add(gmod[:, 256:384], gmod[:, 256:384], rmcr[:, 256:384]))
            g1 = SIV.inc(vector.tensor_mul(gmod[:, 128:256], grow[:, 128:256], rmr[:, 128:256]))
            vector.wait_ge(SIV.h, g1)
            g1 = SIV.inc(vector.tensor_add(gmod[:, 128:256], gmod[:, 128:256], rmcr[:, 128:256]))
            vector.wait_ge(SIV.h, g1)
            SIV.inc(vector.tensor_copy(mrows[:, 256:384], rmr[:, 256:384]))
            SIV.inc(vector.tensor_mul(mrows[:, 128:256], gmod[:, 256:384], rmr[:, 128:256]))
            m0_ = SIV.inc(vector.tensor_mul(mrows[:, 0:128], gmod[:, 256:384], gmod[:, 128:256]))
            vector.wait_ge(SIV.h, m0_)
            m0_ = SIV.inc(vector.tensor_mul(mrows[:, 0:128], mrows[:, 0:128], rmr[:, 0:128]))
            assert m0_ == 8
            vector.wait_ge(SID.h, sid_mark["mb2"])
            for j in range(3):
                SGL.wait(vector, j)
                mv = mbx[j][:].unsqueeze(2).broadcast_to([128, 128, 64])
                sgv = sgb[0][:].rearrange("p (h q) -> p h q", q=64)
                if j == 0:
                    sin_j[j] = SIV.inc(vector.tensor_tensor(
                        S[:].rearrange("p (h q) -> p h q", q=64), sgv, mv, ALU.mult))
                    assert sin_j[0] == 9
                else:
                    t_ = SIV.inc(vector.tensor_tensor(
                        ysb[:].rearrange("p (h q) -> p h q", q=64), sgv, mv, ALU.mult))
                    vector.wait_ge(SIV.h, t_)
                    sin_j[j] = SIV.inc(vector.tensor_add(S[:], S[:], ysb[:]))
                vector.wait_ge(SIV.h, sin_j[j])
            vector.wait_ge(B1D.h, 16 * (NCH + 1))
            SIV.inc(vector.tensor_copy(Sb[:], S[:]))

        @block.tensor
        def _(tensor):
            for c in range(NCH):
                tensor.wait_ge(BCS.h, 16 * (c + 1))
                if c > 0:
                    tensor.wait_ge(EVB.h, c)
                for g in range(G):
                    inst = tensor.transpose(ptr[:, g * 128:(g + 1) * 128],
                                            bcm[:, g * 128:(g + 1) * 128], c_identb[:])
                trb_i[c] = B1T.inc(inst)
                assert trb_i[c] == bt_trb(c)
                tensor.wait_ge(EVB.h, c + 1)
                for qq in range(4):
                    tensor.wait_ge(B1V.h, bv_xdd(c))
                    prev = bv_tt2(c - 1, 3) if (qq == 0 and c > 0) else \
                        (bv_tt2(c, qq - 1) if qq > 0 else 0)
                    if prev:
                        tensor.wait_ge(B1V.h, prev)
                    for gg in range(2):
                        g = 2 * qq + gg
                        for jj in range(2):
                            inst = tensor.matmul(
                                pst[:, gg * 1024 + jj * 512:gg * 1024 + (jj + 1) * 512],
                                bT[:, g * 128:(g + 1) * 128],
                                xdd[:, g * 1024 + jj * 512:g * 1024 + (jj + 1) * 512],
                                start=True, stop=True)
                    stq_i[(c, qq)] = B1T.inc(inst)
                    assert stq_i[(c, qq)] == bt_stq(c, qq)

        @block.scalar
        def _(scalar):
            for c in range(NCH):
                scalar.wait_ge(B1T.h, trb_i[c])
                if c > 0:
                    scalar.wait_ge(B1T.h, stq_i[(c - 1, 3)])
                evb_i[c] = EVB.inc(scalar.copy(bT[:], ptr[:]))

        @block.gpsimd
        def _(gpsimd):
            pass
            for c in range(NCH):
                gpsimd.wait_ge(B1V.h, bv_sb(c))
                B1D.inc(gpsimd.dma_start(d_spref.ap()[c], Sb[:]), 16)
            gpsimd.wait_ge(B1V.h, bv_pub)
            B1D.inc(gpsimd.dma_start(ag_s_in.ap(), Sb[:]), 16)
            gpsimd.wait_ge(B1D.h, B1D.n)
            gpsimd.wait_ge(B0D.h, B0D.n)
            CCS.inc(gpsimd.collective_compute(
                "AllGather", ALU.bypass, replica_groups=[[0, 1, 2, 3], [4, 5, 6, 7]],
                ins=[ag_s_in.ap().opt()], outs=[ag_s_out.ap().opt()]))
            CCS.inc(gpsimd.collective_compute(
                "AllGather", ALU.bypass, replica_groups=[[0, 1, 2, 3], [4, 5, 6, 7]],
                ins=[ag_g_in.ap().opt()], outs=[ag_g_out.ap().opt()]))
            gpsimd.wait_ge(CCS.h, 2)

    barrier(dma_sems=[XTS.s[0], XTS.s[1], BCS, B1D, SID])
    dbg_dump([(dbg["dbg_spref"], d_spref.ap()[:, :, :]), (dbg["dbg_spub"], ag_s_in.ap()),
              (dbg["dbg_sin"], Sb[:])] if debug else [])
    bs3p.close()
    if upto <= 3:
        bs2.close(); bsA.close(); bsM.close(); es.close()
        return nc, dbg

    bs2.close()

    B2T = Sm("B2T"); B2V = Sm("B2V"); B2S = Sm("B2S"); B2D = Sm("B2D")
    X4 = Sm("X4"); BC4 = Sm("BC4"); SP4 = Sm("SP4"); DB4 = Sm("DB4")
    DP4 = Stream(Sm, "DP4", 1); RR4 = Stream(Sm, "RR4", 2)
    bs4 = ExitStack()
    p4 = lambda *a, **k: bs4.enter_context(nc.sbuf_tensor(*a, **k))
    ccm = p4("ccm", [128, G * 128], BF16)
    rr = [p4(f"rr{i}", [128, 2048], F32) for i in range(2)]
    spc = p4("spc", [128, INTER], BF16)
    suse = p4("suse", [128, INTER], BF16)
    xdc = p4("xdc", [128, INTER], BF16)
    yrb = p4("yrb", [128, INTER], BF16)
    dbc = p4("dbc", [128, INTER], BF16)
    gmt = p4("gmt", [128, G * 128], F32)
    dcb = [p4(f"dcb{i}", [128, 512], F32) for i in range(4)]
    mbb = [p4(f"mbb{i}", [128, 512], BF16) for i in range(4)]
    ucb = [p4(f"ucb{i}", [128, 512], F32) for i in range(2)]
    dpb = [p4("dpb0", [128, 128], F32)] * 2
    pgm = bs4.enter_context(nc.psum_tensor("pgm", [128, 1024], F32))
    pyd = [bs4.enter_context(nc.psum_tensor(f"pyd{i}", [128, 512], F32)) for i in range(2)]
    pyo = [bs4.enter_context(nc.psum_tensor(f"pyo{i}", [128, 512], F32)) for i in range(2)]

    v_tu = lambda c: 69 * c + 1
    v_su = lambda c: 69 * c + 2
    v_xdc = lambda c: 69 * c + 3
    v_gmt = lambda c: 69 * c + 4
    v_mp = lambda c, a: 69 * c + (5 + a if a < 2 else 4 * (a // 2) + 3 + (a % 2))
    v_tt1 = lambda c, b: 69 * c + (4 * b + 9 if b < 15 else 67)
    v_tt2 = lambda c, b: 69 * c + (4 * b + 10 if b < 15 else 68)
    v_yr = lambda c: 69 * c + 69
    t_gm = lambda c: 17 * c + 1
    t_bank = lambda c, b: 17 * c + 2 + b
    s_pack = lambda c, a: 35 * c + a + 1
    s_sq = lambda c: 35 * c + 33
    s_rstd = lambda c: 35 * c + 35
    xm = {}

    with nc.Block() as block:
        @block.sync
        def _(sync):
            DB4.inc(sync.dma_start(dbc[:], dexp.unsqueeze(0).partition_broadcast(128)), 16)
            for c in range(b2n):
                if c > 0:
                    sync.wait_ge(B2T.h, t_bank(c - 1, 15))
                    sync.wait_ge(B2V.h, v_xdc(c - 1))
                xm[(c, "xT")] = X4.inc(sync.dma_start(
                    xTb[:], d_hbc.ap()[0:INTER, c * 128:(c + 1) * 128], transpose=True), 16)
                if c > 0:
                    sync.wait_ge(B2T.h, t_gm(c - 1))
                xm[(c, "bc")] = BC4.inc(sync.dma_start(
                    bcm[:].rearrange("p (g t) -> p g t", t=128),
                    d_hbc.ap()[INTER:INTER + 1024, c * 128:(c + 1) * 128]
                    .rearrange("(g n) t -> n g t", n=128)), 16)
                xm[(c, "cc")] = BC4.inc(sync.dma_start(
                    ccm[:].rearrange("p (g t) -> p g t", t=128),
                    d_hbc.ap()[INTER + 1024:INTER + 2048, c * 128:(c + 1) * 128]
                    .rearrange("(g n) t -> n g t", n=128)), 16)
                if c > 0:
                    sync.wait_ge(B2V.h, v_su(c - 1))
                xm[(c, "sp")] = SP4.inc(sync.dma_start(spc[:], d_spref.ap()[c]), 16)
                if c > 0:
                    sync.wait_ge(B2V.h, v_tu(c - 1))
                DP4.load(sync.dma_start(
                    dpb[c % 2][:], d_dpref.ap()[c].unsqueeze(0).partition_broadcast(128)), c)
                for q in range(8):
                    if c > 0 or q >= 2:
                        pc, pq = (c, q - 2) if q >= 2 else (c - 1, q + 6)
                        sync.wait_ge(B2S.h, s_pack(pc, 4 * pq + 3))
                    RR4.load(sync.dma_start(
                        rr[q % 2][:],
                        d_cum.ap()[c][16 * 128 * q:16 * 128 * (q + 1)]
                        .unsqueeze(0).partition_broadcast(128)), 8 * c + q)

        @block.scalar
        def _(scalar):
            def rms(c):
                scalar.wait_ge(B2V.h, v_tt2(c, 15))
                scalar.wait_ge(B2T.h, t_bank(c, 15))
                n = B2S.inc(scalar.activation(xdc[:], ysb[:], AF.Square,
                                              accum_out=stat[:, 8:9]))
                assert n == s_sq(c), (c, n)
                scalar.wait_ge(B2S.h, n)
                n = B2S.inc(scalar.activation(stat[:, 9:10], stat[:, 8:9], AF.Ln,
                                              bias=c_eps[:], scale=1.0 / INTER))
                scalar.wait_ge(B2S.h, n)
                n = B2S.inc(scalar.activation(stat[:, 10:11], stat[:, 9:10], AF.Exp, scale=-0.5))
                assert n == s_rstd(c), (c, n)

            for c in range(b2n):
                if c >= 1:
                    rms(c - 1)
                for a in range(32):
                    q = a // 4
                    RR4.wait(scalar, 8 * c + q)
                    if c > 0 or a >= 4:
                        pc, pa = (c, a - 4) if a >= 4 else (c - 1, a + 28)
                        scalar.wait_ge(B2V.h, v_mp(pc, pa))
                    for j in range(4):
                        h = 4 * a + j
                        inst = scalar.activation(
                            dcb[a % 4][:, j * 128:(j + 1) * 128],
                            rr[q % 2][:, (h % 16) * 128:(h % 16 + 1) * 128],
                            AF.Exp, bias=biasT_all[:, c * 128 + h:c * 128 + h + 1])
                    n = B2S.inc(inst)
                    assert n == s_pack(c, a)
                pass

            if b2n > 0:
                rms(b2n - 1)

        @block.vector
        def _(vector):
            for c in range(b2n):
                DP4.wait(vector, c)
                if c > 0:
                    vector.wait_ge(B2V.h, v_yr(c - 1))
                    vector.wait_ge(B2D.h, 16 * (c))
                dpv = dpb[c % 2][:].unsqueeze(2).broadcast_to([128, 128, 64])
                n = B2V.inc(vector.tensor_tensor(
                    ysb[:].rearrange("p (h q) -> p h q", q=64),
                    Sb[:].rearrange("p (h q) -> p h q", q=64), dpv, ALU.mult))
                assert n == v_tu(c)
                vector.wait_ge(B2V.h, n)
                vector.wait_ge(SP4.h, 16 * (c + 1))
                if c > 0:
                    vector.wait_ge(B2T.h, t_bank(c - 1, 15))
                n = B2V.inc(vector.tensor_add(suse[:], ysb[:], spc[:]))
                assert n == v_su(c)
                vector.wait_ge(X4.h, 16 * (c + 1))
                vector.wait_ge(DB4.h, 16)
                if c > 0:
                    vector.wait_ge(B2S.h, s_sq(c - 1))
                n = B2V.inc(vector.tensor_mul(xdc[:], xTb[:], dbc[:]))
                assert n == v_xdc(c)
                vector.wait_ge(B2T.h, t_gm(c))
                tv = c_triu[:].unsqueeze(1).broadcast_to([128, 8, 128])
                n = B2V.inc(vector.tensor_tensor(
                    gmt[:].rearrange("p (g l) -> p g l", l=128),
                    pgm[:].rearrange("p (g l) -> p g l", l=128), tv, ALU.mult))
                assert n == v_gmt(c)
                vector.wait_ge(B2V.h, n)
                def emit_pack(a):
                    g = a // 4
                    vector.wait_ge(B2S.h, s_pack(c, a))
                    if c > 0 or a >= 4:
                        pc, pa = (c, a - 4) if a >= 4 else (c - 1, a + 28)
                        vector.wait_ge(B2T.h, t_bank(pc, pa // 2))
                    gv = gmt[:, g * 128:(g + 1) * 128].unsqueeze(1).broadcast_to([128, 4, 128])
                    n = B2V.inc(vector.scalar_tensor_tensor(
                        mbb[a % 4][:].rearrange("p (j l) -> p j l", l=128),
                        dcb[a % 4][:].rearrange("p (j l) -> p j l", l=128),
                        CLAMP, gv, ALU.min, ALU.mult))
                    assert n == v_mp(c, a), (c, a, n)

                def emit_evac(b):
                    vector.wait_ge(B2T.h, t_bank(c, b))
                    if c > 0 or b >= 2:
                        pc, pb = (c, b - 2) if b >= 2 else (c - 1, b + 14)
                        vector.wait_ge(B2V.h, v_tt2(pc, pb))
                    sdv = expcA_all[:, c * 128 + 8 * b:c * 128 + 8 * b + 8] \
                        .unsqueeze(2).broadcast_to([128, 8, 64])
                    n = B2V.inc(vector.tensor_tensor(
                        ucb[b % 2][:].rearrange("p (j q) -> p j q", q=64),
                        pyo[b % 2][:].rearrange("p (j q) -> p j q", q=64), sdv, ALU.mult))
                    assert n == v_tt1(c, b), (c, b, n)
                    vector.wait_ge(B2V.h, n)
                    n = B2V.inc(vector.tensor_add(ysb[:, 512 * b:512 * (b + 1)],
                                                  ucb[b % 2][:], pyd[b % 2][:]))
                    assert n == v_tt2(c, b), (c, b, n)

                for b in range(16):
                    emit_pack(2 * b)
                    emit_pack(2 * b + 1)
                    if b >= 1:
                        emit_evac(b - 1)
                emit_evac(15)
                vector.wait_ge(B2S.h, s_rstd(c))
                if c > 0:
                    vector.wait_ge(B2D.h, 16 * c)
                n = B2V.inc(vector.tensor_scalar(yrb[:], ysb[:], stat[:, 10:11], None, ALU.mult))
                assert n == v_yr(c)

        @block.tensor
        def _(tensor):
            for c in range(b2n):
                tensor.wait_ge(BC4.h, 32 * (c + 1))
                if c > 0:
                    tensor.wait_ge(B2V.h, v_gmt(c - 1))
                for g in range(G):
                    inst = tensor.matmul(pgm[:, g * 128:(g + 1) * 128],
                                         bcm[:, g * 128:(g + 1) * 128],
                                         ccm[:, g * 128:(g + 1) * 128],
                                         start=True, stop=True)
                n = B2T.inc(inst)
                assert n == t_gm(c)
                for b in range(16):
                    g = b // 2
                    tensor.wait_ge(B2V.h, v_mp(c, 2 * b + 1))
                    if c > 0 or b >= 2:
                        pc, pb = (c, b - 2) if b >= 2 else (c - 1, b + 14)
                        tensor.wait_ge(B2V.h, v_tt2(pc, pb))
                    for hh in range(8):
                        h = 8 * b + hh
                        a = h // 4
                        tensor.matmul(pyd[b % 2][:, hh * 64:(hh + 1) * 64],
                                      mbb[a % 4][:, (h % 4) * 128:(h % 4 + 1) * 128],
                                      xTb[:, h * 64:(h + 1) * 64],
                                      start=(hh == 0), stop=False)
                    tensor.matmul(pyd[b % 2][:, 0:512], c_identb[:],
                                  xdc[:, 512 * b:512 * (b + 1)], start=False, stop=True)
                    inst = tensor.matmul(pyo[b % 2][:, 0:512], ccm[:, g * 128:(g + 1) * 128],
                                         suse[:, 512 * b:512 * (b + 1)], start=True, stop=True)
                    n = B2T.inc(inst)
                    assert n == t_bank(c, b)

        @block.gpsimd
        def _(gpsimd):
            for c in range(b2n):
                gpsimd.wait_ge(B2V.h, v_yr(c))
                B2D.inc(gpsimd.dma_start(d_yrms.ap()[c * 128:(c + 1) * 128, :], yrb[:]), 16)
            gpsimd.wait_ge(B2D.h, B2D.n)

    barrier(dma_sems=[X4, BC4, SP4, DB4, B2D])
    dbg_dump([(dbg["dbg_yrms"], d_yrms.ap()[:, :])] if debug else [])
    bs4.close()
    if upto <= 4:
        bsA.close(); bsM.close(); es.close()
        return nc, dbg

    bsA.close(); bsM.close()

    YGV = Sm("YGV"); OPK = Sm("OPK"); EOP = Sm("EOP")
    YTR = Stream(Sm, "YTR", 2); SGD = Stream(Sm, "SGD", 2)
    OWS = Stream(Sm, "OWS", 2); AOS = Stream(Sm, "AOS", 2)
    bs5 = ExitStack()
    p5 = lambda *a, **k: bs5.enter_context(nc.sbuf_tensor(*a, **k))
    YG = p5("YG", [128, 64 * 1024], BF16)
    ytr = [p5(f"ytr{i}", [128, 1024], BF16) for i in range(2)]
    sgl = [p5(f"sgl{i}", [128, 1024], BF16) for i in range(2)]
    ow = [p5(f"ow{i}", [128, 16 * 512], BF16) for i in range(2)]
    ao = [p5(f"ao{i}", [128, 512], BF16) for i in range(2)]
    pso = bs5.enter_context(nc.psum_tensor("pso", [128, 512 * 8], F32))

    with nc.Block() as block:
        @block.sync
        def _(sync):
            for j in range(64):
                if j >= 2:
                    sync.wait_ge(YGV.h, j - 1)
                YTR.load(sync.dma_start(ytr[j % 2][:],
                                        d_yrms.ap()[0:1024, j * 128:(j + 1) * 128],
                                        transpose=True), j)
                SGD.load(sync.dma_start(sgl[j % 2][:],
                                        d_sg.ap()[j * 128:(j + 1) * 128, :]), j)
            for dj in range(8):
                for kg in range(4):
                    i = 4 * dj + kg
                    if i >= 2:
                        sync.wait_ge(OPK.h, i - 1)
                    OWS.load(sync.dma_start(ow[i % 2][:], opw[dj, kg]), i)

        @block.vector
        def _(vector):
            for j in range(64):
                YTR.wait(vector, j)
                SGD.wait(vector, j)
                YGV.inc(vector.tensor_mul(YG[:, j * 1024:(j + 1) * 1024],
                                          ytr[j % 2][:], sgl[j % 2][:]))

        @block.tensor
        def _(tensor):
            for dj in range(8):
                if dj > 0:
                    tensor.wait_ge(EOP.h, 8 * dj)
                for kg in range(4):
                    i = 4 * dj + kg
                    OWS.wait(tensor, i)
                    for kt in range(16):
                        k = 16 * kg + kt
                        tensor.wait_ge(YGV.h, k + 1)
                        for t in range(8):
                            inst = tensor.matmul(
                                pso[:, t * 512:(t + 1) * 512],
                                YG[:, k * 1024 + t * 128:k * 1024 + (t + 1) * 128],
                                ow[i % 2][:, kt * 512:(kt + 1) * 512],
                                start=(k == 0), stop=(k == 63))
                    OPK.inc(inst)

        @block.scalar
        def _(scalar):
            for dj in range(8):
                scalar.wait_ge(OPK.h, 4 * dj + 4)
                for t in range(8):
                    i = 8 * dj + t
                    AOS.wait(scalar, i - 2)
                    EOP.inc(scalar.activation(ao[i % 2][:], pso[:, t * 512:(t + 1) * 512],
                                              AF.Copy))

        @block.gpsimd
        def _(gpsimd):
            for dj in range(8):
                for t in range(8):
                    i = 8 * dj + t
                    gpsimd.wait_ge(EOP.h, i + 1)
                    AOS.load(gpsimd.dma_start(
                        d_attn.ap()[t * 128:(t + 1) * 128, dj * 512:(dj + 1) * 512],
                        ao[i % 2][:]), i)
            for b in range(2):
                gpsimd.wait_ge(AOS.s[b].h, AOS.s[b].n)

    barrier(dma_sems=[])
    dbg_dump([(dbg["dbg_attn"], d_attn.ap()[:, :])] if debug else [])
    bs5.close()
    if upto <= 5:
        es.close()
        return nc, dbg

    DS = Sm("DS"); DV = Sm("DV")
    ATL = Stream(Sm, "ATL", 2); XBL = Stream(Sm, "XBL", 2)
    XNS = Stream(Sm, "XNS", 2); XL2S = Stream(Sm, "XL2S", 2)
    bs6 = ExitStack()
    p6 = lambda *a, **k: bs6.enter_context(nc.sbuf_tensor(*a, **k))
    atb = [p6(f"atb{i}", [128, DIM], BF16) for i in range(2)]
    xbb = [p6(f"xbb{i}", [128, DIM], F32) for i in range(2)]
    xnb = [p6(f"xnb{i}", [128, DIM], F32) for i in range(2)]
    xl2b = [p6(f"xl2b{i}", [128, DIM], BF16) for i in range(2)]
    tb6 = p6("tb6", [128, DIM], F32)
    dump6 = p6("dump6", [128, DIM], F32)

    with nc.Block() as block:
        @block.sync
        def _(sync):
            for i in range(8):
                if i >= 2:
                    sync.wait_ge(DV.h, 11 * (i - 1))
                    sync.wait_ge(DS.h, 6 * (i - 1))
                ATL.load(sync.dma_start(atb[i % 2][:], d_attn.ap()[i * 128:(i + 1) * 128, :]), i)
                XBL.load(sync.dma_start(xbb[i % 2][:], xh[3 + i * 128:3 + (i + 1) * 128, :]), i)

        @block.scalar
        def _(scalar):
            for i in range(8):
                b = i % 2
                o = 16 * (i % 2)
                ATL.wait(scalar, i)
                n = DS.inc(scalar.activation(dump6[:], atb[b][:], AF.Square,
                                             accum_out=stat[:, o + 0:o + 1]))
                scalar.wait_ge(DV.h, 11 * i + 4)
                n = DS.inc(scalar.activation(stat[:, o + 11:o + 12], stat[:, o + 3:o + 4], AF.Ln, bias=c_eps[:]))
                scalar.wait_ge(DS.h, n)
                DS.inc(scalar.activation(stat[:, o + 4:o + 5], stat[:, o + 11:o + 12], AF.Exp, scale=-0.5))
                scalar.wait_ge(DV.h, 11 * i + 6)
                n = DS.inc(scalar.activation(dump6[:], xnb[b][:], AF.Square,
                                             accum_out=stat[:, o + 5:o + 6]))
                scalar.wait_ge(DV.h, 11 * i + 10)
                n = DS.inc(scalar.activation(stat[:, o + 12:o + 13], stat[:, o + 7:o + 8], AF.Ln, bias=c_eps[:]))
                scalar.wait_ge(DS.h, n)
                DS.inc(scalar.activation(stat[:, o + 8:o + 9], stat[:, o + 12:o + 13], AF.Exp, scale=-0.5))

        @block.vector
        def _(vector):
            for i in range(8):
                b = i % 2
                o = 16 * (i % 2)
                ATL.wait(vector, i)
                if i >= 2:
                    vector.wait_ge(DS.h, 6 * (i - 2) + 6)
                n = DV.inc(vector.tensor_reduce(stat[:, o + 1:o + 2], atb[b][:], AX.X, ALU.add))
                vector.wait_ge(DV.h, n)
                n = DV.inc(vector.tensor_scalar(stat[:, o + 2:o + 3], stat[:, o + 1:o + 2], 1.0 / DIM, None, ALU.mult))
                vector.wait_ge(DV.h, n)
                n = DV.inc(vector.tensor_mul(stat[:, o + 13:o + 14], stat[:, o + 2:o + 3], stat[:, o + 2:o + 3]))
                vector.wait_ge(DV.h, n)
                vector.wait_ge(DS.h, 6 * i + 1)
                n = DV.inc(vector.tensor_scalar(stat[:, o + 3:o + 4], stat[:, o + 0:o + 1], 1.0 / DIM,
                                                stat[:, o + 13:o + 14], ALU.mult, ALU.subtract))
                assert n == 11 * i + 4
                vector.wait_ge(DS.h, 6 * i + 3)
                n = DV.inc(vector.tensor_scalar(tb6[:], atb[b][:], stat[:, o + 2:o + 3],
                                                stat[:, o + 4:o + 5], ALU.subtract, ALU.mult))
                XBL.wait(vector, i)
                vector.wait_ge(DV.h, n)
                XNS.wait(vector, i - 2)
                n = DV.inc(vector.tensor_add(xnb[b][:], tb6[:], xbb[b][:]))
                assert n == 11 * i + 6
                vector.wait_ge(DV.h, n)
                n = DV.inc(vector.tensor_reduce(stat[:, o + 6:o + 7], xnb[b][:], AX.X, ALU.add))
                vector.wait_ge(DV.h, n)
                n = DV.inc(vector.tensor_scalar(stat[:, o + 14:o + 15], stat[:, o + 6:o + 7], 1.0 / DIM, None, ALU.mult))
                vector.wait_ge(DV.h, n)
                n = DV.inc(vector.tensor_mul(stat[:, o + 15:o + 16], stat[:, o + 14:o + 15], stat[:, o + 14:o + 15]))
                vector.wait_ge(DV.h, n)
                vector.wait_ge(DS.h, 6 * i + 4)
                n = DV.inc(vector.tensor_scalar(stat[:, o + 7:o + 8], stat[:, o + 5:o + 6], 1.0 / DIM,
                                                stat[:, o + 15:o + 16], ALU.mult, ALU.subtract))
                assert n == 11 * i + 10
                vector.wait_ge(DS.h, 6 * i + 6)
                XL2S.wait(vector, i - 2)
                vector.wait_ge(DV.h, n)
                n = DV.inc(vector.tensor_scalar(xl2b[b][:], xnb[b][:], stat[:, o + 14:o + 15],
                                                stat[:, o + 8:o + 9], ALU.subtract, ALU.mult))
                assert n == 11 * i + 11

        @block.gpsimd
        def _(gpsimd):
            for i in range(8):
                gpsimd.wait_ge(DV.h, 11 * i + 6)
                XNS.load(gpsimd.dma_start(d_xnew.ap()[i * 128:(i + 1) * 128, :],
                                          xnb[i % 2][:]), i)
                gpsimd.wait_ge(DV.h, 11 * i + 11)
                XL2S.load(gpsimd.dma_start(d_xln2.ap()[i * 128:(i + 1) * 128, :],
                                           xl2b[i % 2][:]), i)
            for st_ in (XNS, XL2S):
                for b in range(2):
                    gpsimd.wait_ge(st_.s[b].h, st_.s[b].n)

    barrier(dma_sems=[])
    dbg_dump([(dbg["dbg_xnew"], d_xnew.ap()[:, :]), (dbg["dbg_xln2"], d_xln2.ap()[:, :])]
             if debug else [])
    bs6.close()
    if upto <= 6:
        es.close()
        return nc, dbg

    EXF = Sm("EXF"); EC = EXF; ERT = Sm("ERT"); ERV = Sm("ERV"); ERS = Sm("ERS")
    RTT = Sm("RTT"); RTE = Sm("RTE"); EPK = Sm("EPK"); EFE = Sm("EFE")
    ES = Sm("ES"); EV = Sm("EV"); OUTS = Sm("OUTS")
    BWS = Stream(Sm, "BWS", 2); BTS = Stream(Sm, "BTS", 2); XNL = Stream(Sm, "XNL", 1)
    FFS = Stream(Sm, "FFS", 2); FFL = Stream(Sm, "FFL", 2)
    bs7 = ExitStack()
    p7 = lambda *a, **k: bs7.enter_context(nc.sbuf_tensor(*a, **k))
    x2fm = p7("x2fm", [128, KT * 1024], BF16)
    bbr = p7("bbr", [1, DIM], BF16)
    RH = p7("RH", [128, 8 * 64], F32)
    PT = p7("PT", [128, 8 * 16], F32)
    CMB = p7("CMB", [128, 8 * 16], F32)
    RSC = p7("RSC", [128, 64], BF16)
    RT = p7("RT", [64, 128 * 128], BF16)

    bs7p = ExitStack()
    arr = bs7p.enter_context(nc.sbuf_tensor("arr", [128, KT * 80], BF16))
    dump7a = bs7p.enter_context(nc.sbuf_tensor("dump7a", [128, 64], F32))
    psr = bs7p.enter_context(nc.psum_tensor("psr", [128, 4 * 512], F32))
    ptb = bs7p.enter_context(nc.psum_tensor("ptb", [64, 2048], BF16))
    with nc.Block() as block:
        @block.sync
        def _(sync):
            for k in range(KT):
                EXF.inc(sync.dma_start(x2fm[:, k * 1024:(k + 1) * 1024],
                                       d_xln2.ap()[0:1024, k * 128:(k + 1) * 128],
                                       transpose=True), 16)
            EC.inc(sync.dma_start(arr[:], arw), 16)
            EC.inc(sync.dma_start(bbr[:], bbias), 16)

        @block.tensor
        def _(tensor):
            tensor.wait_ge(EXF.h, EXF.n)
            for t in range(8):
                if t >= 4:
                    tensor.wait_ge(ERS.h, 2 * (t - 4) + 2)
                for k in range(KT):
                    inst = tensor.matmul(psr[:, (t % 4) * 512:(t % 4) * 512 + 80],
                                         x2fm[:, k * 1024 + t * 128:k * 1024 + (t + 1) * 128],
                                         arr[:, k * 80:(k + 1) * 80],
                                         start=(k == 0), stop=(k == KT - 1))
                ERT.inc(inst)
            for t in range(8):
                for e in range(16):
                    i = 16 * t + e
                    tensor.wait_ge(ERV.h, 104 + i + 1)
                    if i >= 16:
                        tensor.wait_ge(RTE.h, i // 8 - 1)
                    inst = tensor.transpose(ptb[:, (i % 16) * 128:(i % 16 + 1) * 128],
                                            RSC[:], c_identb[:])
                    RTT.inc(inst)

        @block.vector
        def _(vector):
            def part1(t):
                vector.wait_ge(ERT.h, t + 1)
                if t > 0:
                    vector.wait_ge(ERS.h, 2 * (t - 1) + 2)
                n = ERV.inc(vector.tensor_copy(RH[:, t * 64:(t + 1) * 64],
                                               psr[:, (t % 4) * 512:(t % 4) * 512 + 64]))
                vector.wait_ge(ERV.h, n)
                n = ERV.inc(vector.reduce_max(stat[:, 0:1],
                                              psr[:, (t % 4) * 512 + 64:(t % 4) * 512 + 80], AX.X))
                assert n == 13 * t + 2, (t, n)

            def top2(t):
                vector.wait_ge(ERS.h, 2 * t + 2)
                pt = PT[:, t * 16:(t + 1) * 16]
                n = ERV.inc(vector.reduce_max(stat[:, 2:3], pt, AX.X))
                vector.wait_ge(ERV.h, n)
                n = ERV.inc(vector.tensor_scalar(dump7a[:, 0:16], pt, stat[:, 2:3], None, ALU.is_lt))
                vector.wait_ge(ERV.h, n)
                n = ERV.inc(vector.tensor_mul(dump7a[:, 16:32], pt, dump7a[:, 0:16]))
                vector.wait_ge(ERV.h, n)
                n = ERV.inc(vector.reduce_max(stat[:, 3:4], dump7a[:, 16:32], AX.X))
                vector.wait_ge(ERV.h, n)
                n = ERV.inc(vector.tensor_add(stat[:, 4:5], stat[:, 2:3], stat[:, 3:4]))
                vector.wait_ge(ERV.h, n)
                n = ERV.inc(vector.tensor_scalar(stat[:, 5:6], stat[:, 1:2], 1e-6,
                                                 stat[:, 4:5], ALU.mult, ALU.add))
                assert n == 13 * t + 8, (t, n)
                vector.wait_ge(ERV.h, n)
                n = ERV.inc(vector.tensor_scalar(stat[:, 5:6], stat[:, 5:6], 2.0, None, ALU.mult))
                vector.wait_ge(ERV.h, n)
                n = ERV.inc(vector.reciprocal(stat[:, 6:7], stat[:, 5:6]))
                vector.wait_ge(ERV.h, n)
                n = ERV.inc(vector.tensor_scalar(dump7a[:, 32:48], pt, stat[:, 3:4], None, ALU.is_ge))
                vector.wait_ge(ERV.h, n)
                n = ERV.inc(vector.tensor_mul(dump7a[:, 48:64], pt, dump7a[:, 32:48]))
                vector.wait_ge(ERV.h, n)
                n = ERV.inc(vector.tensor_scalar(CMB[:, t * 16:(t + 1) * 16], dump7a[:, 48:64],
                                                 stat[:, 6:7], None, ALU.mult))
                assert n == 13 * t + 13, (t, n)

            part1(0)
            for t in range(1, 8):
                top2(t - 1)
                part1(t)
            top2(7)
            def rt_evac(p):
                vector.wait_ge(RTT.h, 8 * (p + 1))
                RTE.inc(vector.tensor_copy(
                    RT[:, p * 1024:(p + 1) * 1024],
                    ptb[:, (p % 2) * 1024:(p % 2 + 1) * 1024]))

            for t in range(8):
                for e in range(16):
                    i = 16 * t + e
                    if i >= 8 and i % 8 == 0:
                        rt_evac(i // 8 - 1)
                    if i > 0:
                        vector.wait_ge(RTT.h, i)
                    n = ERV.inc(vector.tensor_scalar(RSC[:], RH[:, t * 64:(t + 1) * 64],
                                                     CMB[:, t * 16 + e:t * 16 + e + 1],
                                                     None, ALU.mult))
                    assert n == 104 + i + 1, (i, n)
            rt_evac(15)

        @block.scalar
        def _(scalar):
            for t in range(8):
                scalar.wait_ge(ERV.h, 13 * t + 2)
                if t > 0:
                    scalar.wait_ge(ERV.h, 13 * (t - 1) + 8)
                n = ERS.inc(scalar.activation(stat[:, 7:8], stat[:, 0:1], AF.Copy, scale=-1.0))
                scalar.wait_ge(ERS.h, n)
                n = ERS.inc(scalar.activation(PT[:, t * 16:(t + 1) * 16],
                                              psr[:, (t % 4) * 512 + 64:(t % 4) * 512 + 80],
                                              AF.Exp, bias=stat[:, 7:8],
                                              accum_out=stat[:, 1:2]))
                assert n == 2 * t + 2

        @block.gpsimd
        def _(gpsimd):
            pass

    bs7p.close()

    bs7q = ExitStack()
    q7 = lambda *a, **k: bs7q.enter_context(nc.sbuf_tensor(*a, **k))
    bwb = [q7(f"bwb{i}", [128, 8 * 512], BF16) for i in range(2)]
    btb = [q7(f"btb{i}", [64, 8 * 512], BF16) for i in range(2)]
    xnl = [q7("xnl0", [128, DIM], F32)] * 2
    ob = [q7("ob0", [128, DIM], F32)] * 2
    fs = [q7(f"fs{i}", [128, 512], BF16) for i in range(2)]
    fol = [q7(f"fol{i}", [128, DIM], BF16) for i in range(2)]
    dump7 = q7("dump7", [128, DIM], F32)
    pso2 = bs7q.enter_context(nc.psum_tensor("pso2", [128, 8 * 512], F32))
    with nc.Block() as block:
        @block.sync
        def _(sync):
            for oj in range(8):
                for q in range(4):
                    i = 4 * oj + q
                    if i >= 2:
                        prev = i - 2
                        sync.wait_ge(EPK.h, 6 * (prev // 4) + (prev % 4) + 1)
                    BWS.load(sync.dma_start(
                        bwb[i % 2][:], bw[oj, q // 2][:, (q % 2) * 4096:(q % 2 + 1) * 4096]), i)
                for hh in range(2):
                    j = 2 * oj + hh
                    if j >= 2:
                        prev = j - 2
                        sync.wait_ge(EPK.h, 6 * (prev // 2) + 5 + (prev % 2))
                    BTS.load(sync.dma_start(
                        btb[j % 2][:], betw[oj][:, hh * 4096:(hh + 1) * 4096]), j)
            for b_ in range(2):
                sync.wait_ge(FFS.s[b_].h, 16 * 32)
            for i in range(8):
                if i >= 1:
                    sync.wait_ge(EV.h, 6 * i)
                FFL.load(sync.dma_start(fol[i % 2][:], d_ffn.ap()[i * 128:(i + 1) * 128, :]), i)
                XNL.load(sync.dma_start(xnl[i % 2][:], d_xnew.ap()[i * 128:(i + 1) * 128, :]), i)

        @block.tensor
        def _(tensor):
            tensor.wait_ge(RTE.h, 16)
            for oj in range(8):
                if oj > 0:
                    tensor.wait_ge(EFE.h, 8 * oj)
                for q in range(4):
                    i = 4 * oj + q
                    BWS.wait(tensor, i)
                    for kt in range(8):
                        k = 8 * q + kt
                        for t in range(8):
                            inst = tensor.matmul(
                                pso2[:, t * 512:(t + 1) * 512],
                                x2fm[:, k * 1024 + t * 128:k * 1024 + (t + 1) * 128],
                                bwb[i % 2][:, kt * 512:(kt + 1) * 512],
                                start=(k == 0), stop=False)
                    n = EPK.inc(inst)
                    assert n == 6 * oj + q + 1
                for t in range(8):
                    tensor.matmul(pso2[:, t * 512:(t + 1) * 512], c_onesb[:],
                                  bbr[:, oj * 512:(oj + 1) * 512], start=False, stop=False)
                for hh in range(2):
                    j = 2 * oj + hh
                    BTS.wait(tensor, j)
                    for e in range(8):
                        eg = 8 * hh + e
                        for t in range(8):
                            inst = tensor.matmul(
                                pso2[:, t * 512:(t + 1) * 512],
                                RT[:, (t * 16 + eg) * 128:(t * 16 + eg + 1) * 128],
                                btb[j % 2][:, e * 512:(e + 1) * 512],
                                start=False, stop=(eg == 15))
                    n = EPK.inc(inst)
                    assert n == 6 * oj + 5 + hh

        @block.scalar
        def _(scalar):
            for oj in range(8):
                scalar.wait_ge(EPK.h, 6 * oj + 6)
                for t in range(8):
                    i = 8 * oj + t
                    FFS.wait(scalar, i - 2)
                    EFE.inc(scalar.activation(fs[i % 2][:],
                                              pso2[:, t * 512:(t + 1) * 512], AF.Copy))
            for i in range(8):
                FFL.wait(scalar, i)
                if i > 0:
                    scalar.wait_ge(EV.h, 6 * (i - 1) + 6)
                n = ES.inc(scalar.activation(dump7[:], fol[i % 2][:], AF.Square,
                                             accum_out=stat[:, 0:1]))
                assert n == 3 * i + 1
                scalar.wait_ge(EV.h, 6 * i + 4)
                n = ES.inc(scalar.activation(stat[:, 11:12], stat[:, 3:4], AF.Ln, bias=c_eps[:]))
                scalar.wait_ge(ES.h, n)
                n = ES.inc(scalar.activation(stat[:, 4:5], stat[:, 11:12], AF.Exp, scale=-0.5))
                assert n == 3 * i + 3

        @block.vector
        def _(vector):
            for i in range(8):
                b = i % 2
                vector.wait_ge(ES.h, 3 * i + 1)
                if i > 0:
                    vector.wait_ge(ES.h, 3 * (i - 1) + 2)
                n = EV.inc(vector.tensor_reduce(stat[:, 1:2], fol[i % 2][:],
                                                AX.X, ALU.add))
                vector.wait_ge(EV.h, n)
                n = EV.inc(vector.tensor_scalar(stat[:, 2:3], stat[:, 1:2], 1.0 / DIM, None, ALU.mult))
                vector.wait_ge(EV.h, n)
                n = EV.inc(vector.tensor_mul(stat[:, 13:14], stat[:, 2:3], stat[:, 2:3]))
                vector.wait_ge(EV.h, n)
                n = EV.inc(vector.tensor_scalar(stat[:, 3:4], stat[:, 0:1], 1.0 / DIM,
                                                stat[:, 13:14], ALU.mult, ALU.subtract))
                assert n == 6 * i + 4
                vector.wait_ge(ES.h, 3 * i + 3)
                n = EV.inc(vector.tensor_scalar(dump7[:], fol[i % 2][:],
                                                stat[:, 2:3], stat[:, 4:5], ALU.subtract, ALU.mult))
                XNL.wait(vector, i)
                vector.wait_ge(EV.h, n)
                if i >= 1:
                    vector.wait_ge(OUTS.h, 16 * i)
                n = EV.inc(vector.tensor_add(ob[b][:], dump7[:], xnl[b][:]))
                assert n == 6 * i + 6

        @block.gpsimd
        def _(gpsimd):
            for oj in range(8):
                for t in range(8):
                    i = 8 * oj + t
                    gpsimd.wait_ge(EFE.h, i + 1)
                    FFS.load(gpsimd.dma_start(
                        d_ffn.ap()[t * 128:(t + 1) * 128, oj * 512:(oj + 1) * 512],
                        fs[i % 2][:]), i)
            for i in range(8):
                gpsimd.wait_ge(EV.h, 6 * i + 6)
                OUTS.inc(gpsimd.dma_start(out[i * 128:(i + 1) * 128, :], ob[i % 2][:]), 16)
            gpsimd.wait_ge(OUTS.h, 16 * 8)

    bs7q.close()
    bs7.close()
    es.close()
    return nc, dbg

def _prep_shared(i):
    w = {}
    ipw = np.asarray(i["L0_attn_in_proj_w"])
    w["w1"] = np.ascontiguousarray(
        ipw.reshape(FT, 128, KT, 128).transpose(0, 3, 2, 1).reshape(FT, 128, KT * 128)
    ).astype(BF)
    cw = np.asarray(i["L0_attn_conv_w"])
    cd = np.zeros((FT_HBC, 4, 128, 128), np.float32)
    idx = np.arange(128)
    cd[:, :, idx, idx] = cw.reshape(FT_HBC, 128, 4).transpose(0, 2, 1)
    w["convd"] = np.ascontiguousarray(
        cd.transpose(0, 2, 1, 3).reshape(FT_HBC, 128, 4 * 128)).astype(BF)
    w["convb"] = np.ascontiguousarray(np.asarray(i["L0_attn_conv_b"]).reshape(FT_HBC, 128).T).astype(np.float32)
    w["normw"] = np.ascontiguousarray(np.asarray(i["L0_attn_norm_w"]).reshape(FT_GATE, 128).T).astype(np.float32)
    w["dtbias"] = np.asarray(i["L0_attn_dt_bias"]).reshape(128, 1).astype(np.float32)
    w["alog"] = np.asarray(i["L0_attn_A_log"]).reshape(128, 1).astype(np.float32)
    w["dexp"] = np.repeat(np.asarray(i["L0_attn_D"]).astype(np.float32), P).astype(BF)
    opw = np.asarray(i["L0_attn_out_proj_w"])
    w["opw"] = np.ascontiguousarray(
        opw.T.reshape(4, 16, 128, 8, 512).transpose(3, 0, 2, 1, 4).reshape(8, 4, 128, 16 * 512)
    ).astype(BF)
    bwm = np.asarray(i["L0_ffn_base_w"])
    w["bw"] = np.ascontiguousarray(
        bwm.T.reshape(2, 16, 128, 8, 512).transpose(3, 0, 2, 1, 4).reshape(8, 2, 128, 16 * 512)
    ).astype(BF)
    A = np.asarray(i["L0_ffn_A"])
    rw = np.asarray(i["L0_ffn_router_w"])
    ar = np.concatenate([A.T, rw.T], axis=1)
    w["arw"] = np.ascontiguousarray(ar.reshape(KT, 128, 80).transpose(1, 0, 2).reshape(128, KT * 80)).astype(BF)
    B = np.asarray(i["L0_ffn_B"])
    w["betw"] = np.ascontiguousarray(
        B.transpose(2, 0, 1).reshape(64, 16, 8, 512).transpose(2, 0, 1, 3).reshape(8, 64, 16 * 512)
    ).astype(BF)
    w["bbias"] = np.asarray(i["L0_ffn_base_b"]).reshape(1, DIM).astype(BF)
    w["identb"] = np.eye(128, dtype=np.float32).astype(BF)
    w["identf"] = np.eye(128, dtype=np.float32)
    w["triu01"] = np.triu(np.ones((128, 128), np.float32))
    w["onesb"] = np.ones((1, 128), np.float32).astype(BF)
    w["epsb"] = np.full((128, 1), EPS, np.float32)
    return w


def make_in_maps(inputs):
    shared = _prep_shared(inputs)
    x = np.asarray(inputs["L0_attn_hidden_states"])
    maps = []
    for c in range(NCORES):
        b, q = c // 4, c % 4
        xhc = np.zeros((TH, DIM), np.float32)
        if q > 0:
            xhc[:] = x[b, q * T - 3:(q + 1) * T]
        else:
            xhc[3:] = x[b, 0:T]
        rank = q
        rm = np.zeros((1, 3 * 128), np.float32)
        for j in range(3):
            if j < rank:
                rm[0, j * 128:(j + 1) * 128] = 1.0
        m = dict(shared)
        m["xh"] = xhc
        m["rmask"] = rm
        m["rmaskc"] = 1.0 - rm
        maps.append(m)
    return maps


def kernel(**inputs):
    nc, _ = build(debug=False, upto=99)
    maps = make_in_maps(inputs)
    res = run_bass_kernel_spmd(nc, maps, core_ids=list(range(NCORES)))
    out = np.zeros((2, 4096, DIM), np.float32)
    for c in range(NCORES):
        b, q = c // 4, c % 4
        out[b, q * T:(q + 1) * T] = res.results[c]["out"]
    return out


# version 04 (final)
# speedup vs baseline: 1.0303x; 1.0173x over previous
import numpy as np
import ml_dtypes
from contextlib import ExitStack

import concourse.bass as bass
import concourse.mybir as mybir
from concourse.bass_utils import run_bass_kernel_spmd

F32 = mybir.dt.float32
BF16 = mybir.dt.bfloat16
AF = mybir.ActivationFunctionType
ALU = mybir.AluOpType
AX = mybir.AxisListType
BF = ml_dtypes.bfloat16

NCORES = 8
T = 1024
TH = T + 3
CS = 128
NCH = T // CS
H, P, G, N = 128, 64, 8, 128
DIM = 4096
INTER = 8192
CONV = 10240
FT_GATE, FT_HBC = 64, 80
FT = FT_GATE + FT_HBC + 1
KT = DIM // 128
EPS = 1e-5
NEG = -1e9
CLAMP = 1e30


class Sem:
    def __init__(self, nc, es, name):
        self.h = es.enter_context(nc.semaphore(name))
        self.n = 0

    def inc(self, inst, k=1):
        inst.then_inc(self.h, k)
        self.n += k
        return self.n

    def wait(self, eng, n=None):
        eng.wait_ge(self.h, self.n if n is None else n)


class Stream:
    def __init__(self, mk, name, nbuf):
        self.s = [mk(f"{name}{i}") for i in range(nbuf)]
        self.nb = nbuf

    def load(self, inst, i, k=16):
        b = i % self.nb
        return self.s[b].inc(inst, k)

    def wait(self, eng, i):
        if i < 0:
            return
        b = i % self.nb
        eng.wait_ge(self.s[b].h, 16 * (i // self.nb + 1))


def build(debug=False, upto=99, b2n=NCH):
    nc = bass.Bass("TRN2", target_bir_lowering=False)
    ap = lambda t: t.ap()

    xh = ap(nc.dram_tensor("xh", [TH, DIM], F32, kind="ExternalInput"))
    w1 = ap(nc.dram_tensor("w1", [FT, 128, KT * 128], BF16, kind="ExternalInput"))
    convd = ap(nc.dram_tensor("convd", [FT_HBC, 128, 4 * 128], BF16, kind="ExternalInput"))
    convb = ap(nc.dram_tensor("convb", [128, FT_HBC], F32, kind="ExternalInput"))
    normw = ap(nc.dram_tensor("normw", [128, FT_GATE], F32, kind="ExternalInput"))
    dtbias = ap(nc.dram_tensor("dtbias", [128, 1], F32, kind="ExternalInput"))
    alog = ap(nc.dram_tensor("alog", [128, 1], F32, kind="ExternalInput"))
    dexp = ap(nc.dram_tensor("dexp", [INTER], BF16, kind="ExternalInput"))
    opw = ap(nc.dram_tensor("opw", [8, 4, 128, 16 * 512], BF16, kind="ExternalInput"))
    bw = ap(nc.dram_tensor("bw", [8, 2, 128, 16 * 512], BF16, kind="ExternalInput"))
    arw = ap(nc.dram_tensor("arw", [128, KT * 80], BF16, kind="ExternalInput"))
    betw = ap(nc.dram_tensor("betw", [8, 64, 16 * 512], BF16, kind="ExternalInput"))
    bbias = ap(nc.dram_tensor("bbias", [1, DIM], BF16, kind="ExternalInput"))
    identb = ap(nc.dram_tensor("identb", [128, 128], BF16, kind="ExternalInput"))
    identf = ap(nc.dram_tensor("identf", [128, 128], F32, kind="ExternalInput"))
    triu01 = ap(nc.dram_tensor("triu01", [128, 128], F32, kind="ExternalInput"))
    onesb = ap(nc.dram_tensor("onesb", [1, 128], BF16, kind="ExternalInput"))
    rmask = ap(nc.dram_tensor("rmask", [1, 3 * 128], F32, kind="ExternalInput"))
    rmaskc = ap(nc.dram_tensor("rmaskc", [1, 3 * 128], F32, kind="ExternalInput"))
    epsb = ap(nc.dram_tensor("epsb", [128, 1], F32, kind="ExternalInput"))

    out = ap(nc.dram_tensor("out", [T, DIM], F32, kind="ExternalOutput"))

    d_xln = nc.dram_tensor("d_xln", [1152, DIM], BF16)
    d_hbc = nc.dram_tensor("d_hbc", [CONV, T], BF16)
    d_sg = nc.dram_tensor("d_sg", [INTER, T], BF16)
    d_cum = nc.dram_tensor("d_cum", [NCH, 128 * 128], F32)
    d_lastr = nc.dram_tensor("d_lastr", [NCH, 128], F32)
    d_dpref = nc.dram_tensor("d_dpref", [NCH + 1, 128], F32)
    d_spref = nc.dram_tensor("d_spref", [NCH, 128, INTER], BF16)
    d_yrms = nc.dram_tensor("d_yrms", [T, INTER], BF16)
    d_attn = nc.dram_tensor("d_attn", [T, DIM], BF16)
    d_xnew = nc.dram_tensor("d_xnew", [T, DIM], F32)
    d_xln2 = nc.dram_tensor("d_xln2", [T, DIM], BF16)
    d_ffn = nc.dram_tensor("d_ffn", [T, DIM], BF16)
    ag_s_in = nc.dram_tensor("ag_s_in", [128, INTER], BF16)
    ag_s_out = nc.dram_tensor("ag_s_out", [4 * 128, INTER], BF16)
    ag_g_in = nc.dram_tensor("ag_g_in", [1, 128], F32)
    ag_g_out = nc.dram_tensor("ag_g_out", [4, 128], F32)
    d_m = nc.dram_tensor("d_m", [3, 128], F32)

    dbg = {}
    if debug:
        for nm, shp, dt in [
            ("dbg_xln", [1027, DIM], BF16), ("dbg_hbc", [CONV, T], BF16),
            ("dbg_g0", [128, T], F32), ("dbg_s0", [128, T], F32),
            ("dbg_sg", [INTER, T], BF16), ("dbg_dt", [128, T], F32),
            ("dbg_cum", [NCH, 128 * 128], F32), ("dbg_spref", [NCH, 128, INTER], BF16),
            ("dbg_spub", [128, INTER], BF16), ("dbg_sin", [128, INTER], BF16),
            ("dbg_yrms", [T, INTER], BF16), ("dbg_attn", [T, DIM], BF16),
            ("dbg_bt", [128, 1024], BF16), ("dbg_xdd", [128, INTER], BF16),
            ("dbg_ddt", [128, T], mybir.dt.float32), ("dbg_xtb", [128, INTER], BF16),
            ("dbg_xnew", [T, DIM], F32), ("dbg_xln2", [T, DIM], BF16),
        ]:
            dbg[nm] = ap(nc.dram_tensor(nm, shp, dt, kind="ExternalOutput"))

    es = ExitStack()
    Sm = lambda name: Sem(nc, es, name)
    sb = lambda *a, **k: es.enter_context(nc.sbuf_tensor(*a, **k))

    PH = Sm("PH")
    DBGS = Sm("DBGS")
    nphase = [0]

    def barrier(dma_sems=()):
        nphase[0] += 1
        tgt = 5 * nphase[0]
        PH.n = tgt
        with nc.Block() as block:
            @block.sync
            def _(sync):
                for s in dma_sems:
                    s.wait(sync)
                sync.sem_inc(PH.h, 1)
                sync.wait_ge(PH.h, tgt)

            @block.tensor
            def _(tensor):
                tensor.sem_inc(PH.h, 1)
                tensor.wait_ge(PH.h, tgt)

            @block.vector
            def _(vector):
                vector.drain()
                vector.sem_inc(PH.h, 1)
                vector.wait_ge(PH.h, tgt)

            @block.scalar
            def _(scalar):
                scalar.drain()
                scalar.sem_inc(PH.h, 1)
                scalar.wait_ge(PH.h, tgt)

            @block.gpsimd
            def _(gpsimd):
                gpsimd.drain()
                gpsimd.sem_inc(PH.h, 1)
                gpsimd.wait_ge(PH.h, tgt)

    def dbg_dump(pairs):
        if not debug or not pairs:
            return
        with nc.Block() as block:
            @block.gpsimd
            def _(gpsimd):
                for dst, src in pairs:
                    DBGS.inc(gpsimd.dma_start(dst, src), 16)
                gpsimd.wait_ge(DBGS.h, DBGS.n)

    c_identb = sb("c_identb", [128, 128], BF16)
    c_identf = sb("c_identf", [128, 128], F32)
    c_triu = sb("c_triu", [128, 128], F32)
    c_onesb = sb("c_onesb", [1, 128], BF16)
    c_dtb = sb("c_dtb", [128, 1], F32)
    c_alog = sb("c_alog", [128, 1], F32)
    c_eps = sb("c_eps", [128, 1], F32)
    stat = sb("stat", [128, 32], F32)
    bsM = ExitStack()
    mb_ = lambda *a, **k: bsM.enter_context(nc.sbuf_tensor(*a, **k))
    biasT_all = mb_("biasT_all", [128, NCH * 128], F32)
    expcA_all = mb_("expcA_all", [128, NCH * 128], F32)
    DT = mb_("DT", [128, T], F32)
    A_dt = mb_("A_dt", [128, T], F32)

    CL = Sm("CL"); SQ = Sm("SQ"); VST = Sm("VST"); RST = Sm("RST")
    XW = Sm("XW"); SSELF = Sm("SSELF"); VSELF = Sm("VSELF")
    LD = Stream(Sm, "LD", 2); XS = Stream(Sm, "XS", 2)
    NT0 = 9
    with ExitStack() as ps:
        psb = lambda *a, **k: ps.enter_context(nc.sbuf_tensor(*a, **k))
        xbuf = [psb(f"a0x{i}", [128, DIM], F32) for i in range(2)]
        xlnb = [psb(f"a0l{i}", [128, DIM], BF16) for i in range(2)]
        dump = psb("a0d", [128, DIM], F32)
        st = [psb(f"a0s{i}", [128, 8], F32) for i in range(2)]

        with nc.Block() as block:
            @block.sync
            def _(sync):
                for dst, src in [(c_identb[:], identb), (c_identf[:], identf),
                                 (c_triu[:], triu01), (c_onesb[:], onesb),
                                 (c_dtb[:], dtbias), (c_alog[:], alog),
                                 (c_eps[:], epsb)]:
                    CL.inc(sync.dma_start(dst, src), 16)
                for i in range(NT0):
                    r0 = i * 128
                    nr = min(128, TH - r0)
                    if i >= 2:
                        sync.wait_ge(XW.h, i - 1)
                        sync.wait_ge(SQ.h, i - 1)
                    LD.load(sync.dma_start(xbuf[i % 2][0:nr, :], xh[r0:r0 + nr, :]), i)

            @block.scalar
            def _(scalar):
                scalar.wait_ge(CL.h, CL.n)
                for i in range(NT0):
                    nr = min(128, TH - i * 128)
                    b = i % 2
                    LD.wait(scalar, i)
                    SQ.inc(scalar.activation(dump[0:nr, :], xbuf[b][0:nr, :], AF.Square,
                                             accum_out=st[b][0:nr, 4:5]))
                    scalar.wait_ge(VST.h, i + 1)
                    sl = SSELF.inc(scalar.activation(st[b][0:nr, 5:6], st[b][0:nr, 3:4],
                                                     AF.Ln, bias=c_eps[0:nr, :]))
                    scalar.wait_ge(SSELF.h, sl)
                    RST.inc(scalar.activation(st[b][0:nr, 6:7], st[b][0:nr, 5:6],
                                              AF.Exp, scale=-0.5))

            @block.vector
            def _(vector):
                for i in range(NT0):
                    nr = min(128, TH - i * 128)
                    b = i % 2
                    LD.wait(vector, i)
                    if i >= 2:
                        vector.wait_ge(RST.h, i - 1)
                    v0 = VSELF.inc(vector.tensor_reduce(st[b][0:nr, 0:1], xbuf[b][0:nr, :],
                                                        AX.X, ALU.add))
                    vector.wait_ge(VSELF.h, v0)
                    v1 = VSELF.inc(vector.tensor_scalar(st[b][0:nr, 1:2], st[b][0:nr, 0:1],
                                                        1.0 / DIM, None, ALU.mult))
                    vector.wait_ge(VSELF.h, v1)
                    v2 = VSELF.inc(vector.tensor_mul(st[b][0:nr, 2:3], st[b][0:nr, 1:2],
                                                     st[b][0:nr, 1:2]))
                    vector.wait_ge(SQ.h, i + 1)
                    vector.wait_ge(VSELF.h, v2)
                    VST.inc(vector.tensor_scalar(st[b][0:nr, 3:4], st[b][0:nr, 4:5],
                                                 1.0 / DIM, st[b][0:nr, 2:3],
                                                 ALU.mult, ALU.subtract))
                    vector.wait_ge(RST.h, i + 1)
                    XS.wait(vector, i - 2)
                    XW.inc(vector.tensor_scalar(xlnb[b][0:nr, :], xbuf[b][0:nr, :],
                                                st[b][0:nr, 1:2], st[b][0:nr, 6:7],
                                                ALU.subtract, ALU.mult))

            @block.gpsimd
            def _(gpsimd):
                for i in range(NT0):
                    nr = min(128, TH - i * 128)
                    gpsimd.wait_ge(XW.h, i + 1)
                    XS.load(gpsimd.dma_start(d_xln.ap()[i * 128:i * 128 + nr, :],
                                             xlnb[i % 2][0:nr, :]), i)
                for b in range(2):
                    gpsimd.wait_ge(XS.s[b].h, XS.s[b].n)
                m = XW.inc(gpsimd.memset(dump[0:125, :], 0.0))
                gpsimd.wait_ge(XW.h, m)
                XS.s[0].inc(gpsimd.dma_start(d_xln.ap()[1027:1152, :], dump[0:125, :]), 16)
                gpsimd.wait_ge(XS.s[0].h, XS.s[0].n)

        barrier(dma_sems=[CL])
        dbg_dump([(dbg["dbg_xln"], d_xln.ap()[0:1027, :])] if debug else [])
    if upto <= 0:
        bsM.close(); es.close()
        return nc, dbg

    XF = Sm("XF"); PJD = Sm("PJD")
    PJE = [Sm("PJE0"), Sm("PJE1")]
    RW = Sm("RW"); CVD = Sm("CVD"); CE = Sm("CE")
    SGA = Sm("SGA"); SGB = Sm("SGB")
    WLs = [Sm("WL0"), Sm("WL1")]
    SGS = Stream(Sm, "SGS", 2); HS = Stream(Sm, "HS", 2)
    A1V = Sm("A1V"); A1S = Sm("A1S")
    with ExitStack() as ps:
        psb = lambda *a, **k: ps.enter_context(nc.sbuf_tensor(*a, **k))
        xfm = psb("xfm", [128, KT * 1152], BF16)
        wbuf = [psb(f"wb{i}", [128, KT * 128], BF16) for i in range(2)]
        cvbuf = [psb(f"cv{i}", [128, 4 * 128], BF16) for i in range(2)]
        rawb = [psb(f"raw{i}", [128, TH], BF16) for i in range(2)]
        hsil = [psb(f"hs{i}", [128, T], BF16) for i in range(2)]
        sgt = [psb(f"sgt{i}", [128, T], F32) for i in range(2)]
        sgo = [psb(f"sgo{i}", [128, T], BF16) for i in range(2)]
        c_convb = psb("c_convb", [128, FT_HBC], F32)
        c_normw = psb("c_normw", [128, FT_GATE], F32)
        zb = psb("zb", [128, T], F32)
        az = psb("az", [128, T], F32)
        statA = psb("statA", [128, 2], F32)
        dbgg = psb("dbgg", [128, 2 * T], F32) if debug else None
        pjp = [ps.enter_context(nc.psum_tensor(f"pj{i}", [128, 1536], F32)) for i in range(2)]
        cvp = ps.enter_context(nc.psum_tensor("cvp", [128, 1024], F32))

        wl_cum = []

        with nc.Block() as block:
            @block.sync
            def _(sync):
                CL.inc(sync.dma_start(c_convb[:], convb), 16)
                CL.inc(sync.dma_start(c_normw[:], normw), 16)
                for k in range(KT):
                    XF.inc(sync.dma_start(xfm[:, k * 1152:(k + 1) * 1152],
                                          d_xln.ap()[0:1152, k * 128:(k + 1) * 128],
                                          transpose=True), 16)
                for ft in range(FT):
                    if ft >= 2:
                        sync.wait_ge(PJD.h, ft - 1)
                        prev_hbc = (ft - 2) - FT_GATE
                        if 0 <= prev_hbc:
                            sync.wait_ge(CVD.h, min(prev_hbc, FT_HBC - 1) + 1)
                    WLs[ft % 2].inc(sync.dma_start(wbuf[ft % 2][:], w1[ft]), 16)
                    if FT_GATE <= ft < FT_GATE + FT_HBC:
                        WLs[ft % 2].inc(sync.dma_start(cvbuf[ft % 2][:], convd[ft - FT_GATE]), 16)

            accp = [0, 0]
            for ft in range(FT):
                accp[ft % 2] += 16
                if FT_GATE <= ft < FT_GATE + FT_HBC:
                    accp[ft % 2] += 16
                wl_cum.append(accp[ft % 2])

            @block.tensor
            def _(tensor):
                tensor.wait_ge(XF.h, XF.n)
                tensor.wait_ge(CL.h, CL.n)
                pje_use = [0, 0]
                for ft in range(FT):
                    b = ft % 2
                    tensor.wait_ge(WLs[b].h, wl_cum[ft])
                    if pje_use[b] > 0:
                        tensor.wait_ge(PJE[b].h, pje_use[b])
                    pje_use[b] += 1
                    is_hbc = FT_GATE <= ft < FT_GATE + FT_HBC
                    cols = ([(0, 512), (512, 1024), (1024, 1027)] if is_hbc
                            else [(3, 515), (515, 1027)])
                    for k in range(KT):
                        lhs = wbuf[b][:, k * 128:(k + 1) * 128]
                        for ci, (c0, c1) in enumerate(cols):
                            o0 = c0 if is_hbc else c0 - 3
                            inst = tensor.matmul(pjp[b][:, o0:o0 + (c1 - c0)], lhs,
                                                 xfm[:, k * 1152 + c0:k * 1152 + c1],
                                                 start=(k == 0), stop=(k == KT - 1))
                    PJD.inc(inst)
                    if is_hbc:
                        hi = ft - FT_GATE
                        tensor.wait_ge(RW.h, hi + 1)
                        if hi > 0:
                            tensor.wait_ge(CE.h, hi)
                        rb = rawb[hi % 2]
                        for k in range(4):
                            dg = cvbuf[b][:, k * 128:(k + 1) * 128]
                            tensor.matmul(cvp[:, 0:512], dg, rb[:, k:k + 512],
                                          start=(k == 0), stop=(k == 3))
                        for k in range(4):
                            dg = cvbuf[b][:, k * 128:(k + 1) * 128]
                            inst = tensor.matmul(cvp[:, 512:1024], dg, rb[:, 512 + k:1024 + k],
                                                 start=(k == 0), stop=(k == 3))
                        CVD.inc(inst)

            @block.vector
            def _(vector):
                gi = 0
                for ft in range(FT):
                    b = ft % 2
                    vector.wait_ge(PJD.h, ft + 1)
                    if ft < FT_GATE:
                        vector.wait_ge(SGA.h, ft + 1)
                        SGS.wait(vector, ft - 2)
                        SGB.inc(vector.tensor_scalar(sgo[b][:], sgt[b][:],
                                                     c_normw[:, ft:ft + 1], None, ALU.mult))
                    elif ft < FT_GATE + FT_HBC:
                        hi = ft - FT_GATE
                        if hi >= 2:
                            vector.wait_ge(CVD.h, hi - 1)
                        n = RW.inc(vector.tensor_copy(rawb[hi % 2][:], pjp[b][:, 0:TH]))
                        vector.wait_ge(RW.h, n)
                        PJE[b].inc(vector.engine_nop())
                    else:
                        v = A1V.inc(vector.tensor_scalar(zb[:], pjp[b][:, 0:T],
                                                         c_dtb[:], None, ALU.add))
                        vector.wait_ge(A1V.h, v)
                        PJE[b].inc(vector.engine_nop())
                        v = A1V.inc(vector.tensor_scalar_max(DT[:], zb[:], 0.0))
                        vector.wait_ge(A1V.h, v)
                        vector.wait_ge(A1S.h, 3)
                        v = A1V.inc(vector.tensor_add(DT[:], DT[:], az[:]))
                        vector.wait_ge(A1V.h, v)
                        vector.wait_ge(A1S.h, 4)
                        A1V.inc(vector.tensor_scalar(A_dt[:], DT[:], statA[:, 0:1], -1.0,
                                                     ALU.mult, ALU.mult))
                        if debug:
                            A1V.inc(vector.tensor_copy(dbgg[:, 0:T], zb[:]))
                            A1V.inc(vector.tensor_copy(dbgg[:, T:2 * T], az[:]))

            @block.scalar
            def _(scalar):
                for ft in range(FT_GATE):
                    b = ft % 2
                    scalar.wait_ge(PJD.h, ft + 1)
                    if ft >= 2:
                        scalar.wait_ge(SGB.h, ft - 1)
                    n = SGA.inc(scalar.activation(sgt[b][:], pjp[b][:, 0:T], AF.Silu))
                    scalar.wait_ge(SGA.h, n)
                    PJE[b].inc(scalar.nop())
                for hi in range(FT_HBC):
                    scalar.wait_ge(CVD.h, hi + 1)
                    HS.wait(scalar, hi - 2)
                    CE.inc(scalar.activation(hsil[hi % 2][:], cvp[:], AF.Silu,
                                             bias=c_convb[:, hi:hi + 1]))
                scalar.wait_ge(A1V.h, 1)
                s1 = A1S.inc(scalar.activation(az[:], zb[:], AF.Abs))
                scalar.wait_ge(A1S.h, s1)
                s1 = A1S.inc(scalar.activation(az[:], az[:], AF.Exp, scale=-1.0))
                scalar.wait_ge(A1S.h, s1)
                A1S.inc(scalar.activation(az[:], az[:], AF.Ln, bias=1.0))
                A1S.inc(scalar.activation(statA[:, 0:1], c_alog[:], AF.Exp))

            @block.gpsimd
            def _(gpsimd):
                for ft in range(FT_GATE):
                    gpsimd.wait_ge(SGB.h, ft + 1)
                    SGS.load(gpsimd.dma_start(d_sg.ap()[ft * 128:(ft + 1) * 128, :],
                                              sgo[ft % 2][:]), ft)
                for hi in range(FT_HBC):
                    gpsimd.wait_ge(CE.h, hi + 1)
                    HS.load(gpsimd.dma_start(d_hbc.ap()[hi * 128:(hi + 1) * 128, :],
                                             hsil[hi % 2][:]), hi)
                for b in range(2):
                    gpsimd.wait_ge(SGS.s[b].h, SGS.s[b].n)
                    gpsimd.wait_ge(HS.s[b].h, HS.s[b].n)

        barrier(dma_sems=[XF, CL])
        dbg_dump([(dbg["dbg_hbc"], d_hbc.ap()[:, :]), (dbg["dbg_sg"], d_sg.ap()[:, :]),
                  (dbg["dbg_dt"], DT[:]),
                  (dbg["dbg_g0"], dbgg[:, 0:T]), (dbg["dbg_s0"], dbgg[:, T:2 * T])]
                 if debug else [])

    if upto <= 1:
        bsM.close(); es.close()
        return nc, dbg

    bsA = ExitStack()
    ab = lambda *a, **k: bsA.enter_context(nc.sbuf_tensor(*a, **k))
    Sb = ab("Sb", [128, INTER], BF16)
    ysb = ab("ysb", [128, INTER], F32)
    xTb = ab("xTb", [128, INTER], BF16)
    bcm = ab("bcm", [128, G * 128], BF16)

    B0V = Sm("B0V"); B0S = Sm("B0S"); B0T = Sm("B0T"); B0D = Sm("B0D"); B0E = Sm("B0E")
    LB = Stream(Sm, "LB", 2)
    bs2 = ExitStack()
    pb2 = lambda *a, **k: bs2.enter_context(nc.sbuf_tensor(*a, **k))
    S = pb2("S", [128, INTER], F32)
    xT2 = pb2("xT2", [128, INTER], BF16)
    xdd = pb2("xdd", [128, INTER], BF16)
    bT = pb2("bT", [128, G * 128], BF16)
    cumA_all = pb2("cumA_all", [128, T], F32)
    ddtT_all = pb2("ddtT_all", [128, T], F32)
    expdA_all = pb2("expdA_all", [128, T], F32)
    dpref = pb2("dpref", [1, (NCH + 1) * 128], F32)
    catb = [pb2(f"catb{i}", [128, 128], F32) for i in range(2)]
    dttb = [pb2(f"dttb{i}", [128, 128], F32) for i in range(2)]
    lastb = [pb2(f"lastb{i}", [128, 128], F32) for i in range(2)]
    lnd = [pb2(f"lnd{i}", [128, 128], F32) for i in range(2)]
    tmb = [pb2(f"tmb{i}", [128, 128], F32) for i in range(2)]
    e2b = [pb2(f"e2b{i}", [128, 128], F32) for i in range(2)]
    pp2 = bs2.enter_context(nc.psum_tensor("pp2", [128, 256], F32))

    scan_i = {}; evA_i = {}; evB_i = {}; biasT_i = {}; tmb_i = {}; ddt_i = {}
    lnd_i = {}; eca_i = {}; eda_i = {}; e2_i = {}; tr_i = {}; lb_i = {}
    dpf_i = {}
    with nc.Block() as block:
        @block.vector
        def _(vector):
            for c in range(NCH):
                scan_i[c] = B0V.inc(vector.tensor_tensor_scan(
                    cumA_all[:, c * 128:(c + 1) * 128], A_dt[:, c * 128:(c + 1) * 128],
                    A_dt[:, c * 128:(c + 1) * 128], 0.0, ALU.add, ALU.bypass))
            for c in range(NCH):
                cb = c % 2
                vector.wait_ge(B0T.h, c + 1)
                if c >= 2:
                    vector.wait_ge(B0S.h, 4 * (c - 2) + 2)
                v = B0E.inc(vector.tensor_copy(catb[cb][:], pp2[:, 0:128]))
                evA_i[c] = v
                vector.wait_ge(B0E.h, v)
                evB_i[c] = B0E.inc(vector.tensor_copy(dttb[cb][:], pp2[:, 128:256]))
                vector.wait_ge(B0E.h, evB_i[c])
                vector.wait_ge(B0S.h, 4 * c + 1)
                biasT_i[c] = B0V.inc(vector.tensor_sub(
                    biasT_all[:, c * 128:(c + 1) * 128], lnd[cb][:], catb[cb][:]))
                LB.wait(vector, c)
                if c >= 2:
                    vector.wait_ge(B0S.h, 4 * (c - 2) + 4)
                tmb_i[c] = B0V.inc(vector.tensor_sub(tmb[cb][:], lastb[cb][:], catb[cb][:]))
                vector.wait_ge(B0V.h, tmb_i[c])
                vector.wait_ge(B0S.h, 4 * c + 4)
                ddt_i[c] = B0V.inc(vector.tensor_mul(
                    ddtT_all[:, c * 128:(c + 1) * 128], e2b[cb][:], dttb[cb][:]))
            v = B0V.inc(vector.memset(dpref[:, 0:128], 1.0))
            for c in range(NCH):
                vector.wait_ge(B0V.h, v)
                vector.wait_ge(B0S.h, 4 * c + 3)
                v = B0V.inc(vector.tensor_mul(dpref[:, (c + 1) * 128:(c + 2) * 128],
                                              dpref[:, c * 128:(c + 1) * 128],
                                              expdA_all[0:1, c * 128:(c + 1) * 128]))
                dpf_i[c] = v

        @block.sync
        def _(sync):
            sync.wait_ge(B0V.h, scan_i[NCH - 1])
            for c in range(NCH):
                B0D.inc(sync.dma_start(d_cum.ap()[c], cumA_all[:, c * 128:(c + 1) * 128]), 16)
            for c in range(NCH):
                B0D.inc(sync.dma_start(d_lastr.ap()[c].unsqueeze(0),
                                       cumA_all[:, c * 128 + 127:c * 128 + 128],
                                       allow_slow_non_contiguous=True), 16)
            for c in range(NCH):
                if c >= 2:
                    sync.wait_ge(B0S.h, 4 * (c - 1) + 3)
                sync.wait_ge(B0D.h, 16 * 2 * NCH)
                lb_i[c] = LB.load(sync.dma_start(
                    lastb[c % 2][:], d_lastr.ap()[c].unsqueeze(0).partition_broadcast(128)), c)
            sync.wait_ge(B0V.h, dpf_i[NCH - 1])
            B0D.inc(sync.dma_start(d_dpref.ap().rearrange("c h -> (c h)").unsqueeze(0),
                                   dpref[:]), 16)
            B0D.inc(sync.dma_start(ag_g_in.ap(), dpref[:, NCH * 128:(NCH + 1) * 128]), 16)

        @block.tensor
        def _(tensor):
            for c in range(NCH):
                tensor.wait_ge(B0V.h, c + 1)
                if c > 0:
                    tensor.wait_ge(B0E.h, evB_i[c - 1])
                tensor.transpose(pp2[:, 0:128], cumA_all[:, c * 128:(c + 1) * 128], c_identf[:])
                tr_i[c] = B0T.inc(tensor.transpose(pp2[:, 128:256],
                                                   DT[:, c * 128:(c + 1) * 128], c_identf[:]))

        @block.scalar
        def _(scalar):
            for c in range(NCH):
                cb = c % 2
                scalar.wait_ge(B0E.h, evB_i[c])
                if c >= 2:
                    scalar.wait_ge(B0V.h, ddt_i[c - 2])
                lnd_i[c] = B0S.inc(scalar.activation(lnd[cb][:], dttb[cb][:], AF.Ln))
                eca_i[c] = B0S.inc(scalar.activation(expcA_all[:, c * 128:(c + 1) * 128],
                                                     catb[cb][:], AF.Exp))
                LB.wait(scalar, c)
                eda_i[c] = B0S.inc(scalar.activation(expdA_all[:, c * 128:(c + 1) * 128],
                                                     lastb[cb][:], AF.Exp))
                scalar.wait_ge(B0V.h, tmb_i[c])
                e2_i[c] = B0S.inc(scalar.activation(e2b[cb][:], tmb[cb][:], AF.Exp))
                assert e2_i[c] == 4 * c + 4 and lnd_i[c] == 4 * c + 1

    barrier(dma_sems=[B0D, LB.s[0], LB.s[1]])
    dbg_dump([(dbg["dbg_cum"], d_cum.ap()[:, :])] if debug else [])
    if upto <= 2:
        bs2.close(); bsA.close(); bsM.close(); es.close()
        return nc, dbg

    B1T = Sm("B1T"); B1V = Sm("B1V"); B1D = Sm("B1D"); EVB = Sm("EVB")
    CCS = Sm("CCS"); SIV = Sm("SIV"); SID = Sm("SID")
    XTS = Stream(Sm, "XTS", 2); BCS = Sm("BCS"); SGL = Stream(Sm, "SGL", 1)
    bs3p = ExitStack()
    p3 = lambda *a, **k: bs3p.enter_context(nc.sbuf_tensor(*a, **k))
    sgb = [p3("sgb0", [128, INTER], BF16)] * 2
    mbx = [p3(f"mbx{i}", [128, 128], F32) for i in range(3)]
    grow = p3("grow", [1, 512], F32)
    rmr = p3("rmr", [1, 384], F32)
    rmcr = p3("rmcr", [1, 384], F32)
    gmod = p3("gmod", [1, 384], F32)
    mrows = p3("mrows", [1, 384], F32)
    pst = bs3p.enter_context(nc.psum_tensor("pst", [128, 2048], F32))
    ptr = bs3p.enter_context(nc.psum_tensor("ptr", [128, 1024], BF16))
    DBV = Sm("DBV")

    bv_sb = lambda c: 10 * c + 2
    bv_xdd = lambda c: 10 * c + 3
    bv_tt2 = lambda c, qq: 10 * c + 8 + qq
    bv_pub = 10 * NCH + 2
    bt_trb = lambda c: 5 * c + 1
    bt_stq = lambda c, qq: 5 * c + 2 + qq
    bx_x = lambda c: 32 * c + 16
    bx_b = lambda c: 32 * c + 32
    sb_i = {}; xdd_i = {}; tt2_i = {}; trb_i = {}; stq_i = {}; evb_i = {}
    xload_i = {}; bload_i = {}; sin_j = {}
    sid_mark = {}
    with nc.Block() as block:
        @block.sync
        def _(sync):
            for c in range(NCH):
                if c > 1:
                    sync.wait_ge(B1V.h, bv_xdd(c - 2))
                if c > 0:
                    sync.wait_ge(B1T.h, bt_trb(c - 1))
                xload_i[c] = XTS.load(sync.dma_start(
                    (xTb if c % 2 == 0 else xT2)[:],
                    d_hbc.ap()[0:INTER, c * 128:(c + 1) * 128], transpose=True), c)
                bload_i[c] = BCS.inc(sync.dma_start(
                    bcm[:].rearrange("p (g t) -> p g t", t=128),
                    d_hbc.ap()[INTER:INTER + 1024, c * 128:(c + 1) * 128]
                    .rearrange("(g n) t -> n g t", n=128)), 16)
            sync.wait_ge(CCS.h, 2)
            SID.inc(sync.dma_start(grow[:], ag_g_out.ap().rearrange("a b -> (a b)").unsqueeze(0)), 16)
            SID.inc(sync.dma_start(rmr[:], rmask), 16)
            sid_mark["gr"] = SID.inc(sync.dma_start(rmcr[:], rmaskc), 16)
            sync.wait_ge(SIV.h, 8)
            n = SID.inc(sync.dma_start(d_m.ap().rearrange("j h -> (j h)").unsqueeze(0),
                                       mrows[:]), 16)
            sync.wait_ge(SID.h, n)
            for j in range(3):
                sid_mark[f"mb{j}"] = SID.inc(sync.dma_start(
                    mbx[j][:], d_m.ap()[j].unsqueeze(0).partition_broadcast(128)), 16)
            for j in range(3):
                if j == 1:
                    sync.wait_ge(SIV.h, 9)
                if j == 2:
                    sync.wait_ge(SIV.h, 11)
                SGL.load(sync.dma_start(
                    sgb[0][:], ag_s_out.ap()[j * 128:(j + 1) * 128, :]), j)

        @block.vector
        def _(vector):
            v0 = B1V.inc(vector.memset(S[:], 0.0))
            for c in range(NCH):
                vector.wait_ge(B1V.h, tt2_i[c - 1][3] if c > 0 else v0)
                if c > 0:
                    vector.wait_ge(B1D.h, 16 * c)
                sb_i[c] = B1V.inc(vector.tensor_copy(Sb[:], S[:]))
                assert sb_i[c] == bv_sb(c), (c, sb_i[c])
                XTS.wait(vector, c)
                xdv = ddtT_all[:, c * 128:(c + 1) * 128].unsqueeze(2).broadcast_to([128, 128, 64])
                if debug and False and c == 1:
                    vector.wait_ge(DBGS.h, DBGS.n + 64)
                xdd_i[c] = B1V.inc(vector.tensor_tensor(
                    xdd[:].rearrange("p (h q) -> p h q", q=64),
                    (xTb if c % 2 == 0 else xT2)[:].rearrange("p (h q) -> p h q", q=64),
                    xdv, ALU.mult))
                tt2_i[c] = {}
                vector.wait_ge(B1V.h, sb_i[c])
                for qq in range(4):
                    dav = expdA_all[:, c * 128 + 32 * qq:c * 128 + 32 * (qq + 1)] \
                        .unsqueeze(2).broadcast_to([128, 32, 64])
                    sl3 = S[:, 2048 * qq:2048 * (qq + 1)].rearrange("p (h q) -> p h q", q=64)
                    B1V.inc(vector.tensor_tensor(sl3, sl3, dav, ALU.mult))
                for qq in range(4):
                    vector.wait_ge(B1T.h, bt_stq(c, qq))
                    vector.wait_ge(B1V.h, 10 * c + 7)
                    tt2_i[c][qq] = B1V.inc(vector.tensor_add(
                        S[:, 2048 * qq:2048 * (qq + 1)], S[:, 2048 * qq:2048 * (qq + 1)], pst[:]))
                    assert tt2_i[c][qq] == bv_tt2(c, qq), (c, qq, tt2_i[c][qq])
            vector.wait_ge(B1V.h, tt2_i[NCH - 1][3])
            vector.wait_ge(B1D.h, 16 * NCH)
            pub_i = B1V.inc(vector.tensor_copy(Sb[:], S[:]))
            assert pub_i == bv_pub
            vector.wait_ge(SID.h, sid_mark["gr"])
            g2 = SIV.inc(vector.tensor_mul(gmod[:, 256:384], grow[:, 256:384], rmr[:, 256:384]))
            vector.wait_ge(SIV.h, g2)
            g2 = SIV.inc(vector.tensor_add(gmod[:, 256:384], gmod[:, 256:384], rmcr[:, 256:384]))
            g1 = SIV.inc(vector.tensor_mul(gmod[:, 128:256], grow[:, 128:256], rmr[:, 128:256]))
            vector.wait_ge(SIV.h, g1)
            g1 = SIV.inc(vector.tensor_add(gmod[:, 128:256], gmod[:, 128:256], rmcr[:, 128:256]))
            vector.wait_ge(SIV.h, g1)
            SIV.inc(vector.tensor_copy(mrows[:, 256:384], rmr[:, 256:384]))
            SIV.inc(vector.tensor_mul(mrows[:, 128:256], gmod[:, 256:384], rmr[:, 128:256]))
            m0_ = SIV.inc(vector.tensor_mul(mrows[:, 0:128], gmod[:, 256:384], gmod[:, 128:256]))
            vector.wait_ge(SIV.h, m0_)
            m0_ = SIV.inc(vector.tensor_mul(mrows[:, 0:128], mrows[:, 0:128], rmr[:, 0:128]))
            assert m0_ == 8
            vector.wait_ge(SID.h, sid_mark["mb2"])
            for j in range(3):
                SGL.wait(vector, j)
                mv = mbx[j][:].unsqueeze(2).broadcast_to([128, 128, 64])
                sgv = sgb[0][:].rearrange("p (h q) -> p h q", q=64)
                if j == 0:
                    sin_j[j] = SIV.inc(vector.tensor_tensor(
                        S[:].rearrange("p (h q) -> p h q", q=64), sgv, mv, ALU.mult))
                    assert sin_j[0] == 9
                else:
                    t_ = SIV.inc(vector.tensor_tensor(
                        ysb[:].rearrange("p (h q) -> p h q", q=64), sgv, mv, ALU.mult))
                    vector.wait_ge(SIV.h, t_)
                    sin_j[j] = SIV.inc(vector.tensor_add(S[:], S[:], ysb[:]))
                vector.wait_ge(SIV.h, sin_j[j])
            vector.wait_ge(B1D.h, 16 * (NCH + 1))
            SIV.inc(vector.tensor_copy(Sb[:], S[:]))

        @block.tensor
        def _(tensor):
            for c in range(NCH):
                tensor.wait_ge(BCS.h, 16 * (c + 1))
                if c > 0:
                    tensor.wait_ge(EVB.h, c)
                for g in range(G):
                    inst = tensor.transpose(ptr[:, g * 128:(g + 1) * 128],
                                            bcm[:, g * 128:(g + 1) * 128], c_identb[:])
                trb_i[c] = B1T.inc(inst)
                assert trb_i[c] == bt_trb(c)
                tensor.wait_ge(EVB.h, c + 1)
                for qq in range(4):
                    tensor.wait_ge(B1V.h, bv_xdd(c))
                    prev = bv_tt2(c - 1, 3) if (qq == 0 and c > 0) else \
                        (bv_tt2(c, qq - 1) if qq > 0 else 0)
                    if prev:
                        tensor.wait_ge(B1V.h, prev)
                    for gg in range(2):
                        g = 2 * qq + gg
                        for jj in range(2):
                            inst = tensor.matmul(
                                pst[:, gg * 1024 + jj * 512:gg * 1024 + (jj + 1) * 512],
                                bT[:, g * 128:(g + 1) * 128],
                                xdd[:, g * 1024 + jj * 512:g * 1024 + (jj + 1) * 512],
                                start=True, stop=True)
                    stq_i[(c, qq)] = B1T.inc(inst)
                    assert stq_i[(c, qq)] == bt_stq(c, qq)

        @block.scalar
        def _(scalar):
            for c in range(NCH):
                scalar.wait_ge(B1T.h, trb_i[c])
                if c > 0:
                    scalar.wait_ge(B1T.h, stq_i[(c - 1, 3)])
                evb_i[c] = EVB.inc(scalar.copy(bT[:], ptr[:]))

        @block.gpsimd
        def _(gpsimd):
            pass
            for c in range(NCH):
                gpsimd.wait_ge(B1V.h, bv_sb(c))
                B1D.inc(gpsimd.dma_start(d_spref.ap()[c], Sb[:]), 16)
            gpsimd.wait_ge(B1V.h, bv_pub)
            B1D.inc(gpsimd.dma_start(ag_s_in.ap(), Sb[:]), 16)
            gpsimd.wait_ge(B1D.h, B1D.n)
            gpsimd.wait_ge(B0D.h, B0D.n)
            CCS.inc(gpsimd.collective_compute(
                "AllGather", ALU.bypass, replica_groups=[[0, 1, 2, 3], [4, 5, 6, 7]],
                ins=[ag_s_in.ap().opt()], outs=[ag_s_out.ap().opt()]))
            CCS.inc(gpsimd.collective_compute(
                "AllGather", ALU.bypass, replica_groups=[[0, 1, 2, 3], [4, 5, 6, 7]],
                ins=[ag_g_in.ap().opt()], outs=[ag_g_out.ap().opt()]))
            gpsimd.wait_ge(CCS.h, 2)

    barrier(dma_sems=[XTS.s[0], XTS.s[1], BCS, B1D, SID])
    dbg_dump([(dbg["dbg_spref"], d_spref.ap()[:, :, :]), (dbg["dbg_spub"], ag_s_in.ap()),
              (dbg["dbg_sin"], Sb[:])] if debug else [])
    bs3p.close()
    if upto <= 3:
        bs2.close(); bsA.close(); bsM.close(); es.close()
        return nc, dbg

    bs2.close()

    B2T = Sm("B2T"); B2V = Sm("B2V"); B2S = Sm("B2S"); B2D = Sm("B2D")
    X4 = Sm("X4"); BC4 = Sm("BC4"); SP4 = Sm("SP4"); DB4 = Sm("DB4")
    DP4 = Stream(Sm, "DP4", 1); RR4 = Stream(Sm, "RR4", 2)
    bs4 = ExitStack()
    p4 = lambda *a, **k: bs4.enter_context(nc.sbuf_tensor(*a, **k))
    ccm = p4("ccm", [128, G * 128], BF16)
    rr = [p4(f"rr{i}", [128, 2048], F32) for i in range(2)]
    spc = p4("spc", [128, INTER], BF16)
    suse = p4("suse", [128, INTER], BF16)
    xdc = p4("xdc", [128, INTER], BF16)
    yrb = p4("yrb", [128, INTER], BF16)
    dbc = p4("dbc", [128, INTER], BF16)
    gmt = p4("gmt", [128, G * 128], F32)
    dcb = [p4(f"dcb{i}", [128, 512], F32) for i in range(4)]
    mbb = [p4(f"mbb{i}", [128, 512], BF16) for i in range(4)]
    ucb = [p4(f"ucb{i}", [128, 512], F32) for i in range(2)]
    dpb = [p4("dpb0", [128, 128], F32)] * 2
    pgm = bs4.enter_context(nc.psum_tensor("pgm", [128, 1024], F32))
    pyd = [bs4.enter_context(nc.psum_tensor(f"pyd{i}", [128, 512], F32)) for i in range(2)]
    pyo = [bs4.enter_context(nc.psum_tensor(f"pyo{i}", [128, 512], F32)) for i in range(2)]

    v_tu = lambda c: 69 * c + 1
    v_su = lambda c: 69 * c + 2
    v_xdc = lambda c: 69 * c + 3
    v_gmt = lambda c: 69 * c + 4
    v_mp = lambda c, a: 69 * c + (5 + a if a < 2 else 4 * (a // 2) + 3 + (a % 2))
    v_tt1 = lambda c, b: 69 * c + (4 * b + 9 if b < 15 else 67)
    v_tt2 = lambda c, b: 69 * c + (4 * b + 10 if b < 15 else 68)
    v_yr = lambda c: 69 * c + 69
    t_gm = lambda c: 17 * c + 1
    t_bank = lambda c, b: 17 * c + 2 + b
    s_pack = lambda c, a: 35 * c + a + 1
    s_sq = lambda c: 35 * c + 33
    s_rstd = lambda c: 35 * c + 35
    xm = {}

    with nc.Block() as block:
        @block.sync
        def _(sync):
            DB4.inc(sync.dma_start(dbc[:], dexp.unsqueeze(0).partition_broadcast(128)), 16)
            for c in range(b2n):
                if c > 0:
                    sync.wait_ge(B2T.h, t_bank(c - 1, 15))
                    sync.wait_ge(B2V.h, v_xdc(c - 1))
                xm[(c, "xT")] = X4.inc(sync.dma_start(
                    xTb[:], d_hbc.ap()[0:INTER, c * 128:(c + 1) * 128], transpose=True), 16)
                if c > 0:
                    sync.wait_ge(B2T.h, t_gm(c - 1))
                xm[(c, "bc")] = BC4.inc(sync.dma_start(
                    bcm[:].rearrange("p (g t) -> p g t", t=128),
                    d_hbc.ap()[INTER:INTER + 1024, c * 128:(c + 1) * 128]
                    .rearrange("(g n) t -> n g t", n=128)), 16)
                xm[(c, "cc")] = BC4.inc(sync.dma_start(
                    ccm[:].rearrange("p (g t) -> p g t", t=128),
                    d_hbc.ap()[INTER + 1024:INTER + 2048, c * 128:(c + 1) * 128]
                    .rearrange("(g n) t -> n g t", n=128)), 16)
                if c > 0:
                    sync.wait_ge(B2V.h, v_su(c - 1))
                xm[(c, "sp")] = SP4.inc(sync.dma_start(spc[:], d_spref.ap()[c]), 16)
                if c > 0:
                    sync.wait_ge(B2V.h, v_tu(c - 1))
                DP4.load(sync.dma_start(
                    dpb[c % 2][:], d_dpref.ap()[c].unsqueeze(0).partition_broadcast(128)), c)
                for q in range(8):
                    if c > 0 or q >= 2:
                        pc, pq = (c, q - 2) if q >= 2 else (c - 1, q + 6)
                        sync.wait_ge(B2S.h, s_pack(pc, 4 * pq + 3))
                    RR4.load(sync.dma_start(
                        rr[q % 2][:],
                        d_cum.ap()[c][16 * 128 * q:16 * 128 * (q + 1)]
                        .unsqueeze(0).partition_broadcast(128)), 8 * c + q)

        @block.scalar
        def _(scalar):
            def rms(c):
                scalar.wait_ge(B2V.h, v_tt2(c, 15))
                scalar.wait_ge(B2T.h, t_bank(c, 15))
                n = B2S.inc(scalar.activation(xdc[:], ysb[:], AF.Square,
                                              accum_out=stat[:, 8:9]))
                assert n == s_sq(c), (c, n)
                scalar.wait_ge(B2S.h, n)
                n = B2S.inc(scalar.activation(stat[:, 9:10], stat[:, 8:9], AF.Ln,
                                              bias=c_eps[:], scale=1.0 / INTER))
                scalar.wait_ge(B2S.h, n)
                n = B2S.inc(scalar.activation(stat[:, 10:11], stat[:, 9:10], AF.Exp, scale=-0.5))
                assert n == s_rstd(c), (c, n)

            for c in range(b2n):
                if c >= 1:
                    rms(c - 1)
                for a in range(32):
                    q = a // 4
                    RR4.wait(scalar, 8 * c + q)
                    if c > 0 or a >= 4:
                        pc, pa = (c, a - 4) if a >= 4 else (c - 1, a + 28)
                        scalar.wait_ge(B2V.h, v_mp(pc, pa))
                    for j in range(4):
                        h = 4 * a + j
                        inst = scalar.activation(
                            dcb[a % 4][:, j * 128:(j + 1) * 128],
                            rr[q % 2][:, (h % 16) * 128:(h % 16 + 1) * 128],
                            AF.Exp, bias=biasT_all[:, c * 128 + h:c * 128 + h + 1])
                    n = B2S.inc(inst)
                    assert n == s_pack(c, a)
                pass

            if b2n > 0:
                rms(b2n - 1)

        @block.vector
        def _(vector):
            for c in range(b2n):
                DP4.wait(vector, c)
                if c > 0:
                    vector.wait_ge(B2V.h, v_yr(c - 1))
                    vector.wait_ge(B2D.h, 16 * (c))
                dpv = dpb[c % 2][:].unsqueeze(2).broadcast_to([128, 128, 64])
                n = B2V.inc(vector.tensor_tensor(
                    ysb[:].rearrange("p (h q) -> p h q", q=64),
                    Sb[:].rearrange("p (h q) -> p h q", q=64), dpv, ALU.mult))
                assert n == v_tu(c)
                vector.wait_ge(B2V.h, n)
                vector.wait_ge(SP4.h, 16 * (c + 1))
                if c > 0:
                    vector.wait_ge(B2T.h, t_bank(c - 1, 15))
                n = B2V.inc(vector.tensor_add(suse[:], ysb[:], spc[:]))
                assert n == v_su(c)
                vector.wait_ge(X4.h, 16 * (c + 1))
                vector.wait_ge(DB4.h, 16)
                if c > 0:
                    vector.wait_ge(B2S.h, s_sq(c - 1))
                n = B2V.inc(vector.tensor_mul(xdc[:], xTb[:], dbc[:]))
                assert n == v_xdc(c)
                vector.wait_ge(B2T.h, t_gm(c))
                tv = c_triu[:].unsqueeze(1).broadcast_to([128, 8, 128])
                n = B2V.inc(vector.tensor_tensor(
                    gmt[:].rearrange("p (g l) -> p g l", l=128),
                    pgm[:].rearrange("p (g l) -> p g l", l=128), tv, ALU.mult))
                assert n == v_gmt(c)
                vector.wait_ge(B2V.h, n)
                def emit_pack(a):
                    g = a // 4
                    vector.wait_ge(B2S.h, s_pack(c, a))
                    if c > 0 or a >= 4:
                        pc, pa = (c, a - 4) if a >= 4 else (c - 1, a + 28)
                        vector.wait_ge(B2T.h, t_bank(pc, pa // 2))
                    gv = gmt[:, g * 128:(g + 1) * 128].unsqueeze(1).broadcast_to([128, 4, 128])
                    n = B2V.inc(vector.scalar_tensor_tensor(
                        mbb[a % 4][:].rearrange("p (j l) -> p j l", l=128),
                        dcb[a % 4][:].rearrange("p (j l) -> p j l", l=128),
                        CLAMP, gv, ALU.min, ALU.mult))
                    assert n == v_mp(c, a), (c, a, n)

                def emit_evac(b):
                    vector.wait_ge(B2T.h, t_bank(c, b))
                    if c > 0 or b >= 2:
                        pc, pb = (c, b - 2) if b >= 2 else (c - 1, b + 14)
                        vector.wait_ge(B2V.h, v_tt2(pc, pb))
                    sdv = expcA_all[:, c * 128 + 8 * b:c * 128 + 8 * b + 8] \
                        .unsqueeze(2).broadcast_to([128, 8, 64])
                    n = B2V.inc(vector.tensor_tensor(
                        ucb[b % 2][:].rearrange("p (j q) -> p j q", q=64),
                        pyo[b % 2][:].rearrange("p (j q) -> p j q", q=64), sdv, ALU.mult))
                    assert n == v_tt1(c, b), (c, b, n)
                    vector.wait_ge(B2V.h, n)
                    n = B2V.inc(vector.tensor_add(ysb[:, 512 * b:512 * (b + 1)],
                                                  ucb[b % 2][:], pyd[b % 2][:]))
                    assert n == v_tt2(c, b), (c, b, n)

                for b in range(16):
                    emit_pack(2 * b)
                    emit_pack(2 * b + 1)
                    if b >= 1:
                        emit_evac(b - 1)
                emit_evac(15)
                vector.wait_ge(B2S.h, s_rstd(c))
                if c > 0:
                    vector.wait_ge(B2D.h, 16 * c)
                n = B2V.inc(vector.tensor_scalar(yrb[:], ysb[:], stat[:, 10:11], None, ALU.mult))
                assert n == v_yr(c)

        @block.tensor
        def _(tensor):
            for c in range(b2n):
                tensor.wait_ge(BC4.h, 32 * (c + 1))
                if c > 0:
                    tensor.wait_ge(B2V.h, v_gmt(c - 1))
                for g in range(G):
                    inst = tensor.matmul(pgm[:, g * 128:(g + 1) * 128],
                                         bcm[:, g * 128:(g + 1) * 128],
                                         ccm[:, g * 128:(g + 1) * 128],
                                         start=True, stop=True)
                n = B2T.inc(inst)
                assert n == t_gm(c)
                for b in range(16):
                    g = b // 2
                    tensor.wait_ge(B2V.h, v_mp(c, 2 * b + 1))
                    if c > 0 or b >= 2:
                        pc, pb = (c, b - 2) if b >= 2 else (c - 1, b + 14)
                        tensor.wait_ge(B2V.h, v_tt2(pc, pb))
                    for hh in range(8):
                        h = 8 * b + hh
                        a = h // 4
                        tensor.matmul(pyd[b % 2][:, hh * 64:(hh + 1) * 64],
                                      mbb[a % 4][:, (h % 4) * 128:(h % 4 + 1) * 128],
                                      xTb[:, h * 64:(h + 1) * 64],
                                      start=(hh == 0), stop=False)
                    tensor.matmul(pyd[b % 2][:, 0:512], c_identb[:],
                                  xdc[:, 512 * b:512 * (b + 1)], start=False, stop=True)
                    inst = tensor.matmul(pyo[b % 2][:, 0:512], ccm[:, g * 128:(g + 1) * 128],
                                         suse[:, 512 * b:512 * (b + 1)], start=True, stop=True)
                    n = B2T.inc(inst)
                    assert n == t_bank(c, b)

        @block.gpsimd
        def _(gpsimd):
            for c in range(b2n):
                gpsimd.wait_ge(B2V.h, v_yr(c))
                B2D.inc(gpsimd.dma_start(d_yrms.ap()[c * 128:(c + 1) * 128, :], yrb[:]), 16)
            gpsimd.wait_ge(B2D.h, B2D.n)

    barrier(dma_sems=[X4, BC4, SP4, DB4, B2D])
    dbg_dump([(dbg["dbg_yrms"], d_yrms.ap()[:, :])] if debug else [])
    bs4.close()
    if upto <= 4:
        bsA.close(); bsM.close(); es.close()
        return nc, dbg

    bsA.close(); bsM.close()

    YGV = Sm("YGV"); OPK = Sm("OPK"); EOP = Sm("EOP")
    YTR = Stream(Sm, "YTR", 2); SGD = Stream(Sm, "SGD", 2)
    OWS = Stream(Sm, "OWS", 2); AOS = Stream(Sm, "AOS", 2)
    bs5 = ExitStack()
    p5 = lambda *a, **k: bs5.enter_context(nc.sbuf_tensor(*a, **k))
    YG = p5("YG", [128, 64 * 1024], BF16)
    ytr = [p5(f"ytr{i}", [128, 1024], BF16) for i in range(2)]
    sgl = [p5(f"sgl{i}", [128, 1024], BF16) for i in range(2)]
    ow = [p5(f"ow{i}", [128, 16 * 512], BF16) for i in range(2)]
    ao = [p5(f"ao{i}", [128, 512], BF16) for i in range(2)]
    pso = bs5.enter_context(nc.psum_tensor("pso", [128, 512 * 8], F32))

    with nc.Block() as block:
        @block.sync
        def _(sync):
            OWS.load(sync.dma_start(ow[0][:], opw[0, 0]), 0)
            OWS.load(sync.dma_start(ow[1][:], opw[0, 1]), 1)
            for j in range(64):
                if j >= 2:
                    sync.wait_ge(YGV.h, j - 1)
                YTR.load(sync.dma_start(ytr[j % 2][:],
                                        d_yrms.ap()[0:1024, j * 128:(j + 1) * 128],
                                        transpose=True), j)
                if j in (31, 47):
                    i = 2 if j == 31 else 3
                    sync.wait_ge(OPK.h, i - 1)
                    OWS.load(sync.dma_start(ow[i % 2][:], opw[i // 4, i % 4]), i)
            for i in range(4, 32):
                sync.wait_ge(OPK.h, i - 1)
                OWS.load(sync.dma_start(ow[i % 2][:], opw[i // 4, i % 4]), i)

        @block.vector
        def _(vector):
            for j in range(64):
                YTR.wait(vector, j)
                SGD.wait(vector, j)
                YGV.inc(vector.tensor_mul(YG[:, j * 1024:(j + 1) * 1024],
                                          ytr[j % 2][:], sgl[j % 2][:]))

        @block.tensor
        def _(tensor):
            for dj in range(8):
                for kg in range(4):
                    i = 4 * dj + kg
                    OWS.wait(tensor, i)
                    for kt in range(16):
                        k = 16 * kg + kt
                        tensor.wait_ge(YGV.h, k + 1)
                        for t in range(8):
                            if k == 0 and dj > 0:
                                tensor.wait_ge(EOP.h, 8 * (dj - 1) + t + 1)
                            inst = tensor.matmul(
                                pso[:, t * 512:(t + 1) * 512],
                                YG[:, k * 1024 + t * 128:k * 1024 + (t + 1) * 128],
                                ow[i % 2][:, kt * 512:(kt + 1) * 512],
                                start=(k == 0), stop=(k == 63))
                    OPK.inc(inst)

        @block.scalar
        def _(scalar):
            for dj in range(8):
                scalar.wait_ge(OPK.h, 4 * dj + 4)
                for t in range(8):
                    i = 8 * dj + t
                    AOS.wait(scalar, i - 2)
                    EOP.inc(scalar.activation(ao[i % 2][:], pso[:, t * 512:(t + 1) * 512],
                                              AF.Copy))

        @block.gpsimd
        def _(gpsimd):
            for j in range(64):
                if j >= 2:
                    gpsimd.wait_ge(YGV.h, j - 1)
                SGD.load(gpsimd.dma_start(sgl[j % 2][:],
                                          d_sg.ap()[j * 128:(j + 1) * 128, :]), j)
            for dj in range(8):
                for t in range(8):
                    i = 8 * dj + t
                    gpsimd.wait_ge(EOP.h, i + 1)
                    AOS.load(gpsimd.dma_start(
                        d_attn.ap()[t * 128:(t + 1) * 128, dj * 512:(dj + 1) * 512],
                        ao[i % 2][:]), i)
            for b in range(2):
                gpsimd.wait_ge(AOS.s[b].h, AOS.s[b].n)

    barrier(dma_sems=[])
    dbg_dump([(dbg["dbg_attn"], d_attn.ap()[:, :])] if debug else [])
    bs5.close()
    if upto <= 5:
        es.close()
        return nc, dbg

    DS = Sm("DS"); DV = Sm("DV")
    ATL = Stream(Sm, "ATL", 2); XBL = Stream(Sm, "XBL", 2)
    XNS = Stream(Sm, "XNS", 2); XL2S = Stream(Sm, "XL2S", 2)
    bs6 = ExitStack()
    p6 = lambda *a, **k: bs6.enter_context(nc.sbuf_tensor(*a, **k))
    atb = [p6(f"atb{i}", [128, DIM], BF16) for i in range(2)]
    xbb = [p6(f"xbb{i}", [128, DIM], F32) for i in range(2)]
    xnb = [p6(f"xnb{i}", [128, DIM], F32) for i in range(2)]
    xl2b = [p6(f"xl2b{i}", [128, DIM], BF16) for i in range(2)]
    tb6 = p6("tb6", [128, DIM], F32)
    dump6 = p6("dump6", [128, DIM], F32)

    with nc.Block() as block:
        @block.sync
        def _(sync):
            for i in range(8):
                if i >= 2:
                    sync.wait_ge(DV.h, 11 * (i - 1))
                    sync.wait_ge(DS.h, 6 * (i - 1))
                ATL.load(sync.dma_start(atb[i % 2][:], d_attn.ap()[i * 128:(i + 1) * 128, :]), i)
                XBL.load(sync.dma_start(xbb[i % 2][:], xh[3 + i * 128:3 + (i + 1) * 128, :]), i)

        @block.scalar
        def _(scalar):
            for i in range(8):
                b = i % 2
                o = 16 * (i % 2)
                ATL.wait(scalar, i)
                n = DS.inc(scalar.activation(dump6[:], atb[b][:], AF.Square,
                                             accum_out=stat[:, o + 0:o + 1]))
                scalar.wait_ge(DV.h, 11 * i + 4)
                n = DS.inc(scalar.activation(stat[:, o + 11:o + 12], stat[:, o + 3:o + 4], AF.Ln, bias=c_eps[:]))
                scalar.wait_ge(DS.h, n)
                DS.inc(scalar.activation(stat[:, o + 4:o + 5], stat[:, o + 11:o + 12], AF.Exp, scale=-0.5))
                scalar.wait_ge(DV.h, 11 * i + 6)
                n = DS.inc(scalar.activation(dump6[:], xnb[b][:], AF.Square,
                                             accum_out=stat[:, o + 5:o + 6]))
                scalar.wait_ge(DV.h, 11 * i + 10)
                n = DS.inc(scalar.activation(stat[:, o + 12:o + 13], stat[:, o + 7:o + 8], AF.Ln, bias=c_eps[:]))
                scalar.wait_ge(DS.h, n)
                DS.inc(scalar.activation(stat[:, o + 8:o + 9], stat[:, o + 12:o + 13], AF.Exp, scale=-0.5))

        @block.vector
        def _(vector):
            for i in range(8):
                b = i % 2
                o = 16 * (i % 2)
                ATL.wait(vector, i)
                if i >= 2:
                    vector.wait_ge(DS.h, 6 * (i - 2) + 6)
                n = DV.inc(vector.tensor_reduce(stat[:, o + 1:o + 2], atb[b][:], AX.X, ALU.add))
                vector.wait_ge(DV.h, n)
                n = DV.inc(vector.tensor_scalar(stat[:, o + 2:o + 3], stat[:, o + 1:o + 2], 1.0 / DIM, None, ALU.mult))
                vector.wait_ge(DV.h, n)
                n = DV.inc(vector.tensor_mul(stat[:, o + 13:o + 14], stat[:, o + 2:o + 3], stat[:, o + 2:o + 3]))
                vector.wait_ge(DV.h, n)
                vector.wait_ge(DS.h, 6 * i + 1)
                n = DV.inc(vector.tensor_scalar(stat[:, o + 3:o + 4], stat[:, o + 0:o + 1], 1.0 / DIM,
                                                stat[:, o + 13:o + 14], ALU.mult, ALU.subtract))
                assert n == 11 * i + 4
                vector.wait_ge(DS.h, 6 * i + 3)
                n = DV.inc(vector.tensor_scalar(tb6[:], atb[b][:], stat[:, o + 2:o + 3],
                                                stat[:, o + 4:o + 5], ALU.subtract, ALU.mult))
                XBL.wait(vector, i)
                vector.wait_ge(DV.h, n)
                XNS.wait(vector, i - 2)
                n = DV.inc(vector.tensor_add(xnb[b][:], tb6[:], xbb[b][:]))
                assert n == 11 * i + 6
                vector.wait_ge(DV.h, n)
                n = DV.inc(vector.tensor_reduce(stat[:, o + 6:o + 7], xnb[b][:], AX.X, ALU.add))
                vector.wait_ge(DV.h, n)
                n = DV.inc(vector.tensor_scalar(stat[:, o + 14:o + 15], stat[:, o + 6:o + 7], 1.0 / DIM, None, ALU.mult))
                vector.wait_ge(DV.h, n)
                n = DV.inc(vector.tensor_mul(stat[:, o + 15:o + 16], stat[:, o + 14:o + 15], stat[:, o + 14:o + 15]))
                vector.wait_ge(DV.h, n)
                vector.wait_ge(DS.h, 6 * i + 4)
                n = DV.inc(vector.tensor_scalar(stat[:, o + 7:o + 8], stat[:, o + 5:o + 6], 1.0 / DIM,
                                                stat[:, o + 15:o + 16], ALU.mult, ALU.subtract))
                assert n == 11 * i + 10
                vector.wait_ge(DS.h, 6 * i + 6)
                XL2S.wait(vector, i - 2)
                vector.wait_ge(DV.h, n)
                n = DV.inc(vector.tensor_scalar(xl2b[b][:], xnb[b][:], stat[:, o + 14:o + 15],
                                                stat[:, o + 8:o + 9], ALU.subtract, ALU.mult))
                assert n == 11 * i + 11

        @block.gpsimd
        def _(gpsimd):
            for i in range(8):
                gpsimd.wait_ge(DV.h, 11 * i + 6)
                XNS.load(gpsimd.dma_start(d_xnew.ap()[i * 128:(i + 1) * 128, :],
                                          xnb[i % 2][:]), i)
                gpsimd.wait_ge(DV.h, 11 * i + 11)
                XL2S.load(gpsimd.dma_start(d_xln2.ap()[i * 128:(i + 1) * 128, :],
                                           xl2b[i % 2][:]), i)
            for st_ in (XNS, XL2S):
                for b in range(2):
                    gpsimd.wait_ge(st_.s[b].h, st_.s[b].n)

    barrier(dma_sems=[])
    dbg_dump([(dbg["dbg_xnew"], d_xnew.ap()[:, :]), (dbg["dbg_xln2"], d_xln2.ap()[:, :])]
             if debug else [])
    bs6.close()
    if upto <= 6:
        es.close()
        return nc, dbg

    EXF = Sm("EXF"); EC = EXF; ERT = Sm("ERT"); ERV = Sm("ERV"); ERS = Sm("ERS")
    RTT = Sm("RTT"); RTE = Sm("RTE"); EPK = Sm("EPK"); EFE = Sm("EFE")
    ES = Sm("ES"); EV = Sm("EV"); OUTS = Sm("OUTS")
    BWS = Stream(Sm, "BWS", 2); BTS = Stream(Sm, "BTS", 2); XNL = Stream(Sm, "XNL", 1)
    FFS = Stream(Sm, "FFS", 2); FFL = Stream(Sm, "FFL", 2)
    bs7 = ExitStack()
    p7 = lambda *a, **k: bs7.enter_context(nc.sbuf_tensor(*a, **k))
    x2fm = p7("x2fm", [128, KT * 1024], BF16)
    bbr = p7("bbr", [1, DIM], BF16)
    RH = p7("RH", [128, 8 * 64], F32)
    PT = p7("PT", [128, 8 * 16], F32)
    CMB = p7("CMB", [128, 8 * 16], F32)
    RSC = p7("RSC", [128, 64], BF16)
    RT = p7("RT", [64, 128 * 128], BF16)

    bs7p = ExitStack()
    arr = bs7p.enter_context(nc.sbuf_tensor("arr", [128, KT * 80], BF16))
    dump7a = bs7p.enter_context(nc.sbuf_tensor("dump7a", [128, 64], F32))
    psr = bs7p.enter_context(nc.psum_tensor("psr", [128, 4 * 512], F32))
    ptb = bs7p.enter_context(nc.psum_tensor("ptb", [64, 2048], BF16))
    with nc.Block() as block:
        @block.sync
        def _(sync):
            for k in range(KT):
                EXF.inc(sync.dma_start(x2fm[:, k * 1024:(k + 1) * 1024],
                                       d_xln2.ap()[0:1024, k * 128:(k + 1) * 128],
                                       transpose=True), 16)
            EC.inc(sync.dma_start(arr[:], arw), 16)
            EC.inc(sync.dma_start(bbr[:], bbias), 16)

        @block.tensor
        def _(tensor):
            tensor.wait_ge(EXF.h, EXF.n)
            for t in range(8):
                if t >= 4:
                    tensor.wait_ge(ERS.h, 2 * (t - 4) + 2)
                for k in range(KT):
                    inst = tensor.matmul(psr[:, (t % 4) * 512:(t % 4) * 512 + 80],
                                         x2fm[:, k * 1024 + t * 128:k * 1024 + (t + 1) * 128],
                                         arr[:, k * 80:(k + 1) * 80],
                                         start=(k == 0), stop=(k == KT - 1))
                ERT.inc(inst)
            for t in range(8):
                for e in range(16):
                    i = 16 * t + e
                    tensor.wait_ge(ERV.h, 104 + i + 1)
                    if i >= 16:
                        tensor.wait_ge(RTE.h, i // 8 - 1)
                    inst = tensor.transpose(ptb[:, (i % 16) * 128:(i % 16 + 1) * 128],
                                            RSC[:], c_identb[:])
                    RTT.inc(inst)

        @block.vector
        def _(vector):
            def part1(t):
                vector.wait_ge(ERT.h, t + 1)
                if t > 0:
                    vector.wait_ge(ERS.h, 2 * (t - 1) + 2)
                n = ERV.inc(vector.tensor_copy(RH[:, t * 64:(t + 1) * 64],
                                               psr[:, (t % 4) * 512:(t % 4) * 512 + 64]))
                vector.wait_ge(ERV.h, n)
                n = ERV.inc(vector.reduce_max(stat[:, 0:1],
                                              psr[:, (t % 4) * 512 + 64:(t % 4) * 512 + 80], AX.X))
                assert n == 13 * t + 2, (t, n)

            def top2(t):
                vector.wait_ge(ERS.h, 2 * t + 2)
                pt = PT[:, t * 16:(t + 1) * 16]
                n = ERV.inc(vector.reduce_max(stat[:, 2:3], pt, AX.X))
                vector.wait_ge(ERV.h, n)
                n = ERV.inc(vector.tensor_scalar(dump7a[:, 0:16], pt, stat[:, 2:3], None, ALU.is_lt))
                vector.wait_ge(ERV.h, n)
                n = ERV.inc(vector.tensor_mul(dump7a[:, 16:32], pt, dump7a[:, 0:16]))
                vector.wait_ge(ERV.h, n)
                n = ERV.inc(vector.reduce_max(stat[:, 3:4], dump7a[:, 16:32], AX.X))
                vector.wait_ge(ERV.h, n)
                n = ERV.inc(vector.tensor_add(stat[:, 4:5], stat[:, 2:3], stat[:, 3:4]))
                vector.wait_ge(ERV.h, n)
                n = ERV.inc(vector.tensor_scalar(stat[:, 5:6], stat[:, 1:2], 1e-6,
                                                 stat[:, 4:5], ALU.mult, ALU.add))
                assert n == 13 * t + 8, (t, n)
                vector.wait_ge(ERV.h, n)
                n = ERV.inc(vector.tensor_scalar(stat[:, 5:6], stat[:, 5:6], 2.0, None, ALU.mult))
                vector.wait_ge(ERV.h, n)
                n = ERV.inc(vector.reciprocal(stat[:, 6:7], stat[:, 5:6]))
                vector.wait_ge(ERV.h, n)
                n = ERV.inc(vector.tensor_scalar(dump7a[:, 32:48], pt, stat[:, 3:4], None, ALU.is_ge))
                vector.wait_ge(ERV.h, n)
                n = ERV.inc(vector.tensor_mul(dump7a[:, 48:64], pt, dump7a[:, 32:48]))
                vector.wait_ge(ERV.h, n)
                n = ERV.inc(vector.tensor_scalar(CMB[:, t * 16:(t + 1) * 16], dump7a[:, 48:64],
                                                 stat[:, 6:7], None, ALU.mult))
                assert n == 13 * t + 13, (t, n)

            part1(0)
            for t in range(1, 8):
                top2(t - 1)
                part1(t)
            top2(7)
            def rt_evac(p):
                vector.wait_ge(RTT.h, 8 * (p + 1))
                RTE.inc(vector.tensor_copy(
                    RT[:, p * 1024:(p + 1) * 1024],
                    ptb[:, (p % 2) * 1024:(p % 2 + 1) * 1024]))

            for t in range(8):
                for e in range(16):
                    i = 16 * t + e
                    if i >= 8 and i % 8 == 0:
                        rt_evac(i // 8 - 1)
                    if i > 0:
                        vector.wait_ge(RTT.h, i)
                    n = ERV.inc(vector.tensor_scalar(RSC[:], RH[:, t * 64:(t + 1) * 64],
                                                     CMB[:, t * 16 + e:t * 16 + e + 1],
                                                     None, ALU.mult))
                    assert n == 104 + i + 1, (i, n)
            rt_evac(15)

        @block.scalar
        def _(scalar):
            for t in range(8):
                scalar.wait_ge(ERV.h, 13 * t + 2)
                if t > 0:
                    scalar.wait_ge(ERV.h, 13 * (t - 1) + 8)
                n = ERS.inc(scalar.activation(stat[:, 7:8], stat[:, 0:1], AF.Copy, scale=-1.0))
                scalar.wait_ge(ERS.h, n)
                n = ERS.inc(scalar.activation(PT[:, t * 16:(t + 1) * 16],
                                              psr[:, (t % 4) * 512 + 64:(t % 4) * 512 + 80],
                                              AF.Exp, bias=stat[:, 7:8],
                                              accum_out=stat[:, 1:2]))
                assert n == 2 * t + 2

        @block.gpsimd
        def _(gpsimd):
            pass

    bs7p.close()

    bs7q = ExitStack()
    q7 = lambda *a, **k: bs7q.enter_context(nc.sbuf_tensor(*a, **k))
    bwb = [q7(f"bwb{i}", [128, 8 * 512], BF16) for i in range(2)]
    btb = [q7(f"btb{i}", [64, 8 * 512], BF16) for i in range(2)]
    xnl = [q7("xnl0", [128, DIM], F32)] * 2
    ob = [q7("ob0", [128, DIM], F32)] * 2
    fs = [q7(f"fs{i}", [128, 512], BF16) for i in range(2)]
    fol = [q7(f"fol{i}", [128, DIM], BF16) for i in range(2)]
    dump7 = q7("dump7", [128, DIM], F32)
    pso2 = bs7q.enter_context(nc.psum_tensor("pso2", [128, 8 * 512], F32))
    with nc.Block() as block:
        @block.sync
        def _(sync):
            for oj in range(8):
                for q in range(4):
                    i = 4 * oj + q
                    if i >= 2:
                        prev = i - 2
                        sync.wait_ge(EPK.h, 6 * (prev // 4) + (prev % 4) + 1)
                    BWS.load(sync.dma_start(
                        bwb[i % 2][:], bw[oj, q // 2][:, (q % 2) * 4096:(q % 2 + 1) * 4096]), i)
                for hh in range(2):
                    j = 2 * oj + hh
                    if j >= 2:
                        prev = j - 2
                        sync.wait_ge(EPK.h, 6 * (prev // 2) + 5 + (prev % 2))
                    BTS.load(sync.dma_start(
                        btb[j % 2][:], betw[oj][:, hh * 4096:(hh + 1) * 4096]), j)
            for b_ in range(2):
                sync.wait_ge(FFS.s[b_].h, 16 * 32)
            for i in range(8):
                if i >= 1:
                    sync.wait_ge(EV.h, 6 * i)
                FFL.load(sync.dma_start(fol[i % 2][:], d_ffn.ap()[i * 128:(i + 1) * 128, :]), i)
                XNL.load(sync.dma_start(xnl[i % 2][:], d_xnew.ap()[i * 128:(i + 1) * 128, :]), i)

        @block.tensor
        def _(tensor):
            tensor.wait_ge(RTE.h, 16)
            for oj in range(8):
                for q in range(4):
                    i = 4 * oj + q
                    BWS.wait(tensor, i)
                    for kt in range(8):
                        k = 8 * q + kt
                        for t in range(8):
                            if k == 0 and oj > 0:
                                tensor.wait_ge(EFE.h, 8 * (oj - 1) + t + 1)
                            inst = tensor.matmul(
                                pso2[:, t * 512:(t + 1) * 512],
                                x2fm[:, k * 1024 + t * 128:k * 1024 + (t + 1) * 128],
                                bwb[i % 2][:, kt * 512:(kt + 1) * 512],
                                start=(k == 0), stop=False)
                    n = EPK.inc(inst)
                    assert n == 6 * oj + q + 1
                for t in range(8):
                    tensor.matmul(pso2[:, t * 512:(t + 1) * 512], c_onesb[:],
                                  bbr[:, oj * 512:(oj + 1) * 512], start=False, stop=False)
                for hh in range(2):
                    j = 2 * oj + hh
                    BTS.wait(tensor, j)
                    for e in range(8):
                        eg = 8 * hh + e
                        for t in range(8):
                            inst = tensor.matmul(
                                pso2[:, t * 512:(t + 1) * 512],
                                RT[:, (t * 16 + eg) * 128:(t * 16 + eg + 1) * 128],
                                btb[j % 2][:, e * 512:(e + 1) * 512],
                                start=False, stop=(eg == 15))
                    n = EPK.inc(inst)
                    assert n == 6 * oj + 5 + hh

        @block.scalar
        def _(scalar):
            for oj in range(8):
                scalar.wait_ge(EPK.h, 6 * oj + 6)
                for t in range(8):
                    i = 8 * oj + t
                    FFS.wait(scalar, i - 2)
                    EFE.inc(scalar.activation(fs[i % 2][:],
                                              pso2[:, t * 512:(t + 1) * 512], AF.Copy))
            for i in range(8):
                FFL.wait(scalar, i)
                if i > 0:
                    scalar.wait_ge(EV.h, 6 * (i - 1) + 6)
                n = ES.inc(scalar.activation(dump7[:], fol[i % 2][:], AF.Square,
                                             accum_out=stat[:, 0:1]))
                assert n == 3 * i + 1
                scalar.wait_ge(EV.h, 6 * i + 4)
                n = ES.inc(scalar.activation(stat[:, 11:12], stat[:, 3:4], AF.Ln, bias=c_eps[:]))
                scalar.wait_ge(ES.h, n)
                n = ES.inc(scalar.activation(stat[:, 4:5], stat[:, 11:12], AF.Exp, scale=-0.5))
                assert n == 3 * i + 3

        @block.vector
        def _(vector):
            for i in range(8):
                b = i % 2
                vector.wait_ge(ES.h, 3 * i + 1)
                if i > 0:
                    vector.wait_ge(ES.h, 3 * (i - 1) + 2)
                n = EV.inc(vector.tensor_reduce(stat[:, 1:2], fol[i % 2][:],
                                                AX.X, ALU.add))
                vector.wait_ge(EV.h, n)
                n = EV.inc(vector.tensor_scalar(stat[:, 2:3], stat[:, 1:2], 1.0 / DIM, None, ALU.mult))
                vector.wait_ge(EV.h, n)
                n = EV.inc(vector.tensor_mul(stat[:, 13:14], stat[:, 2:3], stat[:, 2:3]))
                vector.wait_ge(EV.h, n)
                n = EV.inc(vector.tensor_scalar(stat[:, 3:4], stat[:, 0:1], 1.0 / DIM,
                                                stat[:, 13:14], ALU.mult, ALU.subtract))
                assert n == 6 * i + 4
                vector.wait_ge(ES.h, 3 * i + 3)
                n = EV.inc(vector.tensor_scalar(dump7[:], fol[i % 2][:],
                                                stat[:, 2:3], stat[:, 4:5], ALU.subtract, ALU.mult))
                XNL.wait(vector, i)
                vector.wait_ge(EV.h, n)
                if i >= 1:
                    vector.wait_ge(OUTS.h, 16 * i)
                n = EV.inc(vector.tensor_add(ob[b][:], dump7[:], xnl[b][:]))
                assert n == 6 * i + 6

        @block.gpsimd
        def _(gpsimd):
            for oj in range(8):
                for t in range(8):
                    i = 8 * oj + t
                    gpsimd.wait_ge(EFE.h, i + 1)
                    FFS.load(gpsimd.dma_start(
                        d_ffn.ap()[t * 128:(t + 1) * 128, oj * 512:(oj + 1) * 512],
                        fs[i % 2][:]), i)
            for i in range(8):
                gpsimd.wait_ge(EV.h, 6 * i + 6)
                OUTS.inc(gpsimd.dma_start(out[i * 128:(i + 1) * 128, :], ob[i % 2][:]), 16)
            gpsimd.wait_ge(OUTS.h, 16 * 8)

    bs7q.close()
    bs7.close()
    es.close()
    return nc, dbg

def _prep_shared(i):
    w = {}
    ipw = np.asarray(i["L0_attn_in_proj_w"])
    w["w1"] = np.ascontiguousarray(
        ipw.reshape(FT, 128, KT, 128).transpose(0, 3, 2, 1).reshape(FT, 128, KT * 128)
    ).astype(BF)
    cw = np.asarray(i["L0_attn_conv_w"])
    cd = np.zeros((FT_HBC, 4, 128, 128), np.float32)
    idx = np.arange(128)
    cd[:, :, idx, idx] = cw.reshape(FT_HBC, 128, 4).transpose(0, 2, 1)
    w["convd"] = np.ascontiguousarray(
        cd.transpose(0, 2, 1, 3).reshape(FT_HBC, 128, 4 * 128)).astype(BF)
    w["convb"] = np.ascontiguousarray(np.asarray(i["L0_attn_conv_b"]).reshape(FT_HBC, 128).T).astype(np.float32)
    w["normw"] = np.ascontiguousarray(np.asarray(i["L0_attn_norm_w"]).reshape(FT_GATE, 128).T).astype(np.float32)
    w["dtbias"] = np.asarray(i["L0_attn_dt_bias"]).reshape(128, 1).astype(np.float32)
    w["alog"] = np.asarray(i["L0_attn_A_log"]).reshape(128, 1).astype(np.float32)
    w["dexp"] = np.repeat(np.asarray(i["L0_attn_D"]).astype(np.float32), P).astype(BF)
    opw = np.asarray(i["L0_attn_out_proj_w"])
    w["opw"] = np.ascontiguousarray(
        opw.T.reshape(4, 16, 128, 8, 512).transpose(3, 0, 2, 1, 4).reshape(8, 4, 128, 16 * 512)
    ).astype(BF)
    bwm = np.asarray(i["L0_ffn_base_w"])
    w["bw"] = np.ascontiguousarray(
        bwm.T.reshape(2, 16, 128, 8, 512).transpose(3, 0, 2, 1, 4).reshape(8, 2, 128, 16 * 512)
    ).astype(BF)
    A = np.asarray(i["L0_ffn_A"])
    rw = np.asarray(i["L0_ffn_router_w"])
    ar = np.concatenate([A.T, rw.T], axis=1)
    w["arw"] = np.ascontiguousarray(ar.reshape(KT, 128, 80).transpose(1, 0, 2).reshape(128, KT * 80)).astype(BF)
    B = np.asarray(i["L0_ffn_B"])
    w["betw"] = np.ascontiguousarray(
        B.transpose(2, 0, 1).reshape(64, 16, 8, 512).transpose(2, 0, 1, 3).reshape(8, 64, 16 * 512)
    ).astype(BF)
    w["bbias"] = np.asarray(i["L0_ffn_base_b"]).reshape(1, DIM).astype(BF)
    w["identb"] = np.eye(128, dtype=np.float32).astype(BF)
    w["identf"] = np.eye(128, dtype=np.float32)
    w["triu01"] = np.triu(np.ones((128, 128), np.float32))
    w["onesb"] = np.ones((1, 128), np.float32).astype(BF)
    w["epsb"] = np.full((128, 1), EPS, np.float32)
    return w


def make_in_maps(inputs):
    shared = _prep_shared(inputs)
    x = np.asarray(inputs["L0_attn_hidden_states"])
    maps = []
    for c in range(NCORES):
        b, q = c // 4, c % 4
        xhc = np.zeros((TH, DIM), np.float32)
        if q > 0:
            xhc[:] = x[b, q * T - 3:(q + 1) * T]
        else:
            xhc[3:] = x[b, 0:T]
        rank = q
        rm = np.zeros((1, 3 * 128), np.float32)
        for j in range(3):
            if j < rank:
                rm[0, j * 128:(j + 1) * 128] = 1.0
        m = dict(shared)
        m["xh"] = xhc
        m["rmask"] = rm
        m["rmaskc"] = 1.0 - rm
        maps.append(m)
    return maps


def kernel(**inputs):
    nc, _ = build(debug=False, upto=99)
    maps = make_in_maps(inputs)
    res = run_bass_kernel_spmd(nc, maps, core_ids=list(range(NCORES)))
    out = np.zeros((2, 4096, DIM), np.float32)
    for c in range(NCORES):
        b, q = c // 4, c % 4
        out[b, q * T:(q + 1) * T] = res.results[c]["out"]
    return out
